# version 10 (final)
# speedup vs baseline: 1.1131x; 1.0034x over previous
import sys
if "/opt/trn_rl_repo" not in sys.path:
    sys.path.insert(0, "/opt/trn_rl_repo")
from contextlib import ExitStack
import numpy as np
import concourse.bass as bass
import concourse.mybir as mybir
from concourse.bass_utils import run_bass_kernel_spmd

F32 = mybir.dt.float32
BF16 = mybir.dt.bfloat16
AF = mybir.ActivationFunctionType
ALU = mybir.AluOpType
AX = mybir.AxisListType

P = 128
NCORES = 8
SEQ = 16384
DM = 1024
TPC = SEQ // NCORES
DT = DM // P


_ENGINES = ("tensor", "vector", "scalar", "gpsimd", "sync")

STRICT_SAME_ENGINE = True
USE_DRAINS = False


class _Op:
    __slots__ = ("idx", "engine", "fn", "deps", "dma", "sem_key", "marked",
                 "sem", "value", "tag", "is_store", "needs_drain", "cc")

    def __init__(self, idx, engine, fn, deps, dma, sem_key, tag, is_store):
        self.idx = idx
        self.engine = engine
        self.fn = fn
        self.deps = deps
        self.dma = dma
        self.sem_key = sem_key
        self.marked = False
        self.needs_drain = False
        self.sem = None
        self.value = None
        self.tag = tag
        self.is_store = is_store
        self.cc = False


class Prog:

    def __init__(self, nc):
        self.nc = nc
        self.ops = []
        self.lastw = {}
        self.readers = {}
        self._bar = set()
        self.cidv = {}

    def barrier(self):
        per_engine_last = {}
        dma_last = {}
        for o in self.ops:
            if o.dma or getattr(o, "cc", False):
                dma_last[o.sem_key or o.idx] = o.idx
            else:
                per_engine_last[o.engine] = o.idx
        self._bar = set(per_engine_last.values()) | set(dma_last.values())

    def cc(self, fn, reads=(), writes=(), tag=""):
        idx = self.op("gpsimd", fn, reads, writes, tag=tag,
                      sem_key=f"__cc{len(self.ops)}")
        self.ops[idx].cc = True
        return idx

    def op(self, engine, fn, reads=(), writes=(), dma=False, tag="",
           is_store=False, sem_key=None):
        assert engine in _ENGINES, engine
        idx = len(self.ops)
        deps = set(self._bar)
        for b in reads:
            w = self.lastw.get(b)
            if w is not None:
                deps.add(w)
        for b in writes:
            w = self.lastw.get(b)
            if w is not None:
                deps.add(w)
            for r in self.readers.get(b, ()):
                deps.add(r)
        if dma and sem_key is None:
            if is_store:
                sem_key = (reads[0] if reads else f"st{idx}") + "@st"
            else:
                sem_key = writes[0]
        self.ops.append(_Op(idx, engine, fn, deps, dma, sem_key, tag, is_store))
        for b in reads:
            self.readers.setdefault(b, []).append(idx)
        for b in writes:
            self.lastw[b] = idx
            self.readers[b] = []
        return idx

    def t(self, fn, reads=(), writes=(), tag=""):
        return self.op("tensor", fn, reads, writes, tag=tag)

    def v(self, fn, reads=(), writes=(), tag=""):
        return self.op("vector", fn, reads, writes, tag=tag)

    def s(self, fn, reads=(), writes=(), tag=""):
        return self.op("scalar", fn, reads, writes, tag=tag)

    def g(self, fn, reads=(), writes=(), tag=""):
        return self.op("gpsimd", fn, reads, writes, tag=tag)

    def load(self, fn, reads=(), writes=(), queue="sync", tag="", sem_key=None):
        return self.op(queue, fn, reads, writes, dma=True, tag=tag,
                       sem_key=sem_key)

    def store(self, fn, reads=(), writes=(), queue="sync", tag="", sem_key=None):
        return self.op(queue, fn, reads, writes, dma=True, tag=tag,
                       is_store=True, sem_key=sem_key)

    def finalize(self, stack):
        nc = self.nc
        ops = self.ops
        for o in ops:
            if o.dma or o.cc:
                o.marked = True
        for o in ops:
            for d in o.deps:
                dep = ops[d]
                if dep.dma or dep.cc:
                    dep.marked = True
                elif dep.engine != o.engine:
                    dep.marked = True
                    dep.needs_drain = True
                elif dep.engine == "tensor":
                    pass
                elif STRICT_SAME_ENGINE:
                    dep.marked = True
        eng_sem = {}
        slot_sem = {}
        eng_count = {e: 0 for e in _ENGINES}
        slot_count = {}
        for o in ops:
            if not o.marked:
                continue
            if o.cc:
                slot_sem[o.sem_key] = stack.enter_context(
                    nc.semaphore(f"sem_{len(slot_sem)}"))
                slot_count[o.sem_key] = 1
                o.sem = slot_sem[o.sem_key]
                o.value = 1
            elif o.dma:
                if o.sem_key not in slot_sem:
                    slot_sem[o.sem_key] = stack.enter_context(
                        nc.semaphore(f"sem_{len(slot_sem)}"))
                    slot_count[o.sem_key] = 0
                slot_count[o.sem_key] += 16
                o.sem = slot_sem[o.sem_key]
                o.value = slot_count[o.sem_key]
            else:
                if o.engine not in eng_sem:
                    eng_sem[o.engine] = stack.enter_context(
                        nc.semaphore(f"esem_{o.engine}"))
                eng_count[o.engine] += 1
                o.sem = eng_sem[o.engine]
                o.value = eng_count[o.engine]
        n_sems = len(slot_sem) + len(eng_sem)
        if n_sems > 100:
            raise RuntimeError(f"too many semaphores: {n_sems}")
        final_waits = [(s, slot_count[k]) for k, s in slot_sem.items()
                       if k.endswith("@st")]

        self.debug_ops = ops
        self.debug_eng_sem = eng_sem
        block = stack.enter_context(nc.Block())

        def emit_for(engine):
            def _(eng):
                if engine in ("sync", "gpsimd", "scalar"):
                    self.cidv[engine] = eng.partition_id()
                highwater = {}
                for o in ops:
                    if o.engine != engine:
                        continue
                    want = {}
                    for d in sorted(o.deps):
                        dep = ops[d]
                        if (not dep.dma) and (not dep.cc) \
                                and dep.engine == engine and (
                                engine == "tensor" or not STRICT_SAME_ENGINE):
                            continue
                        assert dep.marked
                        key = id(dep.sem)
                        if want.get(key, (None, 0))[1] < dep.value:
                            want[key] = (dep.sem, dep.value)
                    for sem, val in want.values():
                        if highwater.get(id(sem), 0) < val:
                            eng.wait_ge(sem, val)
                            highwater[id(sem)] = val
                    ins = o.fn(eng)
                    if o.marked:
                        if o.cc:
                            ins.then_inc(o.sem)
                        elif o.needs_drain and not o.dma and USE_DRAINS:
                            eng.drain(fusable=True).then_inc(o.sem, 1)
                        else:
                            ins.then_inc(o.sem, 16 if o.dma else 1)
                if engine == "sync":
                    for sem, val in final_waits:
                        eng.wait_ge(sem, val)
            return _

        for e in _ENGINES:
            getattr(block, e)(emit_for(e))



def ceil_div(a, b):
    return (a + b - 1) // b


class Ctx:

    def __init__(self, nc, prog, stack):
        self.nc = nc
        self.prog = prog
        self.stack = stack
        self.sb = {}
        self.ps = []
        self._psum_rr = 0
        self.uid = 0

    def sbuf(self, name, shape, dtype, persistent=False):
        self._names = getattr(self, "_names", {})
        n = self._names.get(name, 0)
        self._names[name] = n + 1
        uname = name if n == 0 else f"{name}__{n}"
        st = self.stack if (persistent or not getattr(self, "_sec", None)) \
            else self._sec
        h = st.enter_context(self.nc.sbuf_tensor(uname, list(shape), dtype))
        self.sb[name] = h
        return h

    def section(self):
        from contextlib import ExitStack as _ES

        class _Sec:
            def __init__(s, cx):
                s.cx = cx

            def __enter__(s):
                s.es = _ES()
                s.cx._sec = s.es
                return s.es

            def __exit__(s, *a):
                s.es.close()
                s.cx._sec = None
                return False
        return _Sec(self)

    def psum_banks(self, n=8):
        for i in range(n):
            self.ps.append(self.stack.enter_context(
                self.nc.psum_tensor(f"psb{i}", [P, 512], F32)))

    def bank(self):
        i = self._psum_rr % len(self.ps)
        self._psum_rr += 1
        return self.ps[i], f"psb{i}"

    def fresh(self, prefix):
        self.uid += 1
        return f"{prefix}{self.uid}"

    def cc(self, value, parts=P):
        value = float(value)
        if not hasattr(self, "_consts"):
            self._consts = {}
            self._cP = self.sbuf("constP", (P, 16), F32, persistent=True)
            self._c1 = self.sbuf("const1", (1, 16), F32, persistent=True)
            self._cn = {P: 0, 1: 0}
        key = (parts, value)
        if key not in self._consts:
            i = self._cn[parts]
            assert i < 16
            self._cn[parts] = i + 1
            t = self._cP if parts == P else self._c1
            bk = "constP" if parts == P else "const1"
            self.prog.v(lambda e, t=t, i=i, value=value: e.memset(
                t[:, i:i + 1], value), writes=(bk,))
            self._consts[key] = (t, i, bk)
        t, i, bk = self._consts[key]
        return t[:, i:i + 1], bk


def emit_const_ones(cx):
    ones = cx.sbuf("ones_row", (1, P), F32, persistent=True)
    cx.prog.v(lambda e: e.memset(ones[:], 1.0), writes=("ones_row",))
    onec = cx.sbuf("ones_col", (P, 1), F32, persistent=True)
    cx.prog.v(lambda e: e.memset(onec[:], 1.0), writes=("ones_col",))
    ident = cx.sbuf("ident", (P, P), F32, persistent=True)
    it = cx.sbuf("identsrc", (P, P), F32, persistent=True)
    cx.prog.g(lambda e: e.iota(it[:], pattern=[[1, P]], base=0,
                               channel_multiplier=-1,
                               allow_small_or_imprecise_dtypes=True),
              writes=("identsrc",))
    cx.prog.v(lambda e: e.tensor_scalar(
        out=ident[:], in0=it[:], scalar1=0.0, scalar2=None, op0=ALU.is_equal),
        reads=("identsrc",), writes=("ident",))
    return ones


def emit_broadcast_row(cx, row_ap, row_bufs, dst_sb, dst_buf, width,
                       dst_col0=0, scale=None):
    prog = cx.prog
    ones = cx.sb["ones_row"]
    for c0 in range(0, width, 512):
        w = min(512, width - c0)
        pb, pkey = cx.bank()
        prog.t(lambda e, c0=c0, w=w, pb=pb: e.matmul(
            pb[:, 0:w], ones[:], row_ap[:, c0:c0 + w], start=True, stop=True),
            reads=tuple(row_bufs) + ("ones_row",), writes=(pkey,))
        if scale is None:
            prog.v(lambda e, c0=c0, w=w, pb=pb: e.tensor_copy(
                out=dst_sb[:, dst_col0 + c0:dst_col0 + c0 + w], in_=pb[:, 0:w]),
                reads=(pkey,), writes=(dst_buf,))
        else:
            c_s, ck_s = cx.cc(scale, parts=P)
            prog.s(lambda e, c0=c0, w=w, pb=pb, c=c_s: e.activation(
                out=dst_sb[:, dst_col0 + c0:dst_col0 + c0 + w], in_=pb[:, 0:w],
                func=AF.Copy, scale=c),
                reads=(pkey, ck_s), writes=(dst_buf,))


def emit_ln_fm(cx, x_sb, x_buf, T, out_sb, out_buf, n_dt=DT, eps=1e-5,
               gamma=None, beta=None, gb_buf=None, sq_name=None,
               x_col0=0, out_col0=0, out_also=None):
    prog = cx.prog
    nm = cx.fresh("ln")
    sq = cx.sb.get(sq_name or "ln_sq")
    if sq is None:
        sq = cx.sbuf("ln_sq", (P, 512), F32, persistent=True)
    rowbuf = cx.sb.get("ln_rows")
    if rowbuf is None:
        rowbuf = cx.sbuf("ln_rows", (1, 512 * 3), F32, persistent=True)
    rep = cx.sb.get("ln_rep")
    if rep is None:
        rep = cx.sbuf("ln_rep", (P, 512 * 2), F32, persistent=True)
    onec = cx.sb["ones_col"]
    inv_d = 1.0 / (n_dt * P)
    for c0 in range(0, T, 512):
        w = min(512, T - c0)
        pb_s, pk_s = cx.bank()
        pb_q, pk_q = cx.bank()
        for j in range(n_dt):
            xa = x_sb[:, x_col0 + j * T + c0: x_col0 + j * T + c0 + w]
            prog.t(lambda e, xa=xa, pb=pb_s, j=j, w=w: e.matmul(
                pb[0:1, 0:w], onec[:], xa, start=(j == 0), stop=(j == n_dt - 1)),
                reads=(x_buf, "ones_col"), writes=(pk_s,))
            prog.s(lambda e, xa=xa, w=w: e.activation(
                out=sq[:, 0:w], in_=xa, func=AF.Square),
                reads=(x_buf,), writes=("ln_sq",))
            prog.t(lambda e, pb=pb_q, j=j, w=w: e.matmul(
                pb[0:1, 0:w], onec[:], sq[:, 0:w], start=(j == 0),
                stop=(j == n_dt - 1)),
                reads=("ln_sq", "ones_col"), writes=(pk_q,))
        mu = rowbuf[:, 0:w]
        s2 = rowbuf[:, 512:512 + w]
        var = rowbuf[:, 1024:1024 + w]
        c_inv, ck_inv = cx.cc(inv_d, parts=1)
        prog.s(lambda e, pb=pb_s, w=w, c=c_inv, mu=mu: e.activation(
            out=mu, in_=pb[0:1, 0:w], func=AF.Copy, scale=c),
            reads=(pk_s, ck_inv), writes=("ln_rows",))
        prog.s(lambda e, pb=pb_q, w=w, c=c_inv, s2=s2: e.activation(
            out=s2, in_=pb[0:1, 0:w], func=AF.Copy, scale=c),
            reads=(pk_q, ck_inv), writes=("ln_rows",))
        prog.v(lambda e, mu=mu, var=var: e.scalar_tensor_tensor(
            out=var, in0=mu, scalar=-1.0, in1=mu, op0=ALU.mult, op1=ALU.mult),
            reads=("ln_rows",), writes=("ln_rows",))
        prog.v(lambda e, var=var, s2=s2: e.tensor_add(var, var, s2),
               reads=("ln_rows",), writes=("ln_rows",))
        c_eps, ck_eps = cx.cc(eps, parts=1)
        c_nh, ck_nh = cx.cc(-0.5, parts=1)
        prog.s(lambda e, c=c_eps, var=var: e.activation(
            out=var, in_=var, func=AF.Ln, bias=c),
            reads=("ln_rows", ck_eps), writes=("ln_rows",))
        prog.s(lambda e, c=c_nh, var=var: e.activation(
            out=var, in_=var, func=AF.Exp, scale=c),
            reads=("ln_rows", ck_nh), writes=("ln_rows",))
        emit_broadcast_row(cx, mu, ("ln_rows",), rep, "ln_rep", w, dst_col0=0)
        emit_broadcast_row(cx, var, ("ln_rows",), rep, "ln_rep", w, dst_col0=512)
        for j in range(n_dt):
            xa = x_sb[:, x_col0 + j * T + c0: x_col0 + j * T + c0 + w]
            oa = out_sb[:, out_col0 + j * T + c0: out_col0 + j * T + c0 + w]
            prog.v(lambda e, xa=xa, w=w: e.tensor_sub(sq[:, 0:w], xa, rep[:, 0:w]),
                   reads=(x_buf, "ln_rep"), writes=("ln_sq",))
            if gamma is None:
                prog.v(lambda e, oa=oa, w=w: e.tensor_mul(
                    oa, sq[:, 0:w], rep[:, 512:512 + w]),
                    reads=("ln_sq", "ln_rep"), writes=(out_buf,))
            else:
                prog.v(lambda e, w=w: e.tensor_mul(
                    sq[:, 0:w], sq[:, 0:w], rep[:, 512:512 + w]),
                    reads=("ln_sq", "ln_rep"), writes=("ln_sq",))
                prog.v(lambda e, oa=oa, j=j, w=w: e.tensor_scalar(
                    out=oa, in0=sq[:, 0:w], scalar1=gamma[:, j:j + 1],
                    scalar2=beta[:, j:j + 1], op0=ALU.mult, op1=ALU.add),
                    reads=("ln_sq", gb_buf), writes=(out_buf,))
            if out_also is not None:
                sb2, buf2 = out_also
                prog.v(lambda e, sb2=sb2, oa_src=oa, j=j, c0=c0, w=w:
                       e.tensor_copy(out=sb2[:, out_col0 + j * T + c0:
                                             out_col0 + j * T + c0 + w],
                                     in_=oa_src),
                       reads=(out_buf,), writes=(buf2,))



def _wload(dst_sb, src_ap, n):
    kt = src_ap.shape[0] // P
    return lambda e: e.dma_start(
        out=dst_sb.ap().rearrange("p (a n) -> p a n", a=kt),
        in_=src_ap.rearrange("(a p) n -> p a n", p=P))



def _t(w):
    return np.ascontiguousarray(np.asarray(w).T)


def col_layout(v):
    v = np.asarray(v)
    n = v.shape[0] // P
    return np.ascontiguousarray(v.reshape(n, P).T)


def fm_slices(x):
    xt = np.ascontiguousarray(x.T)
    return [np.ascontiguousarray(xt[:, c * TPC:(c + 1) * TPC]) for c in range(NCORES)]


def fm_unslice(parts):
    xt = np.concatenate(parts, axis=1)
    return np.ascontiguousarray(xt.T)



def build_ffn0(debug=False):
    nc = bass.Bass("TRN2", target_bir_lowering=False)
    d_x = nc.dram_tensor("xT", [DM, TPC], F32, kind="ExternalInput").ap()
    d_gate = nc.dram_tensor("gate_wT", [DM, 8], F32, kind="ExternalInput").ap()
    d_ebias = nc.dram_tensor("ebias", [1, 8], F32, kind="ExternalInput").ap()
    d_wup = nc.dram_tensor("wupT", [9, DM, 1408], F32, kind="ExternalInput").ap()
    d_wdn = nc.dram_tensor("wdnT", [9, 1408, DM], F32, kind="ExternalInput").ap()
    d_out = nc.dram_tensor("x1T", [DM, TPC], F32, kind="ExternalOutput").ap()
    if debug:
        d_dbg_c = nc.dram_tensor("dbg_c", [8, TPC], BF16, kind="ExternalOutput").ap()
        d_dbg_ln = nc.dram_tensor("dbg_ln", [DM, TPC], BF16, kind="ExternalOutput").ap()
        d_dbg_r = nc.dram_tensor("dbg_r", [P, 16 * 8], F32, kind="ExternalOutput").ap()
        d_dbg_m = nc.dram_tensor("dbg_m", [P, 16 * 8], F32, kind="ExternalOutput").ap()

    NH = 1408
    NHT = NH // P
    CH = 1024
    NCH = TPC // CH

    with ExitStack() as stack:
        prog = Prog(nc)
        cx = Ctx(nc, prog, stack)
        cx.psum_banks(8)
        emit_const_ones(cx)

        sb_xa = cx.sbuf("xa", (P, DT * TPC), F32)
        sb_ln = cx.sbuf("lnx", (P, DT * TPC), BF16)
        sb_gate = cx.sbuf("gate", (P, DT * 8), BF16)
        sb_route = cx.sbuf("route", (P, 16 * 8), F32)
        sb_routeB = cx.sbuf("routeB", (P, 16 * 8), F32)
        sb_mask = cx.sbuf("maskw", (P, 16 * 8), F32)
        sb_col = cx.sbuf("colw", (P, 2), F32)
        sb_ebias = cx.sbuf("ebiasr", (P, 8), F32)
        sb_eb_row = cx.sbuf("eb_row", (1, 8), F32)
        sb_cT = cx.sbuf("cT", (8, TPC), BF16)
        sb_esel = cx.sbuf("esel", (8, 8 * P), BF16)
        sb_crep = cx.sbuf("crep", (P, TPC), BF16)
        sb_xe = cx.sbuf("xe", (P, DT * CH), BF16)
        sb_wup = cx.sbuf("wup", (P, DT * NH), BF16)
        sb_wdn = cx.sbuf("wdn", (P, NHT * DM), BF16)
        sb_h = cx.sbuf("hbuf", (P, NHT * CH), BF16)
        sb_rt = cx.sbuf("relu_t", (P, 512), F32)
        sb_xre = cx.sbuf("xre", (P, 512), F32)

        prog.g(lambda e: e.iota(sb_esel[:], pattern=[[1, 8], [0, P]], base=0,
                                channel_multiplier=-1,
                                allow_small_or_imprecise_dtypes=True),
               writes=("esel",))
        prog.v(lambda e: e.tensor_scalar(
            out=sb_esel[:], in0=sb_esel[:], scalar1=0.0, scalar2=None,
            op0=ALU.is_equal), reads=("esel",), writes=("esel",))

        prog.load(_wload(sb_xa, d_x, TPC), writes=("xa",), tag="ldx")
        emit_ln_fm(cx, sb_xa, "xa", TPC, sb_ln, "lnx", eps=1e-5)

        prog.load(_wload(sb_gate, d_gate, 8), writes=("gate",), queue="gpsimd")
        prog.load(lambda e: e.dma_start(out=sb_eb_row[:], in_=d_ebias[:]),
                  writes=("eb_row",))
        emit_broadcast_row(cx, sb_eb_row[:], ("eb_row",), sb_ebias, "ebiasr", 8)

        for tt in range(16):
            pb, pk = cx.bank()
            for j in range(DT):
                lhs = sb_ln[:, j * TPC + tt * P: j * TPC + (tt + 1) * P]
                prog.t(lambda e, lhs=lhs, j=j, pb=pb: e.matmul(
                    pb[:, 0:8], lhs, sb_gate[:, j * 8:(j + 1) * 8],
                    start=(j == 0), stop=(j == DT - 1)),
                    reads=("lnx", "gate"), writes=(pk,))
            sl = slice(tt * 8, (tt + 1) * 8)
            prog.s(lambda e, pb=pb, sl=sl: e.activation(
                out=sb_route[:, sl], in_=pb[:, 0:8], func=AF.Sigmoid),
                reads=(pk,), writes=("route",))
            prog.v(lambda e, sl=sl: e.tensor_add(
                sb_routeB[:, sl], sb_route[:, sl], sb_ebias[:, 0:8]),
                reads=("route", "ebiasr"), writes=("routeB",))
            prog.v(lambda e, sl=sl: e.tensor_reduce(
                out=sb_col[:, 0:1], in_=sb_routeB[:, sl], axis=AX.X, op=ALU.max),
                reads=("routeB",), writes=("colw",))
            prog.v(lambda e, sl=sl: e.tensor_scalar(
                out=sb_mask[:, sl], in0=sb_routeB[:, sl], scalar1=sb_col[:, 0:1],
                scalar2=None, op0=ALU.is_equal),
                reads=("routeB", "colw"), writes=("maskw",))
            prog.v(lambda e, sl=sl: e.scalar_tensor_tensor(
                out=sb_routeB[:, sl], in0=sb_mask[:, sl], scalar=-1e30,
                in1=sb_routeB[:, sl], op0=ALU.mult, op1=ALU.add),
                reads=("maskw", "routeB"), writes=("routeB",))
            prog.v(lambda e, sl=sl: e.tensor_reduce(
                out=sb_col[:, 1:2], in_=sb_routeB[:, sl], axis=AX.X, op=ALU.max),
                reads=("routeB",), writes=("colw",))
            prog.v(lambda e, sl=sl: e.tensor_scalar(
                out=sb_routeB[:, sl], in0=sb_routeB[:, sl], scalar1=sb_col[:, 1:2],
                scalar2=None, op0=ALU.is_equal),
                reads=("routeB", "colw"), writes=("routeB",))
            prog.v(lambda e, sl=sl: e.tensor_add(
                sb_mask[:, sl], sb_mask[:, sl], sb_routeB[:, sl]),
                reads=("maskw", "routeB"), writes=("maskw",))
            prog.v(lambda e, sl=sl: e.tensor_mul(
                sb_mask[:, sl], sb_mask[:, sl], sb_route[:, sl]),
                reads=("maskw", "route"), writes=("maskw",))
            pb2, pk2 = cx.bank()
            prog.t(lambda e, sl=sl, pb2=pb2: e.transpose(
                pb2[0:8, 0:P], sb_mask[:, sl], cx.sb["ident"][:]),
                reads=("maskw", "ident"), writes=(pk2,))
            prog.v(lambda e, tt=tt, pb2=pb2: e.tensor_copy(
                out=sb_cT[:, tt * P:(tt + 1) * P], in_=pb2[0:8, 0:P]),
                reads=(pk2,), writes=("cT",))

        if debug:
            prog.store(lambda e: e.dma_start(out=d_dbg_r[:], in_=sb_route[:]),
                       reads=("route",), tag="stdbgr")
            prog.store(lambda e: e.dma_start(out=d_dbg_m[:], in_=sb_mask[:]),
                       reads=("maskw",), tag="stdbgm")
            prog.store(lambda e: e.dma_start(out=d_dbg_c[:], in_=sb_cT[:]),
                       reads=("cT",), tag="stdbgc")
            prog.store(lambda e: e.dma_start(
                out=d_dbg_ln.rearrange("(a p) t -> p a t", p=P),
                in_=sb_ln.ap().rearrange("p (a t) -> p a t", a=DT)),
                reads=("lnx",), tag="stdbgl")

        for ei in range(9):
            prog.load(_wload(sb_wup, d_wup[ei], NH),
                writes=("wup",), queue="gpsimd", tag=f"wup{ei}")
            prog.load(_wload(sb_wdn, d_wdn[ei], DM),
                writes=("wdn",), queue="gpsimd", tag=f"wdn{ei}")
            if ei < 8:
                for c0 in range(0, TPC, 512):
                    pb, pk = cx.bank()
                    prog.t(lambda e, ei=ei, c0=c0, pb=pb: e.matmul(
                        pb[:, 0:512], sb_esel[:, ei * P:(ei + 1) * P],
                        sb_cT[:, c0:c0 + 512], start=True, stop=True),
                        reads=("cT", "esel"), writes=(pk,))
                    prog.v(lambda e, c0=c0, pb=pb: e.tensor_copy(
                        out=sb_crep[:, c0:c0 + 512], in_=pb[:, 0:512]),
                        reads=(pk,), writes=("crep",))
            for ch in range(NCH):
                tc0 = ch * CH
                if ei < 8:
                    for j in range(DT):
                        prog.v(lambda e, j=j, tc0=tc0: e.tensor_mul(
                            sb_xe[:, j * CH:(j + 1) * CH],
                            sb_ln[:, j * TPC + tc0: j * TPC + tc0 + CH],
                            sb_crep[:, tc0:tc0 + CH]),
                            reads=("lnx", "crep"), writes=("xe",))
                for mh in range(NHT):
                    for n0 in range(0, CH, 512):
                        pb, pk = cx.bank()
                        for j in range(DT):
                            if ei < 8:
                                rhs = sb_xe[:, j * CH + n0: j * CH + n0 + 512]
                                rkey = "xe"
                            else:
                                rhs = sb_ln[:, j * TPC + tc0 + n0:
                                            j * TPC + tc0 + n0 + 512]
                                rkey = "lnx"
                            lhs = sb_wup[:, j * NH + mh * P: j * NH + (mh + 1) * P]
                            prog.t(lambda e, lhs=lhs, rhs=rhs, j=j, pb=pb:
                                   e.matmul(pb[:], lhs, rhs, start=(j == 0),
                                            stop=(j == DT - 1)),
                                   reads=("wup", rkey), writes=(pk,))
                        prog.s(lambda e, pb=pb: e.activation(
                            out=sb_rt[:], in_=pb[:], func=AF.Relu),
                            reads=(pk,), writes=("relu_t",))
                        prog.v(lambda e, mh=mh, n0=n0: e.tensor_mul(
                            sb_h[:, mh * CH + n0: mh * CH + n0 + 512],
                            sb_rt[:], sb_rt[:]),
                            reads=("relu_t",), writes=("hbuf",))
                for md in range(DT):
                    for n0 in range(0, CH, 512):
                        pb, pk = cx.bank()
                        for mh in range(NHT):
                            lhs = sb_wdn[:, mh * DM + md * P: mh * DM + (md + 1) * P]
                            rhs = sb_h[:, mh * CH + n0: mh * CH + n0 + 512]
                            prog.t(lambda e, lhs=lhs, rhs=rhs, mh=mh, pb=pb:
                                   e.matmul(pb[:], lhs, rhs, start=(mh == 0),
                                            stop=(mh == NHT - 1)),
                                   reads=("wdn", "hbuf"), writes=(pk,))
                        dst = sb_xa[:, md * TPC + tc0 + n0:
                                    md * TPC + tc0 + n0 + 512]
                        if ei == 0:
                            prog.v(lambda e, dst=dst, pb=pb: e.tensor_copy(
                                out=dst, in_=pb[:]),
                                reads=(pk,), writes=("xa",))
                        else:
                            prog.v(lambda e, dst=dst, pb=pb: e.tensor_add(
                                dst, dst, pb[:]),
                                reads=(pk, "xa"), writes=("xa",))

        for j in range(DT):
            for n0 in range(0, TPC, 512):
                slot = (j * 4 + n0 // 512) % 2
                xr = sb_xre[:, 0:512] if slot == 0 else sb_rt[:, 0:512]
                xrkey = "xre" if slot == 0 else "relu_t"
                prog.load(lambda e, j=j, n0=n0, xr=xr: e.dma_start(
                    out=xr, in_=d_x[j * P:(j + 1) * P, n0:n0 + 512]),
                    writes=(xrkey,), tag="ldxre")
                sl = slice(j * TPC + n0, j * TPC + n0 + 512)
                prog.v(lambda e, sl=sl, xr=xr: e.scalar_tensor_tensor(
                    out=sb_xa[:, sl], in0=sb_xa[:, sl], scalar=0.5,
                    in1=xr, op0=ALU.mult, op1=ALU.add),
                    reads=("xa", xrkey), writes=("xa",))
        prog.store(lambda e: e.dma_start(
            out=d_out.rearrange("(a p) t -> p a t", p=P),
            in_=sb_xa.ap().rearrange("p (a t) -> p a t", a=DT)),
            reads=("xa",), tag="stx")

        prog.finalize(stack)
    return nc




def build_ffn1(debug=False):
    nc = bass.Bass("TRN2", target_bir_lowering=False)
    d_x = nc.dram_tensor("xT", [DM, TPC], F32, kind="ExternalInput").ap()
    d_rw = nc.dram_tensor("router_wT", [DM, 8], F32, kind="ExternalInput").ap()
    d_rb = nc.dram_tensor("router_b", [1, 8], F32, kind="ExternalInput").ap()
    d_lw = nc.dram_tensor("lang_wT", [DM, 4], F32, kind="ExternalInput").ap()
    d_lb = nc.dram_tensor("lang_b", [1, 4], F32, kind="ExternalInput").ap()
    d_w1 = nc.dram_tensor("w1", [8, DM, 4096], F32, kind="ExternalInput").ap()
    d_b1 = nc.dram_tensor("b1c", [8, P, 32], F32, kind="ExternalInput").ap()
    d_w2 = nc.dram_tensor("w2", [8, 4096, DM], F32, kind="ExternalInput").ap()
    d_b2 = nc.dram_tensor("b2", [8, DM], F32, kind="ExternalInput").ap()
    d_out = nc.dram_tensor("xoT", [DM, TPC], F32, kind="ExternalOutput").ap()
    if debug:
        d_dbg_c = nc.dram_tensor("dbg_c", [8, TPC], BF16, kind="ExternalOutput").ap()
        d_dbg_s = nc.dram_tensor("dbg_s", [P, 40], F32, kind="ExternalOutput").ap()
        d_dbg_ln = nc.dram_tensor("dbg_ln", [DM, TPC], BF16, kind="ExternalOutput").ap()

    HC = 1024
    NHC = 4096 // HC
    HT = HC // P

    with ExitStack() as stack:
        prog = Prog(nc)
        cx = Ctx(nc, prog, stack)
        cx.psum_banks(8)
        emit_const_ones(cx)

        sb_xa = cx.sbuf("xa", (P, DT * TPC), F32)
        sb_ln = cx.sbuf("lnx", (P, DT * TPC), BF16)
        sb_rw = cx.sbuf("rw", (P, DT * 8), BF16)
        sb_lw = cx.sbuf("lw", (P, DT * 4), BF16)
        sb_rbr = cx.sbuf("rbr", (P, 8), F32)
        sb_lbr = cx.sbuf("lbr", (P, 4), F32)
        sb_row8 = cx.sbuf("row8", (1, 8), F32)
        sb_row4 = cx.sbuf("row4", (1, 4), F32)
        sb_lp = cx.sbuf("lp", (P, 4), F32)
        sb_ew = cx.sbuf("ew", (P, 8), F32)
        sb_route = cx.sbuf("route", (P, 8), F32)
        sb_routeB = cx.sbuf("routeB", (P, 8), F32)
        sb_mask = cx.sbuf("maskw", (P, 8), F32)
        sb_col = cx.sbuf("colw", (P, 4), F32)
        sb_cT = cx.sbuf("cT", (8, TPC), BF16)
        sb_esel = cx.sbuf("esel", (8, 8 * P), BF16)
        sb_wrep = cx.sbuf("wrep", (P, TPC), F32)
        sb_w1c = cx.sbuf("w1c", (P, DT * HC), BF16)
        sb_w2c = cx.sbuf("w2c", (P, HT * DM), BF16)
        sb_h = cx.sbuf("hbuf", (P, HT * TPC), BF16)
        sb_b1 = cx.sbuf("b1s", (P, 8 * 32), F32)
        sb_b2 = cx.sbuf("b2s", (8, DM), BF16)
        sb_tmp = cx.sbuf("tmp5", (P, 512), F32)
        sb_xre = cx.sbuf("xre", (P, 512), F32)

        prog.g(lambda e: e.iota(sb_esel[:], pattern=[[1, 8], [0, P]], base=0,
                                channel_multiplier=-1,
                                allow_small_or_imprecise_dtypes=True),
               writes=("esel",))
        prog.v(lambda e: e.tensor_scalar(
            out=sb_esel[:], in0=sb_esel[:], scalar1=0.0, scalar2=None,
            op0=ALU.is_equal), reads=("esel",), writes=("esel",))

        prog.load(_wload(sb_xa, d_x, TPC), writes=("xa",), tag="ldx")
        emit_ln_fm(cx, sb_xa, "xa", TPC, sb_ln, "lnx", eps=1e-5)

        prog.load(_wload(sb_rw, d_rw, 8), writes=("rw",), queue="gpsimd")
        prog.load(_wload(sb_lw, d_lw, 4), writes=("lw",), queue="gpsimd")
        prog.load(lambda e: e.dma_start(out=sb_row8[:], in_=d_rb[:]),
                  writes=("row8",))
        prog.load(lambda e: e.dma_start(out=sb_row4[:], in_=d_lb[:]),
                  writes=("row4",))
        prog.load(lambda e: e.dma_start(out=sb_b1.ap().rearrange(
            "p (a n) -> p a n", a=8), in_=d_b1.rearrange("a p n -> p a n")),
            writes=("b1s",))
        prog.load(lambda e: e.dma_start(out=sb_b2[:], in_=d_b2[:]),
                  writes=("b2s",), queue="gpsimd")
        emit_broadcast_row(cx, sb_row8[:], ("row8",), sb_rbr, "rbr", 8)
        emit_broadcast_row(cx, sb_row4[:], ("row4",), sb_lbr, "lbr", 4)

        if debug:
            prog.store(lambda e: e.dma_start(
                out=d_dbg_ln.rearrange("(a p) t -> p a t", p=P),
                in_=sb_ln.ap().rearrange("p (a t) -> p a t", a=DT)),
                reads=("lnx",), tag="stdbgln")

        for tt in range(16):
            pb_l, pk_l = cx.bank()
            pb_r, pk_r = cx.bank()
            for j in range(DT):
                lhs = sb_ln[:, j * TPC + tt * P: j * TPC + (tt + 1) * P]
                prog.t(lambda e, lhs=lhs, j=j, pb=pb_l: e.matmul(
                    pb[:, 0:4], lhs, sb_lw[:, j * 4:(j + 1) * 4],
                    start=(j == 0), stop=(j == DT - 1)),
                    reads=("lnx", "lw"), writes=(pk_l,), tag=f"langmm{tt}_{j}")
            for j in range(DT):
                lhs = sb_ln[:, j * TPC + tt * P: j * TPC + (tt + 1) * P]
                prog.t(lambda e, lhs=lhs, j=j, pb=pb_r: e.matmul(
                    pb[:, 0:8], lhs, sb_rw[:, j * 8:(j + 1) * 8],
                    start=(j == 0), stop=(j == DT - 1)),
                    reads=("lnx", "rw"), writes=(pk_r,))
            if debug and tt == 0:
                sb_dbg2 = cx.sbuf("dbg2", (P, 8), F32)
                prog.v(lambda e, pb=pb_l: e.tensor_copy(out=sb_dbg2[:, 0:4], in_=pb[:, 0:4]),
                       reads=(pk_l,), writes=("dbg2",))
            prog.v(lambda e, pb=pb_l: e.tensor_add(
                sb_lp[:], pb[:, 0:4], sb_lbr[:, 0:4]),
                reads=(pk_l, "lbr"), writes=("lp",), tag=f"langevac{tt}")
            prog.v(lambda e: e.tensor_reduce(
                out=sb_col[:, 0:1], in_=sb_lp[:], axis=AX.X, op=ALU.max),
                reads=("lp",), writes=("colw",))
            prog.v(lambda e: e.tensor_scalar(
                out=sb_lp[:], in0=sb_lp[:], scalar1=sb_col[:, 0:1], scalar2=None,
                op0=ALU.subtract), reads=("lp", "colw"), writes=("lp",))
            prog.s(lambda e: e.activation(
                out=sb_lp[:], in_=sb_lp[:], func=AF.Exp,
                accum_out=sb_col[:, 1:2]),
                reads=("lp",), writes=("lp", "colw"))
            prog.v(lambda e: e.reciprocal(out=sb_col[:, 2:3], in_=sb_col[:, 1:2]),
                   reads=("colw",), writes=("colw",))
            prog.v(lambda e: e.tensor_scalar(
                out=sb_lp[:], in0=sb_lp[:], scalar1=sb_col[:, 2:3], scalar2=None,
                op0=ALU.mult), reads=("lp", "colw"), writes=("lp",))
            prog.v(lambda e: e.tensor_add(
                sb_ew[:, 0:3], sb_lp[:, 0:3], sb_lp[:, 1:4]),
                reads=("lp",), writes=("ew",))
            prog.v(lambda e: e.tensor_copy(out=sb_ew[:, 3:4], in_=sb_lp[:, 3:4]),
                   reads=("lp",), writes=("ew",))
            prog.v(lambda e: e.tensor_copy(out=sb_ew[:, 4:8], in_=sb_ew[:, 0:4]),
                   reads=("ew",), writes=("ew",))
            prog.v(lambda e, pb=pb_r: e.tensor_add(
                sb_route[:], pb[:, 0:8], sb_rbr[:, 0:8]),
                reads=(pk_r, "rbr"), writes=("route",))
            prog.v(lambda e: e.scalar_tensor_tensor(
                out=sb_route[:], in0=sb_ew[:], scalar=0.1, in1=sb_route[:],
                op0=ALU.mult, op1=ALU.add),
                reads=("ew", "route"), writes=("route",))
            prog.v(lambda e: e.tensor_reduce(
                out=sb_col[:, 0:1], in_=sb_route[:], axis=AX.X, op=ALU.max),
                reads=("route",), writes=("colw",))
            prog.v(lambda e: e.tensor_scalar(
                out=sb_route[:], in0=sb_route[:], scalar1=sb_col[:, 0:1],
                scalar2=None, op0=ALU.subtract),
                reads=("route", "colw"), writes=("route",))
            prog.s(lambda e: e.activation(
                out=sb_route[:], in_=sb_route[:], func=AF.Exp,
                accum_out=sb_col[:, 1:2]),
                reads=("route",), writes=("route", "colw"))
            prog.v(lambda e: e.reciprocal(out=sb_col[:, 2:3], in_=sb_col[:, 1:2]),
                   reads=("colw",), writes=("colw",))
            prog.v(lambda e: e.tensor_scalar(
                out=sb_route[:], in0=sb_route[:], scalar1=sb_col[:, 2:3],
                scalar2=None, op0=ALU.mult),
                reads=("route", "colw"), writes=("route",))
            prog.v(lambda e: e.tensor_reduce(
                out=sb_col[:, 0:1], in_=sb_route[:], axis=AX.X, op=ALU.max),
                reads=("route",), writes=("colw",))
            prog.v(lambda e: e.tensor_scalar(
                out=sb_mask[:], in0=sb_route[:], scalar1=sb_col[:, 0:1],
                scalar2=None, op0=ALU.is_equal),
                reads=("route", "colw"), writes=("maskw",))
            prog.v(lambda e: e.scalar_tensor_tensor(
                out=sb_routeB[:], in0=sb_mask[:], scalar=-1e30, in1=sb_route[:],
                op0=ALU.mult, op1=ALU.add),
                reads=("maskw", "route"), writes=("routeB",))
            prog.v(lambda e: e.tensor_reduce(
                out=sb_col[:, 1:2], in_=sb_routeB[:], axis=AX.X, op=ALU.max),
                reads=("routeB",), writes=("colw",))
            prog.v(lambda e: e.tensor_scalar(
                out=sb_routeB[:], in0=sb_routeB[:], scalar1=sb_col[:, 1:2],
                scalar2=None, op0=ALU.is_equal),
                reads=("routeB", "colw"), writes=("routeB",))
            prog.v(lambda e: e.tensor_add(
                sb_mask[:], sb_mask[:], sb_routeB[:]),
                reads=("maskw", "routeB"), writes=("maskw",))
            prog.v(lambda e: e.tensor_mul(
                sb_mask[:], sb_mask[:], sb_route[:]),
                reads=("maskw", "route"), writes=("maskw",))
            prog.v(lambda e: e.tensor_reduce(
                out=sb_col[:, 3:4], in_=sb_mask[:], axis=AX.X, op=ALU.add),
                reads=("maskw",), writes=("colw",))
            prog.v(lambda e: e.reciprocal(out=sb_col[:, 3:4], in_=sb_col[:, 3:4]),
                   reads=("colw",), writes=("colw",))
            prog.v(lambda e: e.tensor_scalar(
                out=sb_mask[:], in0=sb_mask[:], scalar1=sb_col[:, 3:4],
                scalar2=None, op0=ALU.mult),
                reads=("maskw", "colw"), writes=("maskw",))
            if debug and tt == 0:
                sb_dbgs = cx.sbuf("dbgs", (P, 40), F32)
                prog.v(lambda e: e.tensor_copy(out=sb_dbgs[:, 0:8], in_=sb_route[:]),
                       reads=("route",), writes=("dbgs",))
                prog.v(lambda e: e.tensor_copy(out=sb_dbgs[:, 8:16], in_=sb_mask[:]),
                       reads=("maskw",), writes=("dbgs",))
                prog.v(lambda e: e.tensor_copy(out=sb_dbgs[:, 16:20], in_=sb_lp[:]),
                       reads=("lp",), writes=("dbgs",))
                prog.v(lambda e: e.tensor_copy(out=sb_dbgs[:, 20:28], in_=sb_ew[:]),
                       reads=("ew",), writes=("dbgs",))
                prog.v(lambda e: e.tensor_copy(out=sb_dbgs[:, 28:32], in_=sb_col[:]),
                       reads=("colw",), writes=("dbgs",))
                prog.v(lambda e: e.tensor_copy(out=sb_dbgs[:, 32:40], in_=cx.sb["dbg2"][:, 0:8]),
                       reads=("dbg2",), writes=("dbgs",))
                prog.store(lambda e: e.dma_start(out=d_dbg_s[:], in_=sb_dbgs[:]),
                           reads=("dbgs",), tag="stdbgs")
            pb2, pk2 = cx.bank()
            prog.t(lambda e, pb2=pb2: e.transpose(
                pb2[0:8, 0:P], sb_mask[:], cx.sb["ident"][:]),
                reads=("maskw", "ident"), writes=(pk2,))
            prog.v(lambda e, tt=tt, pb2=pb2: e.tensor_copy(
                out=sb_cT[:, tt * P:(tt + 1) * P], in_=pb2[0:8, 0:P]),
                reads=(pk2,), writes=("cT",))
            if debug and tt == 0:
                prog.s(lambda e, pb=pb_l: e.activation(out=cx.sb["dbg2"][:, 4:8],
                                              in_=pb[:, 0:4], func=AF.Copy),
                       reads=(pk_l, pk2), writes=("dbg2",), tag="lateread")

        if debug:
            prog.store(lambda e: e.dma_start(out=d_dbg_c[:], in_=sb_cT[:]),
                       reads=("cT",), tag="stdbgc")

        for md in range(DT):
            for n0 in range(0, TPC, 512):
                pb, pk = cx.bank()
                prog.t(lambda e, md=md, n0=n0, pb=pb: e.matmul(
                    pb[:], sb_b2[:, md * P:(md + 1) * P], sb_cT[:, n0:n0 + 512],
                    start=True, stop=True),
                    reads=("b2s", "cT"), writes=(pk,))
                prog.v(lambda e, md=md, n0=n0, pb=pb: e.tensor_copy(
                    out=sb_xa[:, md * TPC + n0: md * TPC + n0 + 512], in_=pb[:]),
                    reads=(pk,), writes=("xa",))

        for ei in range(8):
            for c0 in range(0, TPC, 512):
                pb, pk = cx.bank()
                prog.t(lambda e, ei=ei, c0=c0, pb=pb: e.matmul(
                    pb[:, 0:512], sb_esel[:, ei * P:(ei + 1) * P],
                    sb_cT[:, c0:c0 + 512], start=True, stop=True),
                    reads=("cT", "esel"), writes=(pk,))
                prog.v(lambda e, c0=c0, pb=pb: e.tensor_copy(
                    out=sb_wrep[:, c0:c0 + 512], in_=pb[:, 0:512]),
                    reads=(pk,), writes=("wrep",))
            for hc in range(NHC):
                prog.load(lambda e, ei=ei, hc=hc: e.dma_start(
                    out=sb_w1c.ap().rearrange("p (a n) -> p a n", a=DT),
                    in_=d_w1[ei, :, hc * HC:(hc + 1) * HC].rearrange(
                        "(a p) n -> p a n", p=P)),
                    writes=("w1c",), queue="gpsimd", tag=f"w1_{ei}_{hc}")
                prog.load(lambda e, ei=ei, hc=hc: e.dma_start(
                    out=sb_w2c.ap().rearrange("p (a n) -> p a n", a=HT),
                    in_=d_w2[ei, hc * HC:(hc + 1) * HC, :].rearrange(
                        "(a p) n -> p a n", p=P)),
                    writes=("w2c",), queue="gpsimd", tag=f"w2_{ei}_{hc}")
                for mh in range(HT):
                    for n0 in range(0, TPC, 512):
                        pb, pk = cx.bank()
                        for j in range(DT):
                            lhs = sb_w1c[:, j * HC + mh * P: j * HC + (mh + 1) * P]
                            rhs = sb_ln[:, j * TPC + n0: j * TPC + n0 + 512]
                            prog.t(lambda e, lhs=lhs, rhs=rhs, j=j, pb=pb:
                                   e.matmul(pb[:], lhs, rhs, start=(j == 0),
                                            stop=(j == DT - 1)),
                                   reads=("w1c", "lnx"), writes=(pk,))
                        bcol = sb_b1[:, ei * 32 + hc * HT + mh:
                                     ei * 32 + hc * HT + mh + 1]
                        prog.s(lambda e, mh=mh, n0=n0, pb=pb, bcol=bcol:
                               e.activation(
                                   out=sb_h[:, mh * TPC + n0: mh * TPC + n0 + 512],
                                   in_=pb[:], func=AF.Gelu, bias=bcol),
                               reads=(pk, "b1s"), writes=("hbuf",))
                for md in range(DT):
                    for n0 in range(0, TPC, 512):
                        pb, pk = cx.bank()
                        for mh in range(HT):
                            lhs = sb_w2c[:, mh * DM + md * P: mh * DM + (md + 1) * P]
                            rhs = sb_h[:, mh * TPC + n0: mh * TPC + n0 + 512]
                            prog.t(lambda e, lhs=lhs, rhs=rhs, mh=mh, pb=pb:
                                   e.matmul(pb[:], lhs, rhs, start=(mh == 0),
                                            stop=(mh == HT - 1)),
                                   reads=("w2c", "hbuf"), writes=(pk,))
                        sl = slice(md * TPC + n0, md * TPC + n0 + 512)
                        prog.v(lambda e, sl=sl, n0=n0, pb=pb: e.tensor_mul(
                            sb_tmp[:], pb[:], sb_wrep[:, n0:n0 + 512]),
                            reads=(pk, "wrep"), writes=("tmp5",))
                        prog.v(lambda e, sl=sl: e.tensor_add(
                            sb_xa[:, sl], sb_xa[:, sl], sb_tmp[:]),
                            reads=("tmp5", "xa"), writes=("xa",))

        for j in range(DT):
            for n0 in range(0, TPC, 512):
                slot = (j * 4 + n0 // 512) % 2
                xr = sb_xre[:, 0:512] if slot == 0 else sb_tmp[:, 0:512]
                xrkey = "xre" if slot == 0 else "tmp5"
                prog.load(lambda e, j=j, n0=n0, xr=xr: e.dma_start(
                    out=xr, in_=d_x[j * P:(j + 1) * P, n0:n0 + 512]),
                    writes=(xrkey,), tag="ldxre")
                sl = slice(j * TPC + n0, j * TPC + n0 + 512)
                prog.v(lambda e, sl=sl, xr=xr: e.scalar_tensor_tensor(
                    out=sb_xa[:, sl], in0=sb_xa[:, sl], scalar=0.5,
                    in1=xr, op0=ALU.mult, op1=ALU.add),
                    reads=("xa", xrkey), writes=("xa",))
        prog.store(lambda e: e.dma_start(
            out=d_out.rearrange("(a p) t -> p a t", p=P),
            in_=sb_xa.ap().rearrange("p (a t) -> p a t", a=DT)),
            reads=("xa",), tag="stx")

        prog.finalize(stack)
        nc._dbg_prog = prog
    return nc




def build_qkv():
    nc = bass.Bass("TRN2", target_bir_lowering=False)
    d_x = nc.dram_tensor("xT", [DM, TPC], F32, kind="ExternalInput").ap()
    d_wq = nc.dram_tensor("Wq", [DM, DM], F32, kind="ExternalInput").ap()
    d_wk = nc.dram_tensor("Wk", [DM, DM], F32, kind="ExternalInput").ap()
    d_wv = nc.dram_tensor("Wv", [DM, DM], F32, kind="ExternalInput").ap()
    d_qT = nc.dram_tensor("qT", [DM, TPC], BF16, kind="ExternalOutput").ap()
    d_kT = nc.dram_tensor("kT", [DM, TPC], BF16, kind="ExternalOutput").ap()
    d_v = nc.dram_tensor("v", [TPC, DM], BF16, kind="ExternalOutput").ap()

    with ExitStack() as stack:
        prog = Prog(nc)
        cx = Ctx(nc, prog, stack)
        cx.psum_banks(8)
        emit_const_ones(cx)

        sb_xa = cx.sbuf("xa", (P, DT * TPC), F32)
        sb_ln = cx.sbuf("lnx", (P, DT * TPC), BF16)
        sb_w = cx.sbuf("wbuf", (P, DT * DM), BF16)
        sb_o = cx.sbuf("obuf", (P, DT * TPC), BF16)

        prog.load(_wload(sb_xa, d_x, TPC), writes=("xa",), tag="ldx")
        emit_ln_fm(cx, sb_xa, "xa", TPC, sb_ln, "lnx", eps=1e-5)

        for wi, (d_w, d_out) in enumerate([(d_wq, d_qT), (d_wk, d_kT)]):
            prog.load(_wload(sb_w, d_w, DM), writes=("wbuf",),
                      queue="gpsimd", tag=f"ldw{wi}")
            for mo in range(DT):
                for n0 in range(0, TPC, 512):
                    pb, pk = cx.bank()
                    for j in range(DT):
                        lhs = sb_w[:, j * DM + mo * P: j * DM + (mo + 1) * P]
                        rhs = sb_ln[:, j * TPC + n0: j * TPC + n0 + 512]
                        prog.t(lambda e, lhs=lhs, rhs=rhs, j=j, pb=pb: e.matmul(
                            pb[:], lhs, rhs, start=(j == 0), stop=(j == DT - 1)),
                            reads=("wbuf", "lnx"), writes=(pk,))
                    prog.v(lambda e, mo=mo, n0=n0, pb=pb: e.tensor_copy(
                        out=sb_o[:, mo * TPC + n0: mo * TPC + n0 + 512], in_=pb[:]),
                        reads=(pk,), writes=("obuf",))
            prog.store(lambda e, d_out=d_out: e.dma_start(
                out=d_out.rearrange("(a p) t -> p a t", p=P),
                in_=sb_o.ap().rearrange("p (a t) -> p a t", a=DT)),
                reads=("obuf",), tag=f"st{wi}", sem_key=f"st{wi}")

        prog.load(_wload(sb_w, d_wv, DM), writes=("wbuf",),
                  queue="gpsimd", tag="ldwv")
        for tt in range(16):
            for n0 in range(0, DM, 512):
                pb, pk = cx.bank()
                for j in range(DT):
                    lhs = sb_ln[:, j * TPC + tt * P: j * TPC + (tt + 1) * P]
                    rhs = sb_w[:, j * DM + n0: j * DM + n0 + 512]
                    prog.t(lambda e, lhs=lhs, rhs=rhs, j=j, pb=pb: e.matmul(
                        pb[:], lhs, rhs, start=(j == 0), stop=(j == DT - 1)),
                        reads=("wbuf", "lnx"), writes=(pk,))
                prog.v(lambda e, tt=tt, n0=n0, pb=pb: e.tensor_copy(
                    out=sb_o[:, tt * DM + n0: tt * DM + n0 + 512], in_=pb[:]),
                    reads=(pk,), writes=("obuf",))
        prog.store(lambda e: e.dma_start(
            out=d_v.rearrange("(a p) t -> p a t", p=P),
            in_=sb_o.ap().rearrange("p (a t) -> p a t", a=16)),
            reads=("obuf",), tag="stv", sem_key="stv")

        prog.finalize(stack)
    return nc


def build_attn(core):
    nc = bass.Bass("TRN2", target_bir_lowering=False)
    NKB = 4 * (core + 1)
    NK = NKB * 512
    d_qT = nc.dram_tensor("qT", [DM, TPC], BF16, kind="ExternalInput").ap()
    d_kT = nc.dram_tensor("kTf", [DM, SEQ], BF16, kind="ExternalInput").ap()
    d_v = nc.dram_tensor("vf", [SEQ, DM], BF16, kind="ExternalInput").ap()
    d_x = nc.dram_tensor("xT", [DM, TPC], F32, kind="ExternalInput").ap()
    d_out = nc.dram_tensor("xoT", [DM, TPC], F32, kind="ExternalOutput").ap()

    SCALE = 1.0 / 32.0
    with ExitStack() as stack:
        prog = Prog(nc)
        cx = Ctx(nc, prog, stack)
        cx.psum_banks(6)
        pt_ps = stack.enter_context(nc.psum_tensor("ptps", [P, 512], BF16))
        emit_const_ones(cx)
        identb = cx.sbuf("identb", (P, P), BF16)
        cx.prog.v(lambda e: e.tensor_copy(out=identb[:], in_=cx.sb["ident"][:]),
                  reads=("ident",), writes=("identb",))

        sb_q = cx.sbuf("qb", (P, DT * TPC), BF16)
        sb_o = cx.sbuf("ob", (P, 16 * DM), F32)
        sb_ml = cx.sbuf("ml", (P, 16 * 4), F32)
        sb_k = [cx.sbuf(f"kb{i}", (P, DT * 512), BF16) for i in range(2)]
        sb_v = [cx.sbuf(f"vb{i}", (P, 4 * DM), BF16) for i in range(2)]
        sb_p = cx.sbuf("pbuf", (P, 512), BF16)
        sb_pt = cx.sbuf("ptb", (P, 512), BF16)
        sb_msk = cx.sbuf("msk", (P, 3 * 512), F32)
        sb_st = cx.sbuf("sst", (P, 512), F32)
        sb_col = cx.sbuf("colw", (P, 2), F32)
        sb_xre = cx.sbuf("xre", (P, 2 * 512), F32)

        it = cx.sbuf("iotasrc", (P, 512), F32)
        prog.g(lambda e: e.iota(it[:], pattern=[[1, 512]], base=0,
                                channel_multiplier=-1,
                                allow_small_or_imprecise_dtypes=True),
               writes=("iotasrc",))
        for mi, r in enumerate((0, -128, -256)):
            prog.v(lambda e, mi=mi, r=r: e.tensor_scalar(
                out=sb_msk[:, mi * 512:(mi + 1) * 512], in0=it[:],
                scalar1=float(-r), scalar2=-1e30, op0=ALU.is_gt, op1=ALU.mult),
                reads=("iotasrc",), writes=("msk",))

        prog.load(_wload(sb_q, d_qT, TPC), writes=("qb",), tag="ldq")
        prog.v(lambda e: e.memset(sb_o[:], 0.0), writes=("ob",))
        prog.v(lambda e: e.memset(sb_ml[:], 0.0), writes=("ml",))
        for qt in range(16):
            prog.v(lambda e, qt=qt: e.memset(sb_ml[:, qt * 4: qt * 4 + 1], -1e30),
                   writes=("ml",))

        for kb in range(NKB):
            sl_k = kb % 2
            kkey, vkey = f"kb{sl_k}", f"vb{sl_k}"
            prog.load(lambda e, kb=kb, sl_k=sl_k: e.dma_start(
                out=sb_k[sl_k].ap().rearrange("p (a n) -> p a n", a=DT),
                in_=d_kT[:, kb * 512:(kb + 1) * 512].rearrange(
                    "(a p) n -> p a n", p=P)),
                writes=(kkey,), tag=f"ldk{kb}")
            prog.load(lambda e, kb=kb, sl_k=sl_k: e.dma_start(
                out=sb_v[sl_k].ap().rearrange("p (a n) -> p a n", a=4),
                in_=d_v[kb * 512:(kb + 1) * 512, :].rearrange(
                    "(a p) n -> p a n", p=P)),
                writes=(vkey,), tag=f"ldv{kb}", queue="gpsimd")
            for qt in range(16):
                q0 = core * TPC + qt * P
                r = kb * 512 - q0
                if r > 127:
                    continue
                pb, pk = cx.bank()
                for j in range(DT):
                    lhs = sb_q[:, j * TPC + qt * P: j * TPC + (qt + 1) * P]
                    rhs = sb_k[sl_k][:, j * 512:(j + 1) * 512]
                    prog.t(lambda e, lhs=lhs, rhs=rhs, j=j, pb=pb: e.matmul(
                        pb[:], lhs, rhs, start=(j == 0), stop=(j == DT - 1)),
                        reads=("qb", kkey), writes=(pk,))
                if r > -384:
                    mi = {0: 0, -128: 1, -256: 2}[r]
                    prog.v(lambda e, pb=pb, mi=mi: e.scalar_tensor_tensor(
                        out=sb_st[:], in0=pb[:], scalar=SCALE,
                        in1=sb_msk[:, mi * 512:(mi + 1) * 512],
                        op0=ALU.mult, op1=ALU.add),
                        reads=(pk, "msk"), writes=("sst",))
                else:
                    prog.v(lambda e, pb=pb: e.tensor_scalar(
                        out=sb_st[:], in0=pb[:], scalar1=SCALE, scalar2=None,
                        op0=ALU.mult),
                        reads=(pk,), writes=("sst",))
                mcol = sb_ml[:, qt * 4: qt * 4 + 1]
                lcol = sb_ml[:, qt * 4 + 1: qt * 4 + 2]
                acol = sb_ml[:, qt * 4 + 2: qt * 4 + 3]
                ncol = sb_ml[:, qt * 4 + 3: qt * 4 + 4]
                prog.v(lambda e: e.tensor_reduce(
                    out=sb_col[:, 0:1], in_=sb_st[:], axis=AX.X, op=ALU.max),
                    reads=("sst",), writes=("colw",))
                prog.v(lambda e, mcol=mcol: e.tensor_max(
                    sb_col[:, 1:2], sb_col[:, 0:1], mcol),
                    reads=("colw", "ml"), writes=("colw",))
                prog.v(lambda e, ncol=ncol: e.tensor_scalar(
                    out=ncol, in0=sb_col[:, 1:2], scalar1=-1.0, scalar2=None,
                    op0=ALU.mult), reads=("colw",), writes=("ml",))
                prog.s(lambda e, mcol=mcol, acol=acol, ncol=ncol: e.activation(
                    out=acol, in_=mcol, func=AF.Exp, bias=ncol),
                    reads=("ml",), writes=("ml",))
                prog.v(lambda e, mcol=mcol: e.tensor_copy(
                    out=mcol, in_=sb_col[:, 1:2]),
                    reads=("colw",), writes=("ml",))
                prog.s(lambda e, ncol=ncol: e.activation(
                    out=sb_p[:], in_=sb_st[:], func=AF.Exp, bias=ncol,
                    accum_out=sb_col[:, 0:1]),
                    reads=("sst", "ml"), writes=("pbuf", "colw"))
                prog.v(lambda e, lcol=lcol, acol=acol: e.scalar_tensor_tensor(
                    out=lcol, in0=lcol, scalar=1.0, in1=acol,
                    op0=ALU.mult, op1=ALU.mult),
                    reads=("ml",), writes=("ml",))
                prog.v(lambda e, lcol=lcol: e.tensor_add(
                    lcol, lcol, sb_col[:, 0:1]),
                    reads=("ml", "colw"), writes=("ml",))
                prog.v(lambda e, qt=qt, acol=acol: e.tensor_scalar(
                    out=sb_o[:, qt * DM:(qt + 1) * DM],
                    in0=sb_o[:, qt * DM:(qt + 1) * DM],
                    scalar1=acol, scalar2=None, op0=ALU.mult),
                    reads=("ob", "ml"), writes=("ob",))
                for i in range(4):
                    prog.t(lambda e, i=i: e.transpose(
                        pt_ps[:, i * P:(i + 1) * P],
                        sb_p[:, i * P:(i + 1) * P], identb[:]),
                        reads=("pbuf", "identb"), writes=("ptps",))
                prog.v(lambda e: e.tensor_copy(out=sb_pt[:], in_=pt_ps[:]),
                       reads=("ptps",), writes=("ptb",))
                for n0 in range(0, DM, 512):
                    pb2, pk2 = cx.bank()
                    for i in range(4):
                        lhs = sb_pt[:, i * P:(i + 1) * P]
                        rhs = sb_v[sl_k][:, i * DM + n0: i * DM + n0 + 512]
                        prog.t(lambda e, lhs=lhs, rhs=rhs, i=i, pb2=pb2:
                               e.matmul(pb2[:], lhs, rhs, start=(i == 0),
                                        stop=(i == 3)),
                               reads=("ptb", vkey), writes=(pk2,))
                    prog.v(lambda e, qt=qt, n0=n0, pb2=pb2: e.tensor_add(
                        sb_o[:, qt * DM + n0: qt * DM + n0 + 512],
                        sb_o[:, qt * DM + n0: qt * DM + n0 + 512], pb2[:]),
                        reads=(pk2, "ob"), writes=("ob",))

        for qt in range(16):
            lcol = sb_ml[:, qt * 4 + 1: qt * 4 + 2]
            prog.v(lambda e, lcol=lcol: e.reciprocal(
                out=sb_col[:, 0:1], in_=lcol), reads=("ml",), writes=("colw",))
            prog.v(lambda e, qt=qt: e.tensor_scalar(
                out=sb_o[:, qt * DM:(qt + 1) * DM],
                in0=sb_o[:, qt * DM:(qt + 1) * DM],
                scalar1=sb_col[:, 0:1], scalar2=None, op0=ALU.mult),
                reads=("ob", "colw"), writes=("ob",))
        for j in range(DT):
            for tt in range(16):
                pb, pk = cx.bank()
                prog.t(lambda e, j=j, tt=tt, pb=pb: e.transpose(
                    pb[:, 0:P], sb_o[:, tt * DM + j * P: tt * DM + (j + 1) * P],
                    cx.sb["ident"][:]),
                    reads=("ob", "ident"), writes=(pk,))
                slot = tt % 2
                xr = sb_xre[:, slot * 512: slot * 512 + P]
                xrkey = f"xre{slot}"
                prog.load(lambda e, j=j, tt=tt, xr=xr: e.dma_start(
                    out=xr, in_=d_x[j * P:(j + 1) * P, tt * P:(tt + 1) * P]),
                    writes=(xrkey,), tag="ldxre", sem_key=xrkey)
                prog.v(lambda e, pb=pb, xr=xr: e.tensor_add(
                    sb_st[:, 0:P], pb[:, 0:P], xr),
                    reads=(pk, xrkey), writes=("sst",))
                prog.store(lambda e, j=j, tt=tt: e.dma_start(
                    out=d_out[j * P:(j + 1) * P, tt * P:(tt + 1) * P],
                    in_=sb_st[:, 0:P]),
                    reads=("sst",), tag="stx", sem_key="stx")

        prog.finalize(stack)
    return nc




DI = 2048
DIT = DI // P

def build_mamba_pre():
    nc = bass.Bass("TRN2", target_bir_lowering=False)
    TEXT = TPC + 8
    d_x = nc.dram_tensor("xTe", [DM, TEXT], F32, kind="ExternalInput").ap()
    d_win = nc.dram_tensor("inprojT", [DM, 2 * DI], F32, kind="ExternalInput").ap()
    d_wxp = nc.dram_tensor("xprojT", [DI, 96], F32, kind="ExternalInput").ap()
    d_wdt = nc.dram_tensor("dtprojT", [64, DI], F32, kind="ExternalInput").ap()
    d_convw = nc.dram_tensor("convw", [4, P, DIT], F32, kind="ExternalInput").ap()
    d_convb = nc.dram_tensor("convb", [P, DIT], F32, kind="ExternalInput").ap()
    d_dtb = nc.dram_tensor("dtb", [P, DIT], F32, kind="ExternalInput").ap()
    d_uc = nc.dram_tensor("ucT", [DI, TPC], F32, kind="ExternalOutput").ap()
    d_res = nc.dram_tensor("resT", [DI, TPC], F32, kind="ExternalOutput").ap()
    d_dl = nc.dram_tensor("deltaT", [DI, TPC], F32, kind="ExternalOutput").ap()
    d_bc = nc.dram_tensor("bcT", [32, TPC], F32, kind="ExternalOutput").ap()

    CH = 512
    CHE = CH + 8

    with ExitStack() as stack:
        prog = Prog(nc)
        cx = Ctx(nc, prog, stack)
        cx.psum_banks(8)
        emit_const_ones(cx)

        sb_x = cx.sbuf("xa", (P, DT * CHE), F32)
        sb_ln = cx.sbuf("lnx", (P, DT * CHE), BF16)
        sb_win = cx.sbuf("win", (P, DT * 2 * DI), BF16)
        sb_wxp = cx.sbuf("wxp", (P, DIT * 96), BF16)
        sb_wdt = cx.sbuf("wdt", (64, DI), BF16)
        sb_cw = cx.sbuf("cwc", (P, 4 * DIT), F32)
        sb_cb = cx.sbuf("cbc", (P, DIT), F32)
        sb_dtb = cx.sbuf("dtbc", (P, DIT), F32)
        sb_ue = cx.sbuf("uext", (P, DIT * CHE), F32)
        sb_stg = cx.sbuf("stg", (P, DIT * CH), F32)
        sb_ucb = cx.sbuf("ucb", (P, DIT * CH), BF16)
        sb_cv = cx.sbuf("cvo", (P, CH), F32)
        sb_xd = cx.sbuf("xdbl", (96, CH), F32)
        sb_xdb = cx.sbuf("xdblb", (96, CH), BF16)
        sb_t1 = cx.sbuf("tt1", (P, CH), F32)
        sb_t2 = cx.sbuf("tt2", (P, CH), F32)

        prog.load(_wload(sb_win, d_win, 2 * DI), writes=("win",), queue="gpsimd")
        prog.load(_wload(sb_wxp, d_wxp, 96), writes=("wxp",), queue="gpsimd")
        prog.load(lambda e: e.dma_start(out=sb_wdt[:], in_=d_wdt[:]),
                  writes=("wdt",), queue="gpsimd")
        prog.load(lambda e: e.dma_start(out=sb_cw.ap().rearrange(
            "p (a n) -> p a n", a=4), in_=d_convw.rearrange("a p n -> p a n")),
            writes=("cwc",))
        prog.load(lambda e: e.dma_start(out=sb_cb[:], in_=d_convb[:]),
                  writes=("cbc",))
        prog.load(lambda e: e.dma_start(out=sb_dtb[:], in_=d_dtb[:]),
                  writes=("dtbc",))

        for ci in range(TPC // CH):
            c0 = ci * CH
            prog.load(lambda e, c0=c0: e.dma_start(
                out=sb_x.ap().rearrange("p (a n) -> p a n", a=DT),
                in_=d_x[:, c0:c0 + CHE].rearrange("(a p) n -> p a n", p=P)),
                writes=("xa",), tag=f"ldx{ci}")
            emit_ln_fm(cx, sb_x, "xa", CHE, sb_ln, "lnx", eps=1e-5)
            for mo in range(2 * DIT):
                pbA, pkA = cx.bank()
                for j in range(DT):
                    lhs = sb_win[:, j * 2 * DI + mo * P: j * 2 * DI + (mo + 1) * P]
                    rhs = sb_ln[:, j * CHE + 8: j * CHE + 8 + CH]
                    prog.t(lambda e, lhs=lhs, rhs=rhs, j=j, pbA=pbA: e.matmul(
                        pbA[:], lhs, rhs, start=(j == 0), stop=(j == DT - 1)),
                        reads=("win", "lnx"), writes=(pkA,))
                if mo < DIT:
                    prog.v(lambda e, mo=mo, pbA=pbA: e.tensor_copy(
                        out=sb_ue[:, mo * CHE + 8: mo * CHE + 8 + CH], in_=pbA[:]),
                        reads=(pkA,), writes=("uext",))
                    pbB, pkB = cx.bank()
                    for j in range(DT):
                        lhs = sb_win[:, j * 2 * DI + mo * P:
                                     j * 2 * DI + (mo + 1) * P]
                        rhs = sb_ln[:, j * CHE: j * CHE + 8]
                        prog.t(lambda e, lhs=lhs, rhs=rhs, j=j, pbB=pbB: e.matmul(
                            pbB[:, 0:8], lhs, rhs, start=(j == 0),
                            stop=(j == DT - 1)),
                            reads=("win", "lnx"), writes=(pkB,))
                    prog.v(lambda e, mo=mo, pbB=pbB: e.tensor_copy(
                        out=sb_ue[:, mo * CHE: mo * CHE + 8], in_=pbB[:, 0:8]),
                        reads=(pkB,), writes=("uext",))
                else:
                    prog.v(lambda e, mo=mo, pbA=pbA: e.tensor_copy(
                        out=sb_stg[:, (mo - DIT) * CH:(mo - DIT + 1) * CH],
                        in_=pbA[:]),
                        reads=(pkA,), writes=("stg",))
            prog.store(lambda e, c0=c0: e.dma_start(
                out=d_res[:, c0:c0 + CH].rearrange("(a p) n -> p a n", p=P),
                in_=sb_stg.ap().rearrange("p (a n) -> p a n", a=DIT)),
                reads=("stg",), tag=f"stres{ci}", sem_key="stg@st")
            for mo in range(DIT):
                w0 = sb_cw[:, 0 * DIT + mo: 0 * DIT + mo + 1]
                prog.v(lambda e, mo=mo, w0=w0: e.tensor_scalar(
                    out=sb_cv[:], in0=sb_ue[:, mo * CHE + 5: mo * CHE + 5 + CH],
                    scalar1=w0, scalar2=sb_cb[:, mo:mo + 1],
                    op0=ALU.mult, op1=ALU.add),
                    reads=("uext", "cwc", "cbc"), writes=("cvo",))
                for k in range(1, 4):
                    wk = sb_cw[:, k * DIT + mo: k * DIT + mo + 1]
                    prog.v(lambda e, mo=mo, k=k, wk=wk: e.scalar_tensor_tensor(
                        out=sb_cv[:], in0=sb_ue[:, mo * CHE + 5 + k:
                                                mo * CHE + 5 + k + CH],
                        scalar=wk, in1=sb_cv[:], op0=ALU.mult, op1=ALU.add),
                        reads=("uext", "cwc", "cvo"), writes=("cvo",))
                prog.s(lambda e, mo=mo: e.activation(
                    out=sb_stg[:, mo * CH:(mo + 1) * CH], in_=sb_cv[:],
                    func=AF.Silu),
                    reads=("cvo",), writes=("stg",))
                prog.s(lambda e, mo=mo: e.activation(
                    out=sb_ucb[:, mo * CH:(mo + 1) * CH], in_=sb_cv[:],
                    func=AF.Silu),
                    reads=("cvo",), writes=("ucb",))
            prog.store(lambda e, c0=c0: e.dma_start(
                out=d_uc[:, c0:c0 + CH].rearrange("(a p) n -> p a n", p=P),
                in_=sb_stg.ap().rearrange("p (a n) -> p a n", a=DIT)),
                reads=("stg",), tag=f"stuc{ci}", sem_key="stg@st")
            pbX, pkX = cx.bank()
            for j in range(DIT):
                lhs = sb_wxp[:, j * 96:(j + 1) * 96]
                rhs = sb_ucb[:, j * CH:(j + 1) * CH]
                prog.t(lambda e, lhs=lhs, rhs=rhs, j=j, pbX=pbX: e.matmul(
                    pbX[0:96, :], lhs, rhs, start=(j == 0), stop=(j == DIT - 1)),
                    reads=("wxp", "ucb"), writes=(pkX,))
            prog.v(lambda e, pbX=pbX: e.tensor_copy(out=sb_xd[:], in_=pbX[0:96, :]),
                   reads=(pkX,), writes=("xdbl",))
            prog.v(lambda e: e.tensor_copy(out=sb_xdb[:], in_=sb_xd[:]),
                   reads=("xdbl",), writes=("xdblb",))
            prog.store(lambda e, c0=c0: e.dma_start(
                out=d_bc[:, c0:c0 + CH], in_=sb_xd[64:96, :]),
                reads=("xdbl",), tag=f"stbc{ci}", sem_key="xdbl@st")
            for mo in range(DIT):
                pbD, pkD = cx.bank()
                lhs = sb_wdt[:, mo * P:(mo + 1) * P]
                prog.t(lambda e, lhs=lhs, pbD=pbD: e.matmul(
                    pbD[:], lhs, sb_xdb[0:64, :], start=True, stop=True),
                    reads=("wdt", "xdblb"), writes=(pkD,))
                bcol = sb_dtb[:, mo:mo + 1]
                prog.v(lambda e, pbD=pbD, bcol=bcol: e.tensor_scalar(
                    out=sb_t1[:], in0=pbD[:], scalar1=bcol, scalar2=None,
                    op0=ALU.add),
                    reads=(pkD, "dtbc"), writes=("tt1",))
                prog.s(lambda e: e.activation(out=sb_t2[:], in_=sb_t1[:],
                                              func=AF.Abs),
                       reads=("tt1",), writes=("tt2",))
                prog.s(lambda e: e.activation(out=sb_t2[:], in_=sb_t2[:],
                                              func=AF.Exp, scale=-1.0),
                       reads=("tt2",), writes=("tt2",))
                prog.s(lambda e: e.activation(out=sb_t2[:], in_=sb_t2[:],
                                              func=AF.Ln, bias=1.0),
                       reads=("tt2",), writes=("tt2",))
                prog.v(lambda e: e.tensor_scalar(
                    out=sb_t1[:], in0=sb_t1[:], scalar1=0.0, scalar2=None,
                    op0=ALU.max),
                    reads=("tt1",), writes=("tt1",))
                prog.v(lambda e, mo=mo: e.tensor_add(
                    sb_stg[:, mo * CH:(mo + 1) * CH], sb_t1[:], sb_t2[:]),
                    reads=("tt1", "tt2"), writes=("stg",))
            prog.store(lambda e, c0=c0: e.dma_start(
                out=d_dl[:, c0:c0 + CH].rearrange("(a p) n -> p a n", p=P),
                in_=sb_stg.ap().rearrange("p (a n) -> p a n", a=DIT)),
                reads=("stg",), tag=f"stdl{ci}", sem_key="stg@st")

        prog.finalize(stack)
    return nc




def build_scan():
    nc = bass.Bass("TRN2", target_bir_lowering=False)
    DSL = DI // NCORES
    NJJ = DSL // P
    TC = 2048
    d_dl = nc.dram_tensor("deltaS", [DSL, SEQ], F32, kind="ExternalInput").ap()
    d_uc = nc.dram_tensor("ucS", [DSL, SEQ], F32, kind="ExternalInput").ap()
    d_res = nc.dram_tensor("resS", [DSL, SEQ], F32, kind="ExternalInput").ap()
    d_B = nc.dram_tensor("Bf", [16, SEQ], F32, kind="ExternalInput").ap()
    d_C = nc.dram_tensor("Cf", [16, SEQ], F32, kind="ExternalInput").ap()
    d_al = nc.dram_tensor("alog", [P, NJJ * 16], F32, kind="ExternalInput").ap()
    d_ds = nc.dram_tensor("dskip", [P, NJJ], F32, kind="ExternalInput").ap()
    d_y = nc.dram_tensor("yS", [DSL, SEQ], F32, kind="ExternalOutput").ap()

    with ExitStack() as stack:
        prog = Prog(nc)
        cx = Ctx(nc, prog, stack)
        cx.psum_banks(8)
        emit_const_ones(cx)

        sb_dl = cx.sbuf("dlc", (P, NJJ * TC), F32)
        sb_uc = cx.sbuf("ucc", (P, NJJ * TC), F32)
        sb_du = cx.sbuf("duc", (P, NJJ * TC), F32)
        sb_res = cx.sbuf("resc", (P, NJJ * TC), F32)
        sb_B = cx.sbuf("Bc", (16, TC), F32)
        sb_C = cx.sbuf("Cc", (16, TC), F32)
        sb_Br = cx.sbuf("Brep", (P, TC), F32)
        sb_Cr = cx.sbuf("Crep", (P, TC), F32)
        sb_a = cx.sbuf("abuf", (P, TC), F32)
        sb_b = cx.sbuf("bbuf", (P, TC), F32)
        sb_h = cx.sbuf("hbuf2", (P, TC), F32)
        sb_tm = cx.sbuf("tmc", (P, TC), F32)
        sb_y = cx.sbuf("ybuf", (P, NJJ * TC), F32)
        sb_sr = cx.sbuf("srbuf", (P, TC), F32)
        sb_ac = cx.sbuf("acol", (P, NJJ * 16), F32)
        sb_st = cx.sbuf("state", (P, NJJ * 16), F32)
        sb_ds = cx.sbuf("dscol", (P, NJJ), F32)
        sb_es = cx.sbuf("esel16", (16, 16 * P), F32)

        prog.g(lambda e: e.iota(sb_es[:], pattern=[[1, 16], [0, P]], base=0,
                                channel_multiplier=-1,
                                allow_small_or_imprecise_dtypes=True),
               writes=("esel16",))
        prog.v(lambda e: e.tensor_scalar(
            out=sb_es[:], in0=sb_es[:], scalar1=0.0, scalar2=None,
            op0=ALU.is_equal), reads=("esel16",), writes=("esel16",))

        prog.load(lambda e: e.dma_start(out=sb_ac[:], in_=d_al[:]),
                  writes=("acol",))
        prog.s(lambda e: e.activation(out=sb_ac[:], in_=sb_ac[:], func=AF.Exp),
               reads=("acol",), writes=("acol",))
        prog.v(lambda e: e.tensor_scalar(
            out=sb_ac[:], in0=sb_ac[:], scalar1=-1.0, scalar2=None,
            op0=ALU.mult), reads=("acol",), writes=("acol",))
        prog.load(lambda e: e.dma_start(out=sb_ds[:], in_=d_ds[:]),
                  writes=("dscol",))
        prog.v(lambda e: e.memset(sb_st[:], 0.0), writes=("state",))

        for tc in range(SEQ // TC):
            t0 = tc * TC
            for name, sb, d in [("dlc", sb_dl, d_dl), ("ucc", sb_uc, d_uc),
                                ("resc", sb_res, d_res)]:
                prog.load(lambda e, sb=sb, d=d, t0=t0: e.dma_start(
                    out=sb.ap().rearrange("p (a n) -> p a n", a=NJJ),
                    in_=d[:, t0:t0 + TC].rearrange("(a p) n -> p a n", p=P)),
                    writes=(name,), tag=f"ld{name}{tc}")
            prog.load(lambda e, t0=t0: e.dma_start(
                out=sb_B[:], in_=d_B[:, t0:t0 + TC]), writes=("Bc",))
            prog.load(lambda e, t0=t0: e.dma_start(
                out=sb_C[:], in_=d_C[:, t0:t0 + TC]), writes=("Cc",))
            for jj in range(NJJ):
                prog.v(lambda e, jj=jj: e.tensor_mul(
                    sb_du[:, jj * TC:(jj + 1) * TC],
                    sb_dl[:, jj * TC:(jj + 1) * TC],
                    sb_uc[:, jj * TC:(jj + 1) * TC]),
                    reads=("dlc", "ucc"), writes=("duc",))
                prog.v(lambda e, jj=jj: e.memset(
                    sb_y[:, jj * TC:(jj + 1) * TC], 0.0), writes=("ybuf",))
            for n in range(16):
                for c0 in range(0, TC, 512):
                    pb, pk = cx.bank()
                    prog.t(lambda e, n=n, c0=c0, pb=pb: e.matmul(
                        pb[:], sb_es[:, n * P:(n + 1) * P],
                        sb_B[:, c0:c0 + 512], start=True, stop=True),
                        reads=("esel16", "Bc"), writes=(pk,))
                    prog.v(lambda e, c0=c0, pb=pb: e.tensor_copy(
                        out=sb_Br[:, c0:c0 + 512], in_=pb[:]),
                        reads=(pk,), writes=("Brep",))
                    pb2, pk2 = cx.bank()
                    prog.t(lambda e, n=n, c0=c0, pb2=pb2: e.matmul(
                        pb2[:], sb_es[:, n * P:(n + 1) * P],
                        sb_C[:, c0:c0 + 512], start=True, stop=True),
                        reads=("esel16", "Cc"), writes=(pk2,))
                    prog.v(lambda e, c0=c0, pb2=pb2: e.tensor_copy(
                        out=sb_Cr[:, c0:c0 + 512], in_=pb2[:]),
                        reads=(pk2,), writes=("Crep",))
                for jj in range(NJJ):
                    acol = sb_ac[:, jj * 16 + n: jj * 16 + n + 1]
                    stcol = sb_st[:, jj * 16 + n: jj * 16 + n + 1]
                    prog.s(lambda e, jj=jj, acol=acol: e.activation(
                        out=sb_a[:], in_=sb_dl[:, jj * TC:(jj + 1) * TC],
                        func=AF.Exp, scale=acol),
                        reads=("dlc", "acol"), writes=("abuf",))
                    prog.v(lambda e, jj=jj: e.tensor_mul(
                        sb_b[:], sb_du[:, jj * TC:(jj + 1) * TC], sb_Br[:]),
                        reads=("duc", "Brep"), writes=("bbuf",))
                    prog.v(lambda e, stcol=stcol: e.tensor_tensor_scan(
                        out=sb_h[:], data0=sb_a[:], data1=sb_b[:],
                        initial=stcol, op0=ALU.mult, op1=ALU.add),
                        reads=("abuf", "bbuf", "state"), writes=("hbuf2",))
                    prog.v(lambda e, stcol=stcol: e.tensor_copy(
                        out=stcol, in_=sb_h[:, TC - 1:TC]),
                        reads=("hbuf2",), writes=("state",))
                    prog.g(lambda e: e.tensor_mul(sb_tm[:], sb_h[:], sb_Cr[:]),
                           reads=("hbuf2", "Crep"), writes=("tmc",))
                    prog.v(lambda e, jj=jj: e.tensor_add(
                        sb_y[:, jj * TC:(jj + 1) * TC],
                        sb_y[:, jj * TC:(jj + 1) * TC], sb_tm[:]),
                        reads=("ybuf", "tmc"), writes=("ybuf",))
            for jj in range(NJJ):
                dcol = sb_ds[:, jj:jj + 1]
                prog.v(lambda e, jj=jj, dcol=dcol: e.scalar_tensor_tensor(
                    out=sb_y[:, jj * TC:(jj + 1) * TC],
                    in0=sb_uc[:, jj * TC:(jj + 1) * TC], scalar=dcol,
                    in1=sb_y[:, jj * TC:(jj + 1) * TC],
                    op0=ALU.mult, op1=ALU.add),
                    reads=("ucc", "dscol", "ybuf"), writes=("ybuf",))
                prog.s(lambda e, jj=jj: e.activation(
                    out=sb_sr[:], in_=sb_res[:, jj * TC:(jj + 1) * TC],
                    func=AF.Silu),
                    reads=("resc",), writes=("srbuf",))
                prog.v(lambda e, jj=jj: e.tensor_mul(
                    sb_y[:, jj * TC:(jj + 1) * TC],
                    sb_y[:, jj * TC:(jj + 1) * TC], sb_sr[:]),
                    reads=("ybuf", "srbuf"), writes=("ybuf",))
            prog.store(lambda e, t0=t0: e.dma_start(
                out=d_y[:, t0:t0 + TC].rearrange("(a p) n -> p a n", p=P),
                in_=sb_y.ap().rearrange("p (a n) -> p a n", a=NJJ)),
                reads=("ybuf",), tag=f"sty{tc}", sem_key="ybuf@st")

        prog.finalize(stack)
    return nc




def build_mamba_mid():
    nc = bass.Bass("TRN2", target_bir_lowering=False)
    d_y = nc.dram_tensor("yT", [DI, TPC], F32, kind="ExternalInput").ap()
    d_x1 = nc.dram_tensor("x1T", [DM, TPC], F32, kind="ExternalInput").ap()
    d_wo = nc.dram_tensor("outprojT", [DI, DM], F32, kind="ExternalInput").ap()
    d_l0g = nc.dram_tensor("ln0g", [P, DT], F32, kind="ExternalInput").ap()
    d_l0b = nc.dram_tensor("ln0b", [P, DT], F32, kind="ExternalInput").ap()
    d_wq = nc.dram_tensor("qwT", [DM, DM], F32, kind="ExternalInput").ap()
    d_qb = nc.dram_tensor("qb", [P, DT], F32, kind="ExternalInput").ap()
    d_wk = nc.dram_tensor("kwT", [DM, DM], F32, kind="ExternalInput").ap()
    d_kb = nc.dram_tensor("kb", [P, DT], F32, kind="ExternalInput").ap()
    d_wg = nc.dram_tensor("wgc", [P, DT], F32, kind="ExternalInput").ap()
    d_h = nc.dram_tensor("hT", [DM, TPC], F32, kind="ExternalOutput").ap()
    d_k = nc.dram_tensor("kT", [DM, TPC], F32, kind="ExternalOutput").ap()
    d_pp = nc.dram_tensor("pp", [P, 9], F32, kind="ExternalOutput").ap()

    CH = 512
    with ExitStack() as stack:
        prog = Prog(nc)
        cx = Ctx(nc, prog, stack)
        cx.psum_banks(8)
        emit_const_ones(cx)

        sb_yb = cx.sbuf("ybf", (P, DIT * CH), BF16)
        sb_m = cx.sbuf("marr", (P, DT * CH), F32)
        sb_x1 = cx.sbuf("x1c", (P, DT * CH), F32)
        sb_l1q = cx.sbuf("ln1q", (P, DT * CH), F32)
        sb_hs = cx.sbuf("hstg", (P, DT * CH), F32)
        sb_hb = cx.sbuf("hbf", (P, DT * CH), BF16)
        sb_qg = cx.sbuf("qg", (P, DT * CH), BF16)
        sb_wo = cx.sbuf("wo", (P, DIT * DM), BF16)
        sb_wq = cx.sbuf("wq", (P, DT * DM), BF16)
        sb_wk = cx.sbuf("wk", (P, DT * DM), BF16)
        sb_wg = cx.sbuf("wgb", (P, DT), BF16)
        sb_cols = cx.sbuf("cols", (P, 5 * DT), F32)
        sb_rep = cx.sbuf("repb", (P, CH), F32)
        sb_sq = cx.sbuf("sqb", (P, CH), F32)
        sb_row = cx.sbuf("rows2", (1, 1024), F32)
        sb_qw = cx.sbuf("qwrow", (1, 512), F32)
        sb_p = cx.sbuf("pcols", (P, 10), F32)
        sb_junk = cx.sbuf("junk", (P, CH), F32)

        prog.load(_wload(sb_wo, d_wo, DM), writes=("wo",), queue="gpsimd")
        prog.load(_wload(sb_wq, d_wq, DM), writes=("wq",), queue="gpsimd")
        prog.load(_wload(sb_wk, d_wk, DM), writes=("wk",), queue="gpsimd")
        prog.load(lambda e: e.dma_start(out=sb_wg[:], in_=d_wg[:]),
                  writes=("wgb",), queue="gpsimd")
        for i, d in enumerate([d_l0g, d_l0b, d_qb, d_kb]):
            prog.load(lambda e, i=i, d=d: e.dma_start(
                out=sb_cols[:, i * DT:(i + 1) * DT], in_=d[:]),
                writes=("cols",), sem_key="cols")
        prog.v(lambda e: e.memset(sb_p[:], 0.0), writes=("pcols",))

        for ci in range(TPC // CH):
            c0 = ci * CH
            prog.load(lambda e, c0=c0: e.dma_start(
                out=sb_yb.ap().rearrange("p (a n) -> p a n", a=DIT),
                in_=d_y[:, c0:c0 + CH].rearrange("(a p) n -> p a n", p=P)),
                writes=("ybf",), queue="gpsimd", tag=f"ldy{ci}")
            prog.load(lambda e, c0=c0: e.dma_start(
                out=sb_x1.ap().rearrange("p (a n) -> p a n", a=DT),
                in_=d_x1[:, c0:c0 + CH].rearrange("(a p) n -> p a n", p=P)),
                writes=("x1c",), tag=f"ldx1{ci}")
            for md in range(DT):
                pb, pk = cx.bank()
                for j in range(DIT):
                    lhs = sb_wo[:, j * DM + md * P: j * DM + (md + 1) * P]
                    rhs = sb_yb[:, j * CH:(j + 1) * CH]
                    prog.t(lambda e, lhs=lhs, rhs=rhs, j=j, pb=pb: e.matmul(
                        pb[:], lhs, rhs, start=(j == 0), stop=(j == DIT - 1)),
                        reads=("wo", "ybf"), writes=(pk,))
                prog.v(lambda e, md=md, pb=pb: e.tensor_copy(
                    out=sb_m[:, md * CH:(md + 1) * CH], in_=pb[:]),
                    reads=(pk,), writes=("marr",))
            emit_ln_fm(cx, sb_x1, "x1c", CH, sb_l1q, "ln1q", eps=1e-5)
            for j in range(DT):
                prog.v(lambda e, j=j: e.tensor_add(
                    sb_m[:, j * CH:(j + 1) * CH], sb_m[:, j * CH:(j + 1) * CH],
                    sb_l1q[:, j * CH:(j + 1) * CH]),
                    reads=("marr", "ln1q"), writes=("marr",))
            emit_ln_fm(cx, sb_m, "marr", CH, sb_hs, "hstg", eps=1e-12,
                       gamma=sb_cols[:, 0:DT], beta=sb_cols[:, DT:2 * DT],
                       gb_buf="cols")
            prog.store(lambda e, c0=c0: e.dma_start(
                out=d_h[:, c0:c0 + CH].rearrange("(a p) n -> p a n", p=P),
                in_=sb_hs.ap().rearrange("p (a n) -> p a n", a=DT)),
                reads=("hstg",), tag=f"sth{ci}", sem_key="hstg@st")
            for j in range(DT):
                prog.v(lambda e, j=j: e.tensor_copy(
                    out=sb_hb[:, j * CH:(j + 1) * CH],
                    in_=sb_hs[:, j * CH:(j + 1) * CH]),
                    reads=("hstg",), writes=("hbf",))
            for which, wkey, wsb, bof, dst, dkey in (
                    ("q", "wq", sb_wq, 2, sb_l1q, "ln1q"),
                    ("k", "wk", sb_wk, 3, sb_m, "marr")):
                for mo in range(DT):
                    pb, pk = cx.bank()
                    for j in range(DT):
                        lhs = wsb[:, j * DM + mo * P: j * DM + (mo + 1) * P]
                        rhs = sb_hb[:, j * CH:(j + 1) * CH]
                        prog.t(lambda e, lhs=lhs, rhs=rhs, j=j, pb=pb: e.matmul(
                            pb[:], lhs, rhs, start=(j == 0), stop=(j == DT - 1)),
                            reads=(wkey, "hbf"), writes=(pk,))
                    bcol = sb_cols[:, bof * DT + mo: bof * DT + mo + 1]
                    prog.v(lambda e, mo=mo, pb=pb, bcol=bcol, dst=dst,
                           dkey=dkey: e.tensor_scalar(
                        out=dst[:, mo * CH:(mo + 1) * CH], in0=pb[:],
                        scalar1=bcol, scalar2=None, op0=ALU.add),
                        reads=(pk, "cols"), writes=(dkey,))
                pbS, pkS = cx.bank()
                for j in range(DT):
                    prog.s(lambda e, j=j, dst=dst, dkey=dkey: e.activation(
                        out=sb_sq[:], in_=dst[:, j * CH:(j + 1) * CH],
                        func=AF.Square),
                        reads=(dkey,), writes=("sqb",))
                    prog.t(lambda e, j=j, pbS=pbS: e.matmul(
                        pbS[0:1, :], cx.sb["ones_col"][:], sb_sq[:],
                        start=(j == 0), stop=(j == DT - 1)),
                        reads=("sqb", "ones_col"), writes=(pkS,))
                prog.s(lambda e, pbS=pbS: e.activation(
                    out=sb_row[0:1, 0:CH], in_=pbS[0:1, :], func=AF.Sqrt),
                    reads=(pkS,), writes=("rows2",))
                prog.v(lambda e: e.tensor_scalar(
                    out=sb_row[0:1, 0:CH], in0=sb_row[0:1, 0:CH],
                    scalar1=1e-12, scalar2=None, op0=ALU.max),
                    reads=("rows2",), writes=("rows2",))
                prog.v(lambda e: e.reciprocal(
                    out=sb_row[0:1, CH:2 * CH], in_=sb_row[0:1, 0:CH]),
                    reads=("rows2",), writes=("rows2",))
                emit_broadcast_row(cx, sb_row[0:1, CH:2 * CH], ("rows2",),
                                   sb_rep, "repb", CH)
                if which == "q":
                    for j in range(DT):
                        prog.v(lambda e, j=j: e.tensor_mul(
                            sb_l1q[:, j * CH:(j + 1) * CH],
                            sb_l1q[:, j * CH:(j + 1) * CH], sb_rep[:]),
                            reads=("ln1q", "repb"), writes=("ln1q",))
                        prog.s(lambda e, j=j: e.activation(
                            out=sb_qg[:, j * CH:(j + 1) * CH],
                            in_=sb_l1q[:, j * CH:(j + 1) * CH], func=AF.Gelu),
                            reads=("ln1q",), writes=("qg",))
                else:
                    for j in range(DT):
                        prog.v(lambda e, j=j: e.tensor_mul(
                            sb_m[:, j * CH:(j + 1) * CH],
                            sb_m[:, j * CH:(j + 1) * CH], sb_rep[:]),
                            reads=("marr", "repb"), writes=("marr",))
                    prog.store(lambda e, c0=c0: e.dma_start(
                        out=d_k[:, c0:c0 + CH].rearrange("(a p) n -> p a n", p=P),
                        in_=sb_m.ap().rearrange("p (a n) -> p a n", a=DT)),
                        reads=("marr",), tag=f"stk{ci}", sem_key="marr@st")
            pbW, pkW = cx.bank()
            for j in range(DT):
                lhs = sb_wg[:, j:j + 1]
                rhs = sb_qg[:, j * CH:(j + 1) * CH]
                prog.t(lambda e, lhs=lhs, rhs=rhs, j=j, pbW=pbW: e.matmul(
                    pbW[0:1, :], lhs, rhs, start=(j == 0), stop=(j == DT - 1)),
                    reads=("wgb", "qg"), writes=(pkW,))
            c_s, ck_s = cx.cc(1.0 / 32.0, parts=1)
            prog.s(lambda e, pbW=pbW, c=c_s: e.activation(
                out=sb_qw[0:1, 0:CH], in_=pbW[0:1, :], func=AF.Copy,
                scale=c),
                reads=(pkW, ck_s), writes=("qwrow",))
            prog.s(lambda e: e.activation(
                out=sb_row[0:1, 0:CH], in_=sb_qw[0:1, 0:CH],
                func=AF.Square),
                reads=("qwrow",), writes=("rows2",))
            prog.v(lambda e: e.tensor_reduce(
                out=sb_p[0:1, 9:10], in_=sb_row[0:1, 0:CH], axis=AX.X,
                op=ALU.add),
                reads=("rows2",), writes=("pcols",))
            prog.v(lambda e: e.tensor_add(
                sb_p[0:1, 8:9], sb_p[0:1, 8:9], sb_p[0:1, 9:10]),
                reads=("pcols",), writes=("pcols",))
            emit_broadcast_row(cx, sb_qw[0:1, 0:CH], ("qwrow",),
                               sb_rep, "repb", CH)
            for j in range(DT):
                prog.v(lambda e, j=j: e.scalar_tensor_tensor(
                    out=sb_junk[:], in0=sb_qg[:, j * CH:(j + 1) * CH],
                    scalar=1.0, in1=sb_rep[:], op0=ALU.mult, op1=ALU.mult,
                    accum_out=sb_p[:, 9:10]),
                    reads=("qg", "repb"), writes=("junk", "pcols"))
                prog.v(lambda e, j=j: e.tensor_add(
                    sb_p[:, j:j + 1], sb_p[:, j:j + 1], sb_p[:, 9:10]),
                    reads=("pcols",), writes=("pcols",))
        prog.store(lambda e: e.dma_start(out=d_pp[:], in_=sb_p[:, 0:9]),
                   reads=("pcols",), tag="stpp", sem_key="pcols@st")

        prog.finalize(stack)
    return nc



def build_mamba_post1():
    nc = bass.Bass("TRN2", target_bir_lowering=False)
    d_h = nc.dram_tensor("hT", [DM, TPC], F32, kind="ExternalInput").ap()
    d_k = nc.dram_tensor("kT", [DM, TPC], F32, kind="ExternalInput").ap()
    d_pp = nc.dram_tensor("ppall", [NCORES, P, 9], F32, kind="ExternalInput").ap()
    d_wp = nc.dram_tensor("projT", [DM, DM], F32, kind="ExternalInput").ap()
    d_pb = nc.dram_tensor("projb", [P, DT], F32, kind="ExternalInput").ap()
    d_wf = nc.dram_tensor("flT", [DM, DM], F32, kind="ExternalInput").ap()
    d_fb = nc.dram_tensor("flb", [P, DT], F32, kind="ExternalInput").ap()
    d_lig = nc.dram_tensor("lnig", [P, DT], F32, kind="ExternalInput").ap()
    d_lib = nc.dram_tensor("lnib", [P, DT], F32, kind="ExternalInput").ap()
    d_l1g = nc.dram_tensor("ln1g", [P, DT], F32, kind="ExternalInput").ap()
    d_l1b = nc.dram_tensor("ln1b", [P, DT], F32, kind="ExternalInput").ap()
    d_h2 = nc.dram_tensor("h2T", [DM, TPC], F32, kind="ExternalOutput").ap()

    CH = 1024
    with ExitStack() as stack:
        prog = Prog(nc)
        cx = Ctx(nc, prog, stack)
        cx.psum_banks(8)
        emit_const_ones(cx)

        sb_h = cx.sbuf("hc", (P, DT * CH), F32)
        sb_kb = cx.sbuf("kcb", (P, DT * CH), BF16)
        sb_gk = cx.sbuf("gkb", (P, DT * CH), BF16)
        sb_hs1 = cx.sbuf("hs1", (P, DT * CH), BF16)
        sb_lni = cx.sbuf("lnib2", (P, DT * CH), F32)
        sb_pr = cx.sbuf("prodb", (P, DT * CH), BF16)
        sb_sc = cx.sbuf("scr", (P, DT * CH), F32)
        sb_wp = cx.sbuf("wp", (P, DT * DM), BF16)
        sb_wf = cx.sbuf("wf", (P, DT * DM), BF16)
        sb_cols = cx.sbuf("cols", (P, 6 * DT), F32)
        sb_pp = cx.sbuf("ppb", (P, NCORES * 9), F32)
        sb_g = cx.sbuf("gcol", (P, 2), F32)
        sb_r1 = cx.sbuf("r1", (1, 2), F32)

        prog.load(_wload(sb_wp, d_wp, DM), writes=("wp",), queue="gpsimd")
        prog.load(_wload(sb_wf, d_wf, DM), writes=("wf",), queue="gpsimd")
        for i, d in enumerate([d_pb, d_fb, d_lig, d_lib, d_l1g, d_l1b]):
            prog.load(lambda e, i=i, d=d: e.dma_start(
                out=sb_cols[:, i * DT:(i + 1) * DT], in_=d[:]),
                writes=("cols",), sem_key="cols")
        prog.load(lambda e: e.dma_start(
            out=sb_pp.ap().rearrange("p (a n) -> p a n", a=NCORES),
            in_=d_pp.rearrange("a p n -> p a n")),
            writes=("ppb",))
        sb_gc = cx.sbuf("Gc", (P, DT), F32)
        prog.v(lambda e: e.memset(sb_gc[:], 0.0), writes=("Gc",))
        for c in range(NCORES):
            prog.v(lambda e, c=c: e.tensor_add(
                sb_gc[:], sb_gc[:], sb_pp[:, c * 9: c * 9 + 8]),
                reads=("ppb", "Gc"), writes=("Gc",))
        prog.v(lambda e: e.memset(sb_r1[:, 0:1], 0.0), writes=("r1",))
        for c in range(NCORES):
            prog.v(lambda e, c=c: e.tensor_add(
                sb_r1[0:1, 0:1], sb_r1[0:1, 0:1], sb_pp[0:1, c * 9 + 8:c * 9 + 9]),
                reads=("ppb", "r1"), writes=("r1",))
        prog.s(lambda e: e.activation(out=sb_r1[0:1, 1:2], in_=sb_r1[0:1, 0:1],
                                      func=AF.Sqrt),
               reads=("r1",), writes=("r1",))
        prog.v(lambda e: e.tensor_scalar(
            out=sb_r1[0:1, 1:2], in0=sb_r1[0:1, 1:2], scalar1=1e-12,
            scalar2=None, op0=ALU.max), reads=("r1",), writes=("r1",))
        prog.v(lambda e: e.reciprocal(out=sb_r1[0:1, 0:1], in_=sb_r1[0:1, 1:2]),
               reads=("r1",), writes=("r1",))
        pbG, pkG = cx.bank()
        prog.t(lambda e, pbG=pbG: e.matmul(
            pbG[:, 0:1], cx.sb["ones_row"][:], sb_r1[0:1, 0:1],
            start=True, stop=True),
            reads=("r1", "ones_row"), writes=(pkG,))
        prog.v(lambda e, pbG=pbG: e.tensor_copy(out=sb_g[:, 0:1], in_=pbG[:, 0:1]),
               reads=(pkG,), writes=("gcol",))
        for j in range(DT):
            prog.v(lambda e, j=j: e.tensor_mul(
                sb_gc[:, j:j + 1], sb_gc[:, j:j + 1], sb_g[:, 0:1]),
                reads=("Gc", "gcol"), writes=("Gc",))

        for ci in range(TPC // CH):
            c0 = ci * CH
            prog.load(lambda e, c0=c0: e.dma_start(
                out=sb_h.ap().rearrange("p (a n) -> p a n", a=DT),
                in_=d_h[:, c0:c0 + CH].rearrange("(a p) n -> p a n", p=P)),
                writes=("hc",), tag=f"ldh{ci}")
            prog.load(lambda e, c0=c0: e.dma_start(
                out=sb_kb.ap().rearrange("p (a n) -> p a n", a=DT),
                in_=d_k[:, c0:c0 + CH].rearrange("(a p) n -> p a n", p=P)),
                writes=("kcb",), queue="gpsimd", tag=f"ldk{ci}")
            for j in range(DT):
                prog.v(lambda e, j=j: e.tensor_scalar(
                    out=sb_gk[:, j * CH:(j + 1) * CH],
                    in0=sb_kb[:, j * CH:(j + 1) * CH],
                    scalar1=sb_gc[:, j:j + 1], scalar2=None, op0=ALU.mult),
                    reads=("kcb", "Gc"), writes=("gkb",))
            for mo in range(DT):
                for n0 in range(0, CH, 512):
                    pb, pk = cx.bank()
                    for j in range(DT):
                        lhs = sb_wp[:, j * DM + mo * P: j * DM + (mo + 1) * P]
                        rhs = sb_gk[:, j * CH + n0: j * CH + n0 + 512]
                        prog.t(lambda e, lhs=lhs, rhs=rhs, j=j, pb=pb: e.matmul(
                            pb[:], lhs, rhs, start=(j == 0), stop=(j == DT - 1)),
                            reads=("wp", "gkb"), writes=(pk,))
                    bcol = sb_cols[:, 0 * DT + mo: 0 * DT + mo + 1]
                    prog.v(lambda e, mo=mo, n0=n0, pb=pb, bcol=bcol:
                           e.tensor_scalar(
                        out=sb_hs1[:, mo * CH + n0: mo * CH + n0 + 512],
                        in0=pb[:], scalar1=bcol, scalar2=None, op0=ALU.add),
                        reads=(pk, "cols"), writes=("hs1",))
            emit_ln_fm(cx, sb_h, "hc", CH, sb_lni, "lnib2", eps=1e-12,
                       gamma=sb_cols[:, 2 * DT:3 * DT],
                       beta=sb_cols[:, 3 * DT:4 * DT], gb_buf="cols")
            for j in range(DT):
                prog.v(lambda e, j=j: e.tensor_mul(
                    sb_pr[:, j * CH:(j + 1) * CH],
                    sb_hs1[:, j * CH:(j + 1) * CH],
                    sb_lni[:, j * CH:(j + 1) * CH]),
                    reads=("hs1", "lnib2"), writes=("prodb",))
            for mo in range(DT):
                for n0 in range(0, CH, 512):
                    pb, pk = cx.bank()
                    for j in range(DT):
                        lhs = sb_wf[:, j * DM + mo * P: j * DM + (mo + 1) * P]
                        rhs = sb_pr[:, j * CH + n0: j * CH + n0 + 512]
                        prog.t(lambda e, lhs=lhs, rhs=rhs, j=j, pb=pb: e.matmul(
                            pb[:], lhs, rhs, start=(j == 0), stop=(j == DT - 1)),
                            reads=("wf", "prodb"), writes=(pk,))
                    bcol = sb_cols[:, 1 * DT + mo: 1 * DT + mo + 1]
                    prog.v(lambda e, mo=mo, n0=n0, pb=pb, bcol=bcol:
                           e.tensor_scalar(
                        out=sb_sc[:, mo * CH + n0: mo * CH + n0 + 512],
                        in0=pb[:], scalar1=bcol, scalar2=None, op0=ALU.add),
                        reads=(pk, "cols"), writes=("scr",))
            for j in range(DT):
                prog.v(lambda e, j=j: e.tensor_add(
                    sb_sc[:, j * CH:(j + 1) * CH],
                    sb_sc[:, j * CH:(j + 1) * CH],
                    sb_h[:, j * CH:(j + 1) * CH]),
                    reads=("scr", "hc"), writes=("scr",))
            emit_ln_fm(cx, sb_sc, "scr", CH, sb_lni, "lnib2", eps=1e-12,
                       gamma=sb_cols[:, 4 * DT:5 * DT],
                       beta=sb_cols[:, 5 * DT:6 * DT], gb_buf="cols")
            prog.store(lambda e, c0=c0: e.dma_start(
                out=d_h2[:, c0:c0 + CH].rearrange("(a p) n -> p a n", p=P),
                in_=sb_lni.ap().rearrange("p (a n) -> p a n", a=DT)),
                reads=("lnib2",), tag=f"sth2{ci}", sem_key="lnib2@st")

        prog.finalize(stack)
    return nc



def build_mamba_post2():
    nc = bass.Bass("TRN2", target_bir_lowering=False)
    d_h2 = nc.dram_tensor("h2T", [DM, TPC], F32, kind="ExternalInput").ap()
    d_x1 = nc.dram_tensor("x1T", [DM, TPC], F32, kind="ExternalInput").ap()
    d_w1 = nc.dram_tensor("d1T", [DM, 4096], F32, kind="ExternalInput").ap()
    d_b1 = nc.dram_tensor("d1b", [P, 32], F32, kind="ExternalInput").ap()
    d_w2 = nc.dram_tensor("d2T", [4096, DM], F32, kind="ExternalInput").ap()
    d_b2 = nc.dram_tensor("d2b", [P, DT], F32, kind="ExternalInput").ap()
    d_l2g = nc.dram_tensor("ln2g", [P, DT], F32, kind="ExternalInput").ap()
    d_l2b = nc.dram_tensor("ln2b", [P, DT], F32, kind="ExternalInput").ap()
    d_out = nc.dram_tensor("x2T", [DM, TPC], F32, kind="ExternalOutput").ap()

    HC = 1024
    HT = HC // P
    with ExitStack() as stack:
        prog = Prog(nc)
        cx = Ctx(nc, prog, stack)
        cx.psum_banks(8)
        emit_const_ones(cx)

        sb_h2b = cx.sbuf("h2b", (P, DT * TPC), BF16)
        sb_f = cx.sbuf("facc", (P, DT * TPC), F32)
        sb_w1 = cx.sbuf("w1c", (P, DT * HC), BF16)
        sb_w2 = cx.sbuf("w2c", (P, HT * DM), BF16)
        sb_hh = cx.sbuf("hh", (P, HT * TPC), BF16)
        sb_cols = cx.sbuf("cols", (P, 32 + 3 * DT), F32)
        FC = 256
        sb_pre = cx.sbuf("pre", (P, DT * FC), F32)
        sb_lo = cx.sbuf("lnout", (P, DT * FC), F32)
        sb_st1 = cx.sbuf("stream1", (P, FC), F32)
        sb_st2 = cx.sbuf("stream2", (P, FC), F32)

        prog.load(lambda e: e.dma_start(
            out=sb_h2b.ap().rearrange("p (a n) -> p a n", a=DT),
            in_=d_h2.rearrange("(a p) n -> p a n", p=P)),
            writes=("h2b",), queue="gpsimd", tag="ldh2")
        prog.load(lambda e: e.dma_start(out=sb_cols[:, 0:32], in_=d_b1[:]),
                  writes=("cols",), sem_key="cols")
        for i, d in enumerate([d_b2, d_l2g, d_l2b]):
            prog.load(lambda e, i=i, d=d: e.dma_start(
                out=sb_cols[:, 32 + i * DT:32 + (i + 1) * DT], in_=d[:]),
                writes=("cols",), sem_key="cols")

        for hc in range(4096 // HC):
            prog.load(lambda e, hc=hc: e.dma_start(
                out=sb_w1.ap().rearrange("p (a n) -> p a n", a=DT),
                in_=d_w1[:, hc * HC:(hc + 1) * HC].rearrange(
                    "(a p) n -> p a n", p=P)),
                writes=("w1c",), queue="gpsimd", tag=f"ldw1{hc}")
            prog.load(lambda e, hc=hc: e.dma_start(
                out=sb_w2.ap().rearrange("p (a n) -> p a n", a=HT),
                in_=d_w2[hc * HC:(hc + 1) * HC, :].rearrange(
                    "(a p) n -> p a n", p=P)),
                writes=("w2c",), queue="gpsimd", tag=f"ldw2{hc}")
            for mh in range(HT):
                for n0 in range(0, TPC, 512):
                    pb, pk = cx.bank()
                    for j in range(DT):
                        lhs = sb_w1[:, j * HC + mh * P: j * HC + (mh + 1) * P]
                        rhs = sb_h2b[:, j * TPC + n0: j * TPC + n0 + 512]
                        prog.t(lambda e, lhs=lhs, rhs=rhs, j=j, pb=pb: e.matmul(
                            pb[:], lhs, rhs, start=(j == 0), stop=(j == DT - 1)),
                            reads=("w1c", "h2b"), writes=(pk,))
                    bcol = sb_cols[:, hc * HT + mh: hc * HT + mh + 1]
                    prog.s(lambda e, mh=mh, n0=n0, pb=pb, bcol=bcol:
                           e.activation(
                        out=sb_hh[:, mh * TPC + n0: mh * TPC + n0 + 512],
                        in_=pb[:], func=AF.Gelu, bias=bcol),
                        reads=(pk, "cols"), writes=("hh",))
            for md in range(DT):
                for n0 in range(0, TPC, 512):
                    pb, pk = cx.bank()
                    for mh in range(HT):
                        lhs = sb_w2[:, mh * DM + md * P: mh * DM + (md + 1) * P]
                        rhs = sb_hh[:, mh * TPC + n0: mh * TPC + n0 + 512]
                        prog.t(lambda e, lhs=lhs, rhs=rhs, mh=mh, pb=pb:
                               e.matmul(pb[:], lhs, rhs, start=(mh == 0),
                                        stop=(mh == HT - 1)),
                               reads=("w2c", "hh"), writes=(pk,))
                    sl = slice(md * TPC + n0, md * TPC + n0 + 512)
                    if hc == 0:
                        prog.v(lambda e, sl=sl, pb=pb: e.tensor_copy(
                            out=sb_f[:, sl], in_=pb[:]),
                            reads=(pk,), writes=("facc",))
                    else:
                        prog.v(lambda e, sl=sl, pb=pb: e.tensor_add(
                            sb_f[:, sl], sb_f[:, sl], pb[:]),
                            reads=(pk, "facc"), writes=("facc",))

        for ci in range(TPC // FC):
            c0 = ci * FC
            for j in range(DT):
                prog.load(lambda e, j=j, c0=c0: e.dma_start(
                    out=sb_st1[:], in_=d_h2[j * P:(j + 1) * P, c0:c0 + FC]),
                    writes=("stream1",), tag="ldh2s")
                bcol = sb_cols[:, 32 + j: 32 + j + 1]
                prog.v(lambda e, j=j, c0=c0, bcol=bcol: e.tensor_scalar(
                    out=sb_st1[:], in0=sb_st1[:], scalar1=bcol, scalar2=None,
                    op0=ALU.add),
                    reads=("stream1", "cols"), writes=("stream1",))
                prog.v(lambda e, j=j, c0=c0: e.tensor_add(
                    sb_pre[:, j * FC:(j + 1) * FC],
                    sb_f[:, j * TPC + c0: j * TPC + c0 + FC], sb_st1[:]),
                    reads=("facc", "stream1"), writes=("pre",))
            emit_ln_fm(cx, sb_pre, "pre", FC, sb_lo, "lnout", eps=1e-12,
                       gamma=sb_cols[:, 32 + DT:32 + 2 * DT],
                       beta=sb_cols[:, 32 + 2 * DT:32 + 3 * DT], gb_buf="cols")
            for j in range(DT):
                prog.load(lambda e, j=j, c0=c0: e.dma_start(
                    out=sb_st2[:], in_=d_x1[j * P:(j + 1) * P, c0:c0 + FC]),
                    writes=("stream2",), tag="ldx1s")
                prog.v(lambda e, j=j: e.tensor_add(
                    sb_lo[:, j * FC:(j + 1) * FC],
                    sb_lo[:, j * FC:(j + 1) * FC], sb_st2[:]),
                    reads=("lnout", "stream2"), writes=("lnout",))
            prog.store(lambda e, c0=c0: e.dma_start(
                out=d_out[:, c0:c0 + FC].rearrange("(a p) n -> p a n", p=P),
                in_=sb_lo.ap().rearrange("p (a n) -> p a n", a=DT)),
                reads=("lnout",), tag=f"stx2{ci}", sem_key="lnout@st")

        prog.finalize(stack)
    return nc



_NC_CACHE = {}


def _get_nc(name, builder):
    if name not in _NC_CACHE:
        _NC_CACHE[name] = builder()
    return _NC_CACHE[name]


def _run(nc, in_maps, n=NCORES):
    return run_bass_kernel_spmd(nc, in_maps, core_ids=list(range(n)))


def _slices(xT):
    return [np.ascontiguousarray(xT[:, c * TPC:(c + 1) * TPC])
            for c in range(NCORES)]


def _ffn0_launch(xT, inputs):
    gate_wT = _t(inputs["L0_ffn_gate_w"])
    ebias = np.ascontiguousarray(
        np.asarray(inputs["L0_ffn_expert_bias"], np.float32).reshape(1, 8))
    wupT = np.stack([_t(inputs["L0_ffn_w_up"][e]) for e in range(8)]
                    + [_t(inputs["L0_ffn_shared_w_up"])])
    wdnT = np.stack([_t(inputs["L0_ffn_w_down"][e]) for e in range(8)]
                    + [_t(inputs["L0_ffn_shared_w_down"])])
    nc = _get_nc("ffn0", build_ffn0)
    xs = _slices(xT)
    im = [{"xT": xs[c], "gate_wT": gate_wT, "ebias": ebias,
           "wupT": wupT, "wdnT": wdnT} for c in range(NCORES)]
    res = _run(nc, im)
    return np.concatenate([np.asarray(res.results[c]["x1T"])
                           for c in range(NCORES)], axis=1)


def _ffn1_launch(xT, inputs):
    b1c = np.stack([col_layout(inputs["L1_ffn_b1"][e]) for e in range(8)])
    common = {
        "router_wT": _t(inputs["L1_ffn_router_w"]),
        "router_b": np.ascontiguousarray(
            np.asarray(inputs["L1_ffn_router_b"], np.float32).reshape(1, 8)),
        "lang_wT": _t(inputs["L1_ffn_lang_w"]),
        "lang_b": np.ascontiguousarray(
            np.asarray(inputs["L1_ffn_lang_b"], np.float32).reshape(1, 4)),
        "w1": np.ascontiguousarray(inputs["L1_ffn_w1"]),
        "b1c": b1c,
        "w2": np.ascontiguousarray(inputs["L1_ffn_w2"]),
        "b2": np.ascontiguousarray(inputs["L1_ffn_b2"]),
    }
    nc = _get_nc("ffn1", build_ffn1)
    xs = _slices(xT)
    im = [dict(common, xT=xs[c]) for c in range(NCORES)]
    res = _run(nc, im)
    return np.concatenate([np.asarray(res.results[c]["xoT"])
                           for c in range(NCORES)], axis=1)


def _mamba_launches(x1T, inputs):
    def ext_slice(c):
        out = np.zeros((DM, TPC + 8), np.float32)
        lo = c * TPC - 8
        src = x1T[:, max(lo, 0): c * TPC + TPC]
        out[:, (8 - (c * TPC - max(lo, 0))):] = src
        return np.ascontiguousarray(out)

    convw = np.stack([col_layout(inputs["L0_attn_conv_w"][:, k])
                      for k in range(4)])
    common = {
        "inprojT": _t(inputs["L0_attn_in_proj_w"]),
        "xprojT": _t(inputs["L0_attn_x_proj_w"]),
        "dtprojT": _t(inputs["L0_attn_dt_proj_w"]),
        "convw": convw,
        "convb": col_layout(inputs["L0_attn_conv_b"]),
        "dtb": col_layout(inputs["L0_attn_dt_proj_b"]),
    }
    nc = _get_nc("mpre", build_mamba_pre)
    res = _run(nc, [dict(common, xTe=ext_slice(c)) for c in range(NCORES)])
    ucT = np.concatenate([np.asarray(res.results[c]["ucT"])
                          for c in range(NCORES)], axis=1)
    resT = np.concatenate([np.asarray(res.results[c]["resT"])
                           for c in range(NCORES)], axis=1)
    dlT = np.concatenate([np.asarray(res.results[c]["deltaT"])
                          for c in range(NCORES)], axis=1)
    bcT = np.concatenate([np.asarray(res.results[c]["bcT"])
                          for c in range(NCORES)], axis=1)
    DSL = DI // NCORES
    alog = np.asarray(inputs["L0_attn_A_log"], np.float32)
    dskip = np.asarray(inputs["L0_attn_D_skip"], np.float32)
    Bf = np.ascontiguousarray(bcT[0:16])
    Cf = np.ascontiguousarray(bcT[16:32])
    im = []
    for c in range(NCORES):
        rows = slice(c * DSL, (c + 1) * DSL)
        al = alog[rows]
        al_c = np.ascontiguousarray(
            al.reshape(2, P, 16).transpose(1, 0, 2).reshape(P, 32))
        ds_c = np.ascontiguousarray(dskip[rows].reshape(2, P).T)
        im.append({"deltaS": np.ascontiguousarray(dlT[rows]),
                   "ucS": np.ascontiguousarray(ucT[rows]),
                   "resS": np.ascontiguousarray(resT[rows]),
                   "Bf": Bf, "Cf": Cf, "alog": al_c, "dskip": ds_c})
    nc = _get_nc("scan", build_scan)
    res = _run(nc, im)
    yT = np.concatenate([np.asarray(res.results[c]["yS"])
                         for c in range(NCORES)], axis=0)
    common = {
        "outprojT": _t(inputs["L0_attn_out_proj_w"]),
        "ln0g": col_layout(inputs["L0_attn_ln0_g"]),
        "ln0b": col_layout(inputs["L0_attn_ln0_b"]),
        "qwT": _t(inputs["L0_attn_q_w"]), "qb": col_layout(inputs["L0_attn_q_b"]),
        "kwT": _t(inputs["L0_attn_k_w"]), "kb": col_layout(inputs["L0_attn_k_b"]),
        "wgc": col_layout(np.asarray(inputs["L0_attn_w_g"])[:, 0]),
    }
    nc = _get_nc("mmid", build_mamba_mid)
    im = [dict(common,
               yT=np.ascontiguousarray(yT[:, c * TPC:(c + 1) * TPC]),
               x1T=np.ascontiguousarray(x1T[:, c * TPC:(c + 1) * TPC]))
          for c in range(NCORES)]
    res = _run(nc, im)
    hT = np.concatenate([np.asarray(res.results[c]["hT"])
                         for c in range(NCORES)], axis=1)
    kT = np.concatenate([np.asarray(res.results[c]["kT"])
                         for c in range(NCORES)], axis=1)
    pp = np.stack([np.asarray(res.results[c]["pp"]) for c in range(NCORES)])
    common = {
        "ppall": pp,
        "projT": _t(inputs["L0_attn_proj_w"]),
        "projb": col_layout(inputs["L0_attn_proj_b"]),
        "flT": _t(inputs["L0_attn_fl_w"]),
        "flb": col_layout(inputs["L0_attn_fl_b"]),
        "lnig": col_layout(inputs["L0_attn_lni_g"]),
        "lnib": col_layout(inputs["L0_attn_lni_b"]),
        "ln1g": col_layout(inputs["L0_attn_ln1_g"]),
        "ln1b": col_layout(inputs["L0_attn_ln1_b"]),
    }
    nc = _get_nc("mpost1", build_mamba_post1)
    im = [dict(common,
               hT=np.ascontiguousarray(hT[:, c * TPC:(c + 1) * TPC]),
               kT=np.ascontiguousarray(kT[:, c * TPC:(c + 1) * TPC]))
          for c in range(NCORES)]
    res = _run(nc, im)
    h2T = np.concatenate([np.asarray(res.results[c]["h2T"])
                          for c in range(NCORES)], axis=1)
    common = {
        "d1T": _t(inputs["L0_attn_d1_w"]), "d1b": col_layout(inputs["L0_attn_d1_b"]),
        "d2T": _t(inputs["L0_attn_d2_w"]), "d2b": col_layout(inputs["L0_attn_d2_b"]),
        "ln2g": col_layout(inputs["L0_attn_ln2_g"]),
        "ln2b": col_layout(inputs["L0_attn_ln2_b"]),
    }
    nc = _get_nc("mpost2", build_mamba_post2)
    im = [dict(common,
               h2T=np.ascontiguousarray(h2T[:, c * TPC:(c + 1) * TPC]),
               x1T=np.ascontiguousarray(x1T[:, c * TPC:(c + 1) * TPC]))
          for c in range(NCORES)]
    res = _run(nc, im)
    return np.concatenate([np.asarray(res.results[c]["x2T"])
                           for c in range(NCORES)], axis=1)


def _attn_launches(xT, inputs):
    nc = _get_nc("qkv", build_qkv)
    xs = _slices(xT)
    im = [{"xT": xs[c], "Wq": np.asarray(inputs["L1_attn_Wq"]),
           "Wk": np.asarray(inputs["L1_attn_Wk"]),
           "Wv": np.asarray(inputs["L1_attn_Wv"])} for c in range(NCORES)]
    res = _run(nc, im)
    qT = [np.asarray(res.results[c]["qT"]) for c in range(NCORES)]
    kT_full = np.ascontiguousarray(np.concatenate(
        [np.asarray(res.results[c]["kT"]) for c in range(NCORES)], axis=1))
    v_full = np.ascontiguousarray(np.concatenate(
        [np.asarray(res.results[c]["v"]) for c in range(NCORES)], axis=0))
    outs = []
    for c in range(NCORES):
        nca = _get_nc(f"attn{c}", lambda c=c: build_attn(c))
        r = _run(nca, [{"qT": qT[c], "kTf": kT_full, "vf": v_full,
                        "xT": xs[c]}], n=1)
        outs.append(np.asarray(r.results[0]["xoT"]))
    return np.concatenate(outs, axis=1)


def kernel_unfused(**inputs):
    x = np.asarray(inputs["L0_attn_input_tensor"], np.float32)[0]
    xT = np.ascontiguousarray(x.T)
    x1T = _ffn0_launch(xT, inputs)
    x2T = _mamba_launches(x1T, inputs)
    x3T = _ffn0_launch(x2T, inputs)
    x4T = _ffn1_launch(x3T, inputs)
    x5T = _attn_launches(x4T, inputs)
    x6T = _ffn1_launch(x5T, inputs)
    return np.ascontiguousarray(x6T.T)[None, :, :].astype(np.float32)




def emit_ffn0(cx, dd_x, kx, dd_out, kout, dd_gate, dd_ebias, dd_wup,
              dd_wdn, tail=None, ktail=None, debug=False):
    nc, prog = cx.nc, cx.prog
    NH = 1408
    NHT = NH // P
    CH = 1024
    NCH = TPC // CH
    if True:

        sb_xa = cx.sbuf("xa", (P, DT * TPC), F32)
        sb_ln = cx.sbuf("lnx", (P, DT * TPC), BF16)
        sb_gate = cx.sbuf("gate", (P, DT * 8), BF16)
        sb_route = cx.sbuf("route", (P, 16 * 8), F32)
        sb_routeB = cx.sbuf("routeB", (P, 16 * 8), F32)
        sb_mask = cx.sbuf("maskw", (P, 16 * 8), F32)
        sb_col = cx.sbuf("colw", (P, 2), F32)
        sb_ebias = cx.sbuf("ebiasr", (P, 8), F32)
        sb_eb_row = cx.sbuf("eb_row", (1, 8), F32)
        sb_cT = cx.sbuf("cT", (8, TPC), BF16)
        sb_esel = cx.sbuf("esel", (8, 8 * P), BF16)
        sb_crep = cx.sbuf("crep", (P, TPC), BF16)
        sb_xe = cx.sbuf("xe", (P, DT * CH), BF16)
        sb_wup = cx.sbuf("wup", (P, DT * NH), BF16)
        sb_wdn = cx.sbuf("wdn", (P, NHT * DM), BF16)
        sb_h = cx.sbuf("hbuf", (P, NHT * CH), BF16)
        sb_rt = cx.sbuf("relu_t", (P, 512), F32)
        sb_xre = cx.sbuf("xre", (P, 512), F32)

        prog.g(lambda e: e.iota(sb_esel[:], pattern=[[1, 8], [0, P]], base=0,
                                channel_multiplier=-1,
                                allow_small_or_imprecise_dtypes=True),
               writes=("esel",))
        prog.v(lambda e: e.tensor_scalar(
            out=sb_esel[:], in0=sb_esel[:], scalar1=0.0, scalar2=None,
            op0=ALU.is_equal), reads=("esel",), writes=("esel",))

        prog.load(_wload(sb_xa, dd_x, TPC), reads=(kx,), writes=("xa",), tag="ldx")
        emit_ln_fm(cx, sb_xa, "xa", TPC, sb_ln, "lnx", eps=1e-5)

        prog.load(_wload(sb_gate, dd_gate, 8), writes=("gate",), queue="gpsimd")
        prog.load(lambda e: e.dma_start(out=sb_eb_row[:], in_=dd_ebias[:]),
                  writes=("eb_row",))
        emit_broadcast_row(cx, sb_eb_row[:], ("eb_row",), sb_ebias, "ebiasr", 8)

        for tt in range(16):
            pb, pk = cx.bank()
            for j in range(DT):
                lhs = sb_ln[:, j * TPC + tt * P: j * TPC + (tt + 1) * P]
                prog.t(lambda e, lhs=lhs, j=j, pb=pb: e.matmul(
                    pb[:, 0:8], lhs, sb_gate[:, j * 8:(j + 1) * 8],
                    start=(j == 0), stop=(j == DT - 1)),
                    reads=("lnx", "gate"), writes=(pk,))
            sl = slice(tt * 8, (tt + 1) * 8)
            prog.s(lambda e, pb=pb, sl=sl: e.activation(
                out=sb_route[:, sl], in_=pb[:, 0:8], func=AF.Sigmoid),
                reads=(pk,), writes=("route",))
            prog.v(lambda e, sl=sl: e.tensor_add(
                sb_routeB[:, sl], sb_route[:, sl], sb_ebias[:, 0:8]),
                reads=("route", "ebiasr"), writes=("routeB",))
            prog.v(lambda e, sl=sl: e.tensor_reduce(
                out=sb_col[:, 0:1], in_=sb_routeB[:, sl], axis=AX.X, op=ALU.max),
                reads=("routeB",), writes=("colw",))
            prog.v(lambda e, sl=sl: e.tensor_scalar(
                out=sb_mask[:, sl], in0=sb_routeB[:, sl], scalar1=sb_col[:, 0:1],
                scalar2=None, op0=ALU.is_equal),
                reads=("routeB", "colw"), writes=("maskw",))
            prog.v(lambda e, sl=sl: e.scalar_tensor_tensor(
                out=sb_routeB[:, sl], in0=sb_mask[:, sl], scalar=-1e30,
                in1=sb_routeB[:, sl], op0=ALU.mult, op1=ALU.add),
                reads=("maskw", "routeB"), writes=("routeB",))
            prog.v(lambda e, sl=sl: e.tensor_reduce(
                out=sb_col[:, 1:2], in_=sb_routeB[:, sl], axis=AX.X, op=ALU.max),
                reads=("routeB",), writes=("colw",))
            prog.v(lambda e, sl=sl: e.tensor_scalar(
                out=sb_routeB[:, sl], in0=sb_routeB[:, sl], scalar1=sb_col[:, 1:2],
                scalar2=None, op0=ALU.is_equal),
                reads=("routeB", "colw"), writes=("routeB",))
            prog.v(lambda e, sl=sl: e.tensor_add(
                sb_mask[:, sl], sb_mask[:, sl], sb_routeB[:, sl]),
                reads=("maskw", "routeB"), writes=("maskw",))
            prog.v(lambda e, sl=sl: e.tensor_mul(
                sb_mask[:, sl], sb_mask[:, sl], sb_route[:, sl]),
                reads=("maskw", "route"), writes=("maskw",))
            pb2, pk2 = cx.bank()
            prog.t(lambda e, sl=sl, pb2=pb2: e.transpose(
                pb2[0:8, 0:P], sb_mask[:, sl], cx.sb["ident"][:]),
                reads=("maskw", "ident"), writes=(pk2,))
            prog.v(lambda e, tt=tt, pb2=pb2: e.tensor_copy(
                out=sb_cT[:, tt * P:(tt + 1) * P], in_=pb2[0:8, 0:P]),
                reads=(pk2,), writes=("cT",))

        if debug:
            prog.store(lambda e: e.dma_start(out=d_dbg_r[:], in_=sb_route[:]),
                       reads=("route",), tag="stdbgr")
            prog.store(lambda e: e.dma_start(out=d_dbg_m[:], in_=sb_mask[:]),
                       reads=("maskw",), tag="stdbgm")
            prog.store(lambda e: e.dma_start(out=d_dbg_c[:], in_=sb_cT[:]),
                       reads=("cT",), tag="stdbgc")
            prog.store(lambda e: e.dma_start(
                out=d_dbg_ln.rearrange("(a p) t -> p a t", p=P),
                in_=sb_ln.ap().rearrange("p (a t) -> p a t", a=DT)),
                reads=("lnx",), tag="stdbgl")

        for ei in range(9):
            prog.load(_wload(sb_wup, dd_wup[ei], NH),
                writes=("wup",), queue="gpsimd", tag=f"wup{ei}")
            prog.load(_wload(sb_wdn, dd_wdn[ei], DM),
                writes=("wdn",), queue="gpsimd", tag=f"wdn{ei}")
            if ei < 8:
                for c0 in range(0, TPC, 512):
                    pb, pk = cx.bank()
                    prog.t(lambda e, ei=ei, c0=c0, pb=pb: e.matmul(
                        pb[:, 0:512], sb_esel[:, ei * P:(ei + 1) * P],
                        sb_cT[:, c0:c0 + 512], start=True, stop=True),
                        reads=("cT", "esel"), writes=(pk,))
                    prog.v(lambda e, c0=c0, pb=pb: e.tensor_copy(
                        out=sb_crep[:, c0:c0 + 512], in_=pb[:, 0:512]),
                        reads=(pk,), writes=("crep",))
            for ch in range(NCH):
                tc0 = ch * CH
                if ei < 8:
                    for j in range(DT):
                        prog.v(lambda e, j=j, tc0=tc0: e.tensor_mul(
                            sb_xe[:, j * CH:(j + 1) * CH],
                            sb_ln[:, j * TPC + tc0: j * TPC + tc0 + CH],
                            sb_crep[:, tc0:tc0 + CH]),
                            reads=("lnx", "crep"), writes=("xe",))
                for mh in range(NHT):
                    for n0 in range(0, CH, 512):
                        pb, pk = cx.bank()
                        for j in range(DT):
                            if ei < 8:
                                rhs = sb_xe[:, j * CH + n0: j * CH + n0 + 512]
                                rkey = "xe"
                            else:
                                rhs = sb_ln[:, j * TPC + tc0 + n0:
                                            j * TPC + tc0 + n0 + 512]
                                rkey = "lnx"
                            lhs = sb_wup[:, j * NH + mh * P: j * NH + (mh + 1) * P]
                            prog.t(lambda e, lhs=lhs, rhs=rhs, j=j, pb=pb:
                                   e.matmul(pb[:], lhs, rhs, start=(j == 0),
                                            stop=(j == DT - 1)),
                                   reads=("wup", rkey), writes=(pk,))
                        prog.s(lambda e, pb=pb: e.activation(
                            out=sb_rt[:], in_=pb[:], func=AF.Relu),
                            reads=(pk,), writes=("relu_t",))
                        prog.v(lambda e, mh=mh, n0=n0: e.tensor_mul(
                            sb_h[:, mh * CH + n0: mh * CH + n0 + 512],
                            sb_rt[:], sb_rt[:]),
                            reads=("relu_t",), writes=("hbuf",))
                for md in range(DT):
                    for n0 in range(0, CH, 512):
                        pb, pk = cx.bank()
                        for mh in range(NHT):
                            lhs = sb_wdn[:, mh * DM + md * P: mh * DM + (md + 1) * P]
                            rhs = sb_h[:, mh * CH + n0: mh * CH + n0 + 512]
                            prog.t(lambda e, lhs=lhs, rhs=rhs, mh=mh, pb=pb:
                                   e.matmul(pb[:], lhs, rhs, start=(mh == 0),
                                            stop=(mh == NHT - 1)),
                                   reads=("wdn", "hbuf"), writes=(pk,))
                        dst = sb_xa[:, md * TPC + tc0 + n0:
                                    md * TPC + tc0 + n0 + 512]
                        if ei == 0:
                            prog.v(lambda e, dst=dst, pb=pb: e.tensor_copy(
                                out=dst, in_=pb[:]),
                                reads=(pk,), writes=("xa",))
                        else:
                            prog.v(lambda e, dst=dst, pb=pb: e.tensor_add(
                                dst, dst, pb[:]),
                                reads=(pk, "xa"), writes=("xa",))

        for j in range(DT):
            for n0 in range(0, TPC, 512):
                slot = (j * 4 + n0 // 512) % 2
                xr = sb_xre[:, 0:512] if slot == 0 else sb_rt[:, 0:512]
                xrkey = "xre" if slot == 0 else "relu_t"
                prog.load(lambda e, j=j, n0=n0, xr=xr: e.dma_start(
                    out=xr, in_=dd_x[j * P:(j + 1) * P, n0:n0 + 512]),
                    reads=(kx,), writes=(xrkey,), tag="ldxre")
                sl = slice(j * TPC + n0, j * TPC + n0 + 512)
                prog.v(lambda e, sl=sl, xr=xr: e.scalar_tensor_tensor(
                    out=sb_xa[:, sl], in0=sb_xa[:, sl], scalar=0.5,
                    in1=xr, op0=ALU.mult, op1=ALU.add),
                    reads=("xa", xrkey), writes=("xa",))
        prog.store(lambda e: e.dma_start(
            out=dd_out.rearrange("(a p) t -> p a t", p=P),
            in_=sb_xa.ap().rearrange("p (a t) -> p a t", a=DT)),
            reads=("xa",), writes=(kout,), tag="stx")

        
        if tail is not None:
            prog.store(lambda e: e.dma_start(
                out=tail.rearrange("(a p) t -> p a t", p=P),
                in_=sb_xa.ap().rearrange("p (a t) -> p a t", a=DT)
                [:, :, TPC - 8:TPC]),
                reads=("xa",), writes=(ktail,), tag="sttail")



def emit_ffn1(cx, dd_x, kx, dd_out, kout, dd_rw, dd_rb, dd_lw, dd_lb,
              dd_w1, dd_b1, dd_w2, dd_b2, debug=False):
    nc, prog = cx.nc, cx.prog
    HC = 1024
    NHC = 4096 // HC
    HT = HC // P
    if True:

        sb_xa = cx.sbuf("xa", (P, DT * TPC), F32)
        sb_ln = cx.sbuf("lnx", (P, DT * TPC), BF16)
        sb_rw = cx.sbuf("rw", (P, DT * 8), BF16)
        sb_lw = cx.sbuf("lw", (P, DT * 4), BF16)
        sb_rbr = cx.sbuf("rbr", (P, 8), F32)
        sb_lbr = cx.sbuf("lbr", (P, 4), F32)
        sb_row8 = cx.sbuf("row8", (1, 8), F32)
        sb_row4 = cx.sbuf("row4", (1, 4), F32)
        sb_lp = cx.sbuf("lp", (P, 4), F32)
        sb_ew = cx.sbuf("ew", (P, 8), F32)
        sb_route = cx.sbuf("route", (P, 8), F32)
        sb_routeB = cx.sbuf("routeB", (P, 8), F32)
        sb_mask = cx.sbuf("maskw", (P, 8), F32)
        sb_col = cx.sbuf("colw", (P, 4), F32)
        sb_cT = cx.sbuf("cT", (8, TPC), BF16)
        sb_esel = cx.sbuf("esel", (8, 8 * P), BF16)
        sb_wrep = cx.sbuf("wrep", (P, TPC), F32)
        sb_w1c = cx.sbuf("w1c", (P, DT * HC), BF16)
        sb_w2c = cx.sbuf("w2c", (P, HT * DM), BF16)
        sb_h = cx.sbuf("hbuf", (P, HT * TPC), BF16)
        sb_b1 = cx.sbuf("b1s", (P, 8 * 32), F32)
        sb_b2 = cx.sbuf("b2s", (8, DM), BF16)
        sb_tmp = cx.sbuf("tmp5", (P, 512), F32)
        sb_xre = cx.sbuf("xre", (P, 512), F32)

        prog.g(lambda e: e.iota(sb_esel[:], pattern=[[1, 8], [0, P]], base=0,
                                channel_multiplier=-1,
                                allow_small_or_imprecise_dtypes=True),
               writes=("esel",))
        prog.v(lambda e: e.tensor_scalar(
            out=sb_esel[:], in0=sb_esel[:], scalar1=0.0, scalar2=None,
            op0=ALU.is_equal), reads=("esel",), writes=("esel",))

        prog.load(_wload(sb_xa, dd_x, TPC), reads=(kx,), writes=("xa",), tag="ldx")
        emit_ln_fm(cx, sb_xa, "xa", TPC, sb_ln, "lnx", eps=1e-5)

        prog.load(_wload(sb_rw, dd_rw, 8), writes=("rw",), queue="gpsimd")
        prog.load(_wload(sb_lw, dd_lw, 4), writes=("lw",), queue="gpsimd")
        prog.load(lambda e: e.dma_start(out=sb_row8[:], in_=dd_rb[:]),
                  writes=("row8",))
        prog.load(lambda e: e.dma_start(out=sb_row4[:], in_=dd_lb[:]),
                  writes=("row4",))
        prog.load(lambda e: e.dma_start(out=sb_b1.ap().rearrange(
            "p (a n) -> p a n", a=8), in_=dd_b1.rearrange("a p n -> p a n")),
            writes=("b1s",))
        prog.load(lambda e: e.dma_start(out=sb_b2[:], in_=dd_b2[:]),
                  writes=("b2s",), queue="gpsimd")
        emit_broadcast_row(cx, sb_row8[:], ("row8",), sb_rbr, "rbr", 8)
        emit_broadcast_row(cx, sb_row4[:], ("row4",), sb_lbr, "lbr", 4)

        if debug:
            prog.store(lambda e: e.dma_start(
                out=d_dbg_ln.rearrange("(a p) t -> p a t", p=P),
                in_=sb_ln.ap().rearrange("p (a t) -> p a t", a=DT)),
                reads=("lnx",), tag="stdbgln")

        for tt in range(16):
            pb_l, pk_l = cx.bank()
            pb_r, pk_r = cx.bank()
            for j in range(DT):
                lhs = sb_ln[:, j * TPC + tt * P: j * TPC + (tt + 1) * P]
                prog.t(lambda e, lhs=lhs, j=j, pb=pb_l: e.matmul(
                    pb[:, 0:4], lhs, sb_lw[:, j * 4:(j + 1) * 4],
                    start=(j == 0), stop=(j == DT - 1)),
                    reads=("lnx", "lw"), writes=(pk_l,), tag=f"langmm{tt}_{j}")
            for j in range(DT):
                lhs = sb_ln[:, j * TPC + tt * P: j * TPC + (tt + 1) * P]
                prog.t(lambda e, lhs=lhs, j=j, pb=pb_r: e.matmul(
                    pb[:, 0:8], lhs, sb_rw[:, j * 8:(j + 1) * 8],
                    start=(j == 0), stop=(j == DT - 1)),
                    reads=("lnx", "rw"), writes=(pk_r,))
            if debug and tt == 0:
                sb_dbg2 = cx.sbuf("dbg2", (P, 8), F32)
                prog.v(lambda e, pb=pb_l: e.tensor_copy(out=sb_dbg2[:, 0:4], in_=pb[:, 0:4]),
                       reads=(pk_l,), writes=("dbg2",))
            prog.v(lambda e, pb=pb_l: e.tensor_add(
                sb_lp[:], pb[:, 0:4], sb_lbr[:, 0:4]),
                reads=(pk_l, "lbr"), writes=("lp",), tag=f"langevac{tt}")
            prog.v(lambda e: e.tensor_reduce(
                out=sb_col[:, 0:1], in_=sb_lp[:], axis=AX.X, op=ALU.max),
                reads=("lp",), writes=("colw",))
            prog.v(lambda e: e.tensor_scalar(
                out=sb_lp[:], in0=sb_lp[:], scalar1=sb_col[:, 0:1], scalar2=None,
                op0=ALU.subtract), reads=("lp", "colw"), writes=("lp",))
            prog.s(lambda e: e.activation(
                out=sb_lp[:], in_=sb_lp[:], func=AF.Exp,
                accum_out=sb_col[:, 1:2]),
                reads=("lp",), writes=("lp", "colw"))
            prog.v(lambda e: e.reciprocal(out=sb_col[:, 2:3], in_=sb_col[:, 1:2]),
                   reads=("colw",), writes=("colw",))
            prog.v(lambda e: e.tensor_scalar(
                out=sb_lp[:], in0=sb_lp[:], scalar1=sb_col[:, 2:3], scalar2=None,
                op0=ALU.mult), reads=("lp", "colw"), writes=("lp",))
            prog.v(lambda e: e.tensor_add(
                sb_ew[:, 0:3], sb_lp[:, 0:3], sb_lp[:, 1:4]),
                reads=("lp",), writes=("ew",))
            prog.v(lambda e: e.tensor_copy(out=sb_ew[:, 3:4], in_=sb_lp[:, 3:4]),
                   reads=("lp",), writes=("ew",))
            prog.v(lambda e: e.tensor_copy(out=sb_ew[:, 4:8], in_=sb_ew[:, 0:4]),
                   reads=("ew",), writes=("ew",))
            prog.v(lambda e, pb=pb_r: e.tensor_add(
                sb_route[:], pb[:, 0:8], sb_rbr[:, 0:8]),
                reads=(pk_r, "rbr"), writes=("route",))
            prog.v(lambda e: e.scalar_tensor_tensor(
                out=sb_route[:], in0=sb_ew[:], scalar=0.1, in1=sb_route[:],
                op0=ALU.mult, op1=ALU.add),
                reads=("ew", "route"), writes=("route",))
            prog.v(lambda e: e.tensor_reduce(
                out=sb_col[:, 0:1], in_=sb_route[:], axis=AX.X, op=ALU.max),
                reads=("route",), writes=("colw",))
            prog.v(lambda e: e.tensor_scalar(
                out=sb_route[:], in0=sb_route[:], scalar1=sb_col[:, 0:1],
                scalar2=None, op0=ALU.subtract),
                reads=("route", "colw"), writes=("route",))
            prog.s(lambda e: e.activation(
                out=sb_route[:], in_=sb_route[:], func=AF.Exp,
                accum_out=sb_col[:, 1:2]),
                reads=("route",), writes=("route", "colw"))
            prog.v(lambda e: e.reciprocal(out=sb_col[:, 2:3], in_=sb_col[:, 1:2]),
                   reads=("colw",), writes=("colw",))
            prog.v(lambda e: e.tensor_scalar(
                out=sb_route[:], in0=sb_route[:], scalar1=sb_col[:, 2:3],
                scalar2=None, op0=ALU.mult),
                reads=("route", "colw"), writes=("route",))
            prog.v(lambda e: e.tensor_reduce(
                out=sb_col[:, 0:1], in_=sb_route[:], axis=AX.X, op=ALU.max),
                reads=("route",), writes=("colw",))
            prog.v(lambda e: e.tensor_scalar(
                out=sb_mask[:], in0=sb_route[:], scalar1=sb_col[:, 0:1],
                scalar2=None, op0=ALU.is_equal),
                reads=("route", "colw"), writes=("maskw",))
            prog.v(lambda e: e.scalar_tensor_tensor(
                out=sb_routeB[:], in0=sb_mask[:], scalar=-1e30, in1=sb_route[:],
                op0=ALU.mult, op1=ALU.add),
                reads=("maskw", "route"), writes=("routeB",))
            prog.v(lambda e: e.tensor_reduce(
                out=sb_col[:, 1:2], in_=sb_routeB[:], axis=AX.X, op=ALU.max),
                reads=("routeB",), writes=("colw",))
            prog.v(lambda e: e.tensor_scalar(
                out=sb_routeB[:], in0=sb_routeB[:], scalar1=sb_col[:, 1:2],
                scalar2=None, op0=ALU.is_equal),
                reads=("routeB", "colw"), writes=("routeB",))
            prog.v(lambda e: e.tensor_add(
                sb_mask[:], sb_mask[:], sb_routeB[:]),
                reads=("maskw", "routeB"), writes=("maskw",))
            prog.v(lambda e: e.tensor_mul(
                sb_mask[:], sb_mask[:], sb_route[:]),
                reads=("maskw", "route"), writes=("maskw",))
            prog.v(lambda e: e.tensor_reduce(
                out=sb_col[:, 3:4], in_=sb_mask[:], axis=AX.X, op=ALU.add),
                reads=("maskw",), writes=("colw",))
            prog.v(lambda e: e.reciprocal(out=sb_col[:, 3:4], in_=sb_col[:, 3:4]),
                   reads=("colw",), writes=("colw",))
            prog.v(lambda e: e.tensor_scalar(
                out=sb_mask[:], in0=sb_mask[:], scalar1=sb_col[:, 3:4],
                scalar2=None, op0=ALU.mult),
                reads=("maskw", "colw"), writes=("maskw",))
            if debug and tt == 0:
                sb_dbgs = cx.sbuf("dbgs", (P, 40), F32)
                prog.v(lambda e: e.tensor_copy(out=sb_dbgs[:, 0:8], in_=sb_route[:]),
                       reads=("route",), writes=("dbgs",))
                prog.v(lambda e: e.tensor_copy(out=sb_dbgs[:, 8:16], in_=sb_mask[:]),
                       reads=("maskw",), writes=("dbgs",))
                prog.v(lambda e: e.tensor_copy(out=sb_dbgs[:, 16:20], in_=sb_lp[:]),
                       reads=("lp",), writes=("dbgs",))
                prog.v(lambda e: e.tensor_copy(out=sb_dbgs[:, 20:28], in_=sb_ew[:]),
                       reads=("ew",), writes=("dbgs",))
                prog.v(lambda e: e.tensor_copy(out=sb_dbgs[:, 28:32], in_=sb_col[:]),
                       reads=("colw",), writes=("dbgs",))
                prog.v(lambda e: e.tensor_copy(out=sb_dbgs[:, 32:40], in_=cx.sb["dbg2"][:, 0:8]),
                       reads=("dbg2",), writes=("dbgs",))
                prog.store(lambda e: e.dma_start(out=d_dbg_s[:], in_=sb_dbgs[:]),
                           reads=("dbgs",), tag="stdbgs")
            pb2, pk2 = cx.bank()
            prog.t(lambda e, pb2=pb2: e.transpose(
                pb2[0:8, 0:P], sb_mask[:], cx.sb["ident"][:]),
                reads=("maskw", "ident"), writes=(pk2,))
            prog.v(lambda e, tt=tt, pb2=pb2: e.tensor_copy(
                out=sb_cT[:, tt * P:(tt + 1) * P], in_=pb2[0:8, 0:P]),
                reads=(pk2,), writes=("cT",))
            if debug and tt == 0:
                prog.s(lambda e, pb=pb_l: e.activation(out=cx.sb["dbg2"][:, 4:8],
                                              in_=pb[:, 0:4], func=AF.Copy),
                       reads=(pk_l, pk2), writes=("dbg2",), tag="lateread")

        if debug:
            prog.store(lambda e: e.dma_start(out=d_dbg_c[:], in_=sb_cT[:]),
                       reads=("cT",), tag="stdbgc")

        for md in range(DT):
            for n0 in range(0, TPC, 512):
                pb, pk = cx.bank()
                prog.t(lambda e, md=md, n0=n0, pb=pb: e.matmul(
                    pb[:], sb_b2[:, md * P:(md + 1) * P], sb_cT[:, n0:n0 + 512],
                    start=True, stop=True),
                    reads=("b2s", "cT"), writes=(pk,))
                prog.v(lambda e, md=md, n0=n0, pb=pb: e.tensor_copy(
                    out=sb_xa[:, md * TPC + n0: md * TPC + n0 + 512], in_=pb[:]),
                    reads=(pk,), writes=("xa",))

        for ei in range(8):
            for c0 in range(0, TPC, 512):
                pb, pk = cx.bank()
                prog.t(lambda e, ei=ei, c0=c0, pb=pb: e.matmul(
                    pb[:, 0:512], sb_esel[:, ei * P:(ei + 1) * P],
                    sb_cT[:, c0:c0 + 512], start=True, stop=True),
                    reads=("cT", "esel"), writes=(pk,))
                prog.v(lambda e, c0=c0, pb=pb: e.tensor_copy(
                    out=sb_wrep[:, c0:c0 + 512], in_=pb[:, 0:512]),
                    reads=(pk,), writes=("wrep",))
            for hc in range(NHC):
                prog.load(lambda e, ei=ei, hc=hc: e.dma_start(
                    out=sb_w1c.ap().rearrange("p (a n) -> p a n", a=DT),
                    in_=dd_w1[ei, :, hc * HC:(hc + 1) * HC].rearrange(
                        "(a p) n -> p a n", p=P)),
                    writes=("w1c",), queue="gpsimd", tag=f"w1_{ei}_{hc}")
                prog.load(lambda e, ei=ei, hc=hc: e.dma_start(
                    out=sb_w2c.ap().rearrange("p (a n) -> p a n", a=HT),
                    in_=dd_w2[ei, hc * HC:(hc + 1) * HC, :].rearrange(
                        "(a p) n -> p a n", p=P)),
                    writes=("w2c",), queue="gpsimd", tag=f"w2_{ei}_{hc}")
                for mh in range(HT):
                    for n0 in range(0, TPC, 512):
                        pb, pk = cx.bank()
                        for j in range(DT):
                            lhs = sb_w1c[:, j * HC + mh * P: j * HC + (mh + 1) * P]
                            rhs = sb_ln[:, j * TPC + n0: j * TPC + n0 + 512]
                            prog.t(lambda e, lhs=lhs, rhs=rhs, j=j, pb=pb:
                                   e.matmul(pb[:], lhs, rhs, start=(j == 0),
                                            stop=(j == DT - 1)),
                                   reads=("w1c", "lnx"), writes=(pk,))
                        bcol = sb_b1[:, ei * 32 + hc * HT + mh:
                                     ei * 32 + hc * HT + mh + 1]
                        prog.s(lambda e, mh=mh, n0=n0, pb=pb, bcol=bcol:
                               e.activation(
                                   out=sb_h[:, mh * TPC + n0: mh * TPC + n0 + 512],
                                   in_=pb[:], func=AF.Gelu, bias=bcol),
                               reads=(pk, "b1s"), writes=("hbuf",))
                for md in range(DT):
                    for n0 in range(0, TPC, 512):
                        pb, pk = cx.bank()
                        for mh in range(HT):
                            lhs = sb_w2c[:, mh * DM + md * P: mh * DM + (md + 1) * P]
                            rhs = sb_h[:, mh * TPC + n0: mh * TPC + n0 + 512]
                            prog.t(lambda e, lhs=lhs, rhs=rhs, mh=mh, pb=pb:
                                   e.matmul(pb[:], lhs, rhs, start=(mh == 0),
                                            stop=(mh == HT - 1)),
                                   reads=("w2c", "hbuf"), writes=(pk,))
                        sl = slice(md * TPC + n0, md * TPC + n0 + 512)
                        prog.v(lambda e, sl=sl, n0=n0, pb=pb: e.tensor_mul(
                            sb_tmp[:], pb[:], sb_wrep[:, n0:n0 + 512]),
                            reads=(pk, "wrep"), writes=("tmp5",))
                        prog.v(lambda e, sl=sl: e.tensor_add(
                            sb_xa[:, sl], sb_xa[:, sl], sb_tmp[:]),
                            reads=("tmp5", "xa"), writes=("xa",))

        for j in range(DT):
            for n0 in range(0, TPC, 512):
                slot = (j * 4 + n0 // 512) % 2
                xr = sb_xre[:, 0:512] if slot == 0 else sb_tmp[:, 0:512]
                xrkey = "xre" if slot == 0 else "tmp5"
                prog.load(lambda e, j=j, n0=n0, xr=xr: e.dma_start(
                    out=xr, in_=dd_x[j * P:(j + 1) * P, n0:n0 + 512]),
                    reads=(kx,), writes=(xrkey,), tag="ldxre")
                sl = slice(j * TPC + n0, j * TPC + n0 + 512)
                prog.v(lambda e, sl=sl, xr=xr: e.scalar_tensor_tensor(
                    out=sb_xa[:, sl], in0=sb_xa[:, sl], scalar=0.5,
                    in1=xr, op0=ALU.mult, op1=ALU.add),
                    reads=("xa", xrkey), writes=("xa",))
        prog.store(lambda e: e.dma_start(
            out=dd_out.rearrange("(a p) t -> p a t", p=P),
            in_=sb_xa.ap().rearrange("p (a t) -> p a t", a=DT)),
            reads=("xa",), writes=(kout,), tag="stx")

        


def emit_mamba_pre(cx, dd_x, kx, dd_halo, khalo, dd_hmask, dd_uc, kuc,
                   dd_res, kres, dd_dl, kdl, dd_bc, kbc, dd_win, dd_wxp,
                   dd_wdt, dd_convw, dd_convb, dd_dtb):
    nc, prog = cx.nc, cx.prog
    TEXT = TPC + 8
    CH = 512
    CHE = CH + 8
    if True:

        sb_x = cx.sbuf("xa", (P, DT * CHE), F32)
        sb_ln = cx.sbuf("lnx", (P, DT * CHE), BF16)
        sb_win = cx.sbuf("win", (P, DT * 2 * DI), BF16)
        sb_wxp = cx.sbuf("wxp", (P, DIT * 96), BF16)
        sb_wdt = cx.sbuf("wdt", (64, DI), BF16)
        sb_cw = cx.sbuf("cwc", (P, 4 * DIT), F32)
        sb_cb = cx.sbuf("cbc", (P, DIT), F32)
        sb_dtb = cx.sbuf("dtbc", (P, DIT), F32)
        sb_ue = cx.sbuf("uext", (P, DIT * CHE), F32)
        sb_stg = cx.sbuf("stg", (P, DIT * CH), F32)
        sb_ucb = cx.sbuf("ucb", (P, DIT * CH), BF16)
        sb_cv = cx.sbuf("cvo", (P, CH), F32)
        sb_xd = cx.sbuf("xdbl", (96, CH), F32)
        sb_xdb = cx.sbuf("xdblb", (96, CH), BF16)
        sb_t1 = cx.sbuf("tt1", (P, CH), F32)
        sb_t2 = cx.sbuf("tt2", (P, CH), F32)

        prog.load(_wload(sb_win, dd_win, 2 * DI), writes=("win",), queue="gpsimd")
        prog.load(_wload(sb_wxp, dd_wxp, 96), writes=("wxp",), queue="gpsimd")
        prog.load(lambda e: e.dma_start(out=sb_wdt[:], in_=dd_wdt[:]),
                  writes=("wdt",), queue="gpsimd")
        prog.load(lambda e: e.dma_start(out=sb_cw.ap().rearrange(
            "p (a n) -> p a n", a=4), in_=dd_convw.rearrange("a p n -> p a n")),
            writes=("cwc",))
        prog.load(lambda e: e.dma_start(out=sb_cb[:], in_=dd_convb[:]),
                  writes=("cbc",))
        prog.load(lambda e: e.dma_start(out=sb_dtb[:], in_=dd_dtb[:]),
                  writes=("dtbc",))

        sb_hm = cx.sbuf("halomask_sb", (P, DT * 8), F32)
        prog.load(lambda e: e.dma_start(out=sb_hm[:], in_=dd_hmask[:]),
                  writes=("halomask",))
        for ci in list(range(1, TPC // CH)) + [0]:
            c0 = ci * CH
            if ci == 0:
                prog.load(lambda e: e.dma_start(
                    out=sb_x.ap().rearrange("p (a n) -> p a n", a=DT)[:, :, 0:8],
                    in_=dd_halo.rearrange("(a p) n -> p a n", p=P)),
                    reads=(khalo,), writes=("xa",), tag="ldhalo")
                prog.load(lambda e: e.dma_start(
                    out=sb_x.ap().rearrange("p (a n) -> p a n", a=DT)[:, :, 8:CHE],
                    in_=dd_x[:, 0:CH].rearrange("(a p) n -> p a n", p=P)),
                    reads=(kx,), writes=("xa",), tag="ldx0")
                for j in range(DT):
                    prog.v(lambda e, j=j: e.tensor_mul(
                        sb_x[:, j * CHE: j * CHE + 8],
                        sb_x[:, j * CHE: j * CHE + 8],
                        sb_hm[:, j * 8:(j + 1) * 8]),
                        reads=("xa", "halomask"), writes=("xa",))
            else:
                prog.load(lambda e, c0=c0: e.dma_start(
                    out=sb_x.ap().rearrange("p (a n) -> p a n", a=DT),
                    in_=dd_x[:, c0 - 8:c0 + CH].rearrange("(a p) n -> p a n", p=P)),
                    reads=(kx,), writes=("xa",), tag=f"ldx{ci}")
            emit_ln_fm(cx, sb_x, "xa", CHE, sb_ln, "lnx", eps=1e-5)
            for mo in range(2 * DIT):
                pbA, pkA = cx.bank()
                for j in range(DT):
                    lhs = sb_win[:, j * 2 * DI + mo * P: j * 2 * DI + (mo + 1) * P]
                    rhs = sb_ln[:, j * CHE + 8: j * CHE + 8 + CH]
                    prog.t(lambda e, lhs=lhs, rhs=rhs, j=j, pbA=pbA: e.matmul(
                        pbA[:], lhs, rhs, start=(j == 0), stop=(j == DT - 1)),
                        reads=("win", "lnx"), writes=(pkA,))
                if mo < DIT:
                    prog.v(lambda e, mo=mo, pbA=pbA: e.tensor_copy(
                        out=sb_ue[:, mo * CHE + 8: mo * CHE + 8 + CH], in_=pbA[:]),
                        reads=(pkA,), writes=("uext",))
                    pbB, pkB = cx.bank()
                    for j in range(DT):
                        lhs = sb_win[:, j * 2 * DI + mo * P:
                                     j * 2 * DI + (mo + 1) * P]
                        rhs = sb_ln[:, j * CHE: j * CHE + 8]
                        prog.t(lambda e, lhs=lhs, rhs=rhs, j=j, pbB=pbB: e.matmul(
                            pbB[:, 0:8], lhs, rhs, start=(j == 0),
                            stop=(j == DT - 1)),
                            reads=("win", "lnx"), writes=(pkB,))
                    prog.v(lambda e, mo=mo, pbB=pbB: e.tensor_copy(
                        out=sb_ue[:, mo * CHE: mo * CHE + 8], in_=pbB[:, 0:8]),
                        reads=(pkB,), writes=("uext",))
                else:
                    prog.v(lambda e, mo=mo, pbA=pbA: e.tensor_copy(
                        out=sb_stg[:, (mo - DIT) * CH:(mo - DIT + 1) * CH],
                        in_=pbA[:]),
                        reads=(pkA,), writes=("stg",))
            prog.store(lambda e, c0=c0: e.dma_start(
                out=dd_res[:, c0:c0 + CH].rearrange("(a p) n -> p a n", p=P),
                in_=sb_stg.ap().rearrange("p (a n) -> p a n", a=DIT)),
                reads=("stg",), writes=(kres,), tag=f"stres{ci}", sem_key="stg@st")
            for mo in range(DIT):
                w0 = sb_cw[:, 0 * DIT + mo: 0 * DIT + mo + 1]
                prog.v(lambda e, mo=mo, w0=w0: e.tensor_scalar(
                    out=sb_cv[:], in0=sb_ue[:, mo * CHE + 5: mo * CHE + 5 + CH],
                    scalar1=w0, scalar2=sb_cb[:, mo:mo + 1],
                    op0=ALU.mult, op1=ALU.add),
                    reads=("uext", "cwc", "cbc"), writes=("cvo",))
                for k in range(1, 4):
                    wk = sb_cw[:, k * DIT + mo: k * DIT + mo + 1]
                    prog.v(lambda e, mo=mo, k=k, wk=wk: e.scalar_tensor_tensor(
                        out=sb_cv[:], in0=sb_ue[:, mo * CHE + 5 + k:
                                                mo * CHE + 5 + k + CH],
                        scalar=wk, in1=sb_cv[:], op0=ALU.mult, op1=ALU.add),
                        reads=("uext", "cwc", "cvo"), writes=("cvo",))
                prog.s(lambda e, mo=mo: e.activation(
                    out=sb_stg[:, mo * CH:(mo + 1) * CH], in_=sb_cv[:],
                    func=AF.Silu),
                    reads=("cvo",), writes=("stg",))
                prog.s(lambda e, mo=mo: e.activation(
                    out=sb_ucb[:, mo * CH:(mo + 1) * CH], in_=sb_cv[:],
                    func=AF.Silu),
                    reads=("cvo",), writes=("ucb",))
            prog.store(lambda e, c0=c0: e.dma_start(
                out=dd_uc[:, c0:c0 + CH].rearrange("(a p) n -> p a n", p=P),
                in_=sb_stg.ap().rearrange("p (a n) -> p a n", a=DIT)),
                reads=("stg",), writes=(kuc,), tag=f"stuc{ci}", sem_key="stg@st")
            pbX, pkX = cx.bank()
            for j in range(DIT):
                lhs = sb_wxp[:, j * 96:(j + 1) * 96]
                rhs = sb_ucb[:, j * CH:(j + 1) * CH]
                prog.t(lambda e, lhs=lhs, rhs=rhs, j=j, pbX=pbX: e.matmul(
                    pbX[0:96, :], lhs, rhs, start=(j == 0), stop=(j == DIT - 1)),
                    reads=("wxp", "ucb"), writes=(pkX,))
            prog.v(lambda e, pbX=pbX: e.tensor_copy(out=sb_xd[:], in_=pbX[0:96, :]),
                   reads=(pkX,), writes=("xdbl",))
            prog.v(lambda e: e.tensor_copy(out=sb_xdb[:], in_=sb_xd[:]),
                   reads=("xdbl",), writes=("xdblb",))
            prog.store(lambda e, c0=c0: e.dma_start(
                out=dd_bc[:, c0:c0 + CH], in_=sb_xd[64:96, :]),
                reads=("xdbl",), writes=(kbc,), tag=f"stbc{ci}", sem_key="xdbl@st")
            for mo in range(DIT):
                pbD, pkD = cx.bank()
                lhs = sb_wdt[:, mo * P:(mo + 1) * P]
                prog.t(lambda e, lhs=lhs, pbD=pbD: e.matmul(
                    pbD[:], lhs, sb_xdb[0:64, :], start=True, stop=True),
                    reads=("wdt", "xdblb"), writes=(pkD,))
                bcol = sb_dtb[:, mo:mo + 1]
                prog.v(lambda e, pbD=pbD, bcol=bcol: e.tensor_scalar(
                    out=sb_t1[:], in0=pbD[:], scalar1=bcol, scalar2=None,
                    op0=ALU.add),
                    reads=(pkD, "dtbc"), writes=("tt1",))
                prog.s(lambda e: e.activation(out=sb_t2[:], in_=sb_t1[:],
                                              func=AF.Abs),
                       reads=("tt1",), writes=("tt2",))
                prog.s(lambda e: e.activation(out=sb_t2[:], in_=sb_t2[:],
                                              func=AF.Exp, scale=-1.0),
                       reads=("tt2",), writes=("tt2",))
                prog.s(lambda e: e.activation(out=sb_t2[:], in_=sb_t2[:],
                                              func=AF.Ln, bias=1.0),
                       reads=("tt2",), writes=("tt2",))
                prog.v(lambda e: e.tensor_scalar(
                    out=sb_t1[:], in0=sb_t1[:], scalar1=0.0, scalar2=None,
                    op0=ALU.max),
                    reads=("tt1",), writes=("tt1",))
                prog.v(lambda e, mo=mo: e.tensor_add(
                    sb_stg[:, mo * CH:(mo + 1) * CH], sb_t1[:], sb_t2[:]),
                    reads=("tt1", "tt2"), writes=("stg",))
            prog.store(lambda e, c0=c0: e.dma_start(
                out=dd_dl[:, c0:c0 + CH].rearrange("(a p) n -> p a n", p=P),
                in_=sb_stg.ap().rearrange("p (a n) -> p a n", a=DIT)),
                reads=("stg",), writes=(kdl,), tag=f"stdl{ci}", sem_key="stg@st")

        



def emit_scan(cx, ag_dl, kagdl, ag_uc, kaguc, ag_res, kagres, ag_bc, kagbc,
              dd_al, dd_ds, dd_y, ky):
    nc, prog = cx.nc, cx.prog
    DSL = DI // NCORES
    NJJ = DSL // P
    TC = 2048
    if True:

        sb_dl = cx.sbuf("dlc", (P, NJJ * TC), F32)
        sb_uc = cx.sbuf("ucc", (P, NJJ * TC), F32)
        sb_du = cx.sbuf("duc", (P, NJJ * TC), F32)
        sb_res = cx.sbuf("resc", (P, NJJ * TC), F32)
        sb_B = cx.sbuf("Bc", (16, TC), F32)
        sb_C = cx.sbuf("Cc", (16, TC), F32)
        sb_Br = cx.sbuf("Brep", (P, TC), F32)
        sb_Cr = cx.sbuf("Crep", (P, TC), F32)
        sb_a = cx.sbuf("abuf", (P, TC), F32)
        sb_b = cx.sbuf("bbuf", (P, TC), F32)
        sb_h = cx.sbuf("hbuf2", (P, TC), F32)
        sb_tm = cx.sbuf("tmc", (P, TC), F32)
        sb_y = cx.sbuf("ybuf", (P, NJJ * TC), F32)
        sb_sr = cx.sbuf("srbuf", (P, TC), F32)
        sb_ac = cx.sbuf("acol", (P, NJJ * 16), F32)
        sb_st = cx.sbuf("state", (P, NJJ * 16), F32)
        sb_ds = cx.sbuf("dscol", (P, NJJ), F32)
        sb_es = cx.sbuf("esel16", (16, 16 * P), F32)

        prog.g(lambda e: e.iota(sb_es[:], pattern=[[1, 16], [0, P]], base=0,
                                channel_multiplier=-1,
                                allow_small_or_imprecise_dtypes=True),
               writes=("esel16",))
        prog.v(lambda e: e.tensor_scalar(
            out=sb_es[:], in0=sb_es[:], scalar1=0.0, scalar2=None,
            op0=ALU.is_equal), reads=("esel16",), writes=("esel16",))

        prog.load(lambda e: e.dma_start(out=sb_ac[:], in_=dd_al[:]),
                  writes=("acol",), tag="ldal")
        prog.s(lambda e: e.activation(out=sb_ac[:], in_=sb_ac[:], func=AF.Exp),
               reads=("acol",), writes=("acol",))
        prog.v(lambda e: e.tensor_scalar(
            out=sb_ac[:], in0=sb_ac[:], scalar1=-1.0, scalar2=None,
            op0=ALU.mult), reads=("acol",), writes=("acol",))
        prog.load(lambda e: e.dma_start(out=sb_ds[:], in_=dd_ds[:]),
                  writes=("dscol",))
        prog.v(lambda e: e.memset(sb_st[:], 0.0), writes=("state",))

        for tc in range(SEQ // TC):
            t0 = tc * TC
            for name, sb, d, kk in [("dlc", sb_dl, ag_dl, kagdl),
                                    ("ucc", sb_uc, ag_uc, kaguc),
                                    ("resc", sb_res, ag_res, kagres)]:
                prog.load(lambda e, sb=sb, d=d, t0=t0: e.dma_start(
                    out=sb.ap().rearrange("p (a n) -> p a n", a=NJJ),
                    in_=d[:, t0:t0 + TC].rearrange("(a p) n -> p a n", p=P)),
                    reads=(kk,), writes=(name,), tag=f"ld{name}{tc}")
            prog.load(lambda e, tc=tc: e.dma_start(
                out=sb_B[:], in_=ag_bc[tc * 32: tc * 32 + 16, :]),
                reads=(kagbc,), writes=("Bc",))
            prog.load(lambda e, tc=tc: e.dma_start(
                out=sb_C[:], in_=ag_bc[tc * 32 + 16: tc * 32 + 32, :]),
                reads=(kagbc,), writes=("Cc",))
            for jj in range(NJJ):
                prog.v(lambda e, jj=jj: e.tensor_mul(
                    sb_du[:, jj * TC:(jj + 1) * TC],
                    sb_dl[:, jj * TC:(jj + 1) * TC],
                    sb_uc[:, jj * TC:(jj + 1) * TC]),
                    reads=("dlc", "ucc"), writes=("duc",))
                prog.v(lambda e, jj=jj: e.memset(
                    sb_y[:, jj * TC:(jj + 1) * TC], 0.0), writes=("ybuf",))
            for n in range(16):
                for c0 in range(0, TC, 512):
                    pb, pk = cx.bank()
                    prog.t(lambda e, n=n, c0=c0, pb=pb: e.matmul(
                        pb[:], sb_es[:, n * P:(n + 1) * P],
                        sb_B[:, c0:c0 + 512], start=True, stop=True),
                        reads=("esel16", "Bc"), writes=(pk,))
                    prog.v(lambda e, c0=c0, pb=pb: e.tensor_copy(
                        out=sb_Br[:, c0:c0 + 512], in_=pb[:]),
                        reads=(pk,), writes=("Brep",))
                    pb2, pk2 = cx.bank()
                    prog.t(lambda e, n=n, c0=c0, pb2=pb2: e.matmul(
                        pb2[:], sb_es[:, n * P:(n + 1) * P],
                        sb_C[:, c0:c0 + 512], start=True, stop=True),
                        reads=("esel16", "Cc"), writes=(pk2,))
                    prog.v(lambda e, c0=c0, pb2=pb2: e.tensor_copy(
                        out=sb_Cr[:, c0:c0 + 512], in_=pb2[:]),
                        reads=(pk2,), writes=("Crep",))
                for jj in range(NJJ):
                    acol = sb_ac[:, jj * 16 + n: jj * 16 + n + 1]
                    stcol = sb_st[:, jj * 16 + n: jj * 16 + n + 1]
                    prog.s(lambda e, jj=jj, acol=acol: e.activation(
                        out=sb_a[:], in_=sb_dl[:, jj * TC:(jj + 1) * TC],
                        func=AF.Exp, scale=acol),
                        reads=("dlc", "acol"), writes=("abuf",))
                    prog.v(lambda e, jj=jj: e.tensor_mul(
                        sb_b[:], sb_du[:, jj * TC:(jj + 1) * TC], sb_Br[:]),
                        reads=("duc", "Brep"), writes=("bbuf",))
                    prog.v(lambda e, stcol=stcol: e.tensor_tensor_scan(
                        out=sb_h[:], data0=sb_a[:], data1=sb_b[:],
                        initial=stcol, op0=ALU.mult, op1=ALU.add),
                        reads=("abuf", "bbuf", "state"), writes=("hbuf2",))
                    prog.v(lambda e, stcol=stcol: e.tensor_copy(
                        out=stcol, in_=sb_h[:, TC - 1:TC]),
                        reads=("hbuf2",), writes=("state",))
                    prog.g(lambda e: e.tensor_mul(sb_tm[:], sb_h[:], sb_Cr[:]),
                           reads=("hbuf2", "Crep"), writes=("tmc",))
                    prog.v(lambda e, jj=jj: e.tensor_add(
                        sb_y[:, jj * TC:(jj + 1) * TC],
                        sb_y[:, jj * TC:(jj + 1) * TC], sb_tm[:]),
                        reads=("ybuf", "tmc"), writes=("ybuf",))
            for jj in range(NJJ):
                dcol = sb_ds[:, jj:jj + 1]
                prog.v(lambda e, jj=jj, dcol=dcol: e.scalar_tensor_tensor(
                    out=sb_y[:, jj * TC:(jj + 1) * TC],
                    in0=sb_uc[:, jj * TC:(jj + 1) * TC], scalar=dcol,
                    in1=sb_y[:, jj * TC:(jj + 1) * TC],
                    op0=ALU.mult, op1=ALU.add),
                    reads=("ucc", "dscol", "ybuf"), writes=("ybuf",))
                prog.s(lambda e, jj=jj: e.activation(
                    out=sb_sr[:], in_=sb_res[:, jj * TC:(jj + 1) * TC],
                    func=AF.Silu),
                    reads=("resc",), writes=("srbuf",))
                prog.v(lambda e, jj=jj: e.tensor_mul(
                    sb_y[:, jj * TC:(jj + 1) * TC],
                    sb_y[:, jj * TC:(jj + 1) * TC], sb_sr[:]),
                    reads=("ybuf", "srbuf"), writes=("ybuf",))
            prog.store(lambda e, tc=tc: e.dma_start(
                out=dd_y[tc * DSL:(tc + 1) * DSL, :].rearrange(
                    "(a p) n -> p a n", p=P),
                in_=sb_y.ap().rearrange("p (a n) -> p a n", a=NJJ)),
                reads=("ybuf",), writes=(ky,), tag=f"sty{tc}", sem_key="ybuf@st")

        


def emit_scan_ts(cx, dd_dl, kdl, dd_uc, kuc, dd_res, kres, dd_bc, kbc,
                 dd_al, dd_ds, dd_sel, s_hp, kshp, ag_hp, kaghp,
                 dd_y, ky, AGfn):
    nc, prog = cx.nc, cx.prog
    NT = DI // P
    TC = 512
    NTC = TPC // TC
    NS = 16

    sb_es = cx.sbuf("esel16t", (16, 16 * P), F32)
    prog.g(lambda e: e.iota(sb_es[:], pattern=[[1, 16], [0, P]], base=0,
                            channel_multiplier=-1,
                            allow_small_or_imprecise_dtypes=True),
           writes=("esel16t",))
    prog.v(lambda e: e.tensor_scalar(
        out=sb_es[:], in0=sb_es[:], scalar1=0.0, scalar2=None,
        op0=ALU.is_equal), reads=("esel16t",), writes=("esel16t",))

    sb_ac = cx.sbuf("acolt", (P, NT * NS), F32)
    prog.load(lambda e: e.dma_start(out=sb_ac[:], in_=dd_al[:]),
              writes=("acolt",))
    prog.s(lambda e: e.activation(out=sb_ac[:], in_=sb_ac[:], func=AF.Exp),
           reads=("acolt",), writes=("acolt",))
    prog.v(lambda e: e.tensor_scalar(
        out=sb_ac[:], in0=sb_ac[:], scalar1=-1.0, scalar2=None,
        op0=ALU.mult), reads=("acolt",), writes=("acolt",))
    sb_ds = cx.sbuf("dscolt", (P, NT), F32)
    prog.load(lambda e: e.dma_start(out=sb_ds[:], in_=dd_ds[:]),
              writes=("dscolt",))
    sb_sel = cx.sbuf("selc", (P, NCORES), F32)
    prog.load(lambda e: e.dma_start(out=sb_sel[:], in_=dd_sel[:]),
              writes=("selc",))

    STK = [[f"sst{t}_{n}" for n in range(NS)] for t in range(NT)]
    ALLST = tuple(k for row in STK for k in row)
    sb_st = cx.sbuf("statet", (P, NT * NS), F32)
    prog.v(lambda e: e.memset(sb_st[:], 0.0), writes=ALLST)
    sb_sd = cx.sbuf("sdelta", (P, NT), F32)
    sb_sdt = cx.sbuf("sdtmp", (P, NT), F32)
    sb_pp = cx.sbuf("pdec", (P, NT * NS), F32)
    sb_h0 = cx.sbuf("h0buf", (P, NT * NS), F32)
    sb_z = cx.sbuf("zch", (P, NT * NS), F32)
    sb_hpj = [cx.sbuf(f"hpj{q}", (P, 2 * NT * NS), F32) for q in range(2)]

    sb_dla = cx.sbuf("dla", (P, NT * TC), F32)
    sb_dua = cx.sbuf("dua", (P, NT * TC), F32)
    sb_ya = cx.sbuf("yat", (P, NT * TC), F32)
    sb_uct = [cx.sbuf(f"uct{q}", (P, TC), F32) for q in range(2)]
    sb_rest = [cx.sbuf(f"rest{q}", (P, TC), F32) for q in range(2)]
    sb_B = cx.sbuf("Bct", (16, TC), F32)
    sb_C = cx.sbuf("Cct", (16, TC), F32)
    sb_a = [cx.sbuf(f"abuft{q}", (P, TC), F32) for q in range(3)]
    sb_b = [cx.sbuf(f"bbuft{q}", (P, TC), F32) for q in range(3)]
    sb_h = [cx.sbuf(f"hbuft{q}", (P, TC), F32) for q in range(3)]
    sb_tm = [cx.sbuf(f"tmt{q}", (P, TC), F32) for q in range(3)]
    sb_sr = [cx.sbuf(f"srt{q}", (P, TC), F32) for q in range(2)]
    sb_Br = [cx.sbuf(f"Brs{q}", (P, TC), F32) for q in range(2)]
    sb_Cr = [cx.sbuf(f"Crs{q}", (P, TC), F32) for q in range(2)]
    YK = [f"ya{t}" for t in range(NT)]

    def one_pass(second):
        tag = "p2" if second else "p1"
        pend = []
        for tc in range(NTC):
            t0 = tc * TC
            prog.load(lambda e, t0=t0: e.dma_start(
                out=sb_dla.ap().rearrange("p (a n) -> p a n", a=NT),
                in_=dd_dl[:, t0:t0 + TC].rearrange("(a p) n -> p a n", p=P)),
                reads=(kdl,), writes=("dla",), tag=f"lddl{tag}{tc}")
            prog.load(lambda e, t0=t0: e.dma_start(
                out=sb_B[:], in_=dd_bc[0:16, t0:t0 + TC]),
                reads=(kbc,), writes=("Bct",))
            if second:
                prog.load(lambda e, t0=t0: e.dma_start(
                    out=sb_C[:], in_=dd_bc[16:32, t0:t0 + TC]),
                    reads=(kbc,), writes=("Cct",))
            for tile in range(NT):
                qt = tile % 2
                sl = slice(tile * TC, (tile + 1) * TC)
                prog.load(lambda e, tile=tile, t0=t0, qt=qt: e.dma_start(
                    out=sb_uct[qt][:],
                    in_=dd_uc[tile * P:(tile + 1) * P, t0:t0 + TC]),
                    reads=(kuc,), writes=(f"uct{qt}",),
                    tag=f"lduc{tag}{tc}_{tile}")
                prog.v(lambda e, sl=sl, qt=qt: e.tensor_mul(
                    sb_dua[:, sl], sb_dla[:, sl], sb_uct[qt][:]),
                    reads=("dla", f"uct{qt}"), writes=("dua",))
                if not second:
                    prog.v(lambda e, sl=sl, tile=tile: e.tensor_reduce(
                        out=sb_sdt[:, tile:tile + 1], in_=sb_dla[:, sl],
                        axis=AX.X, op=ALU.add),
                        reads=("dla",), writes=("sdtmp",))
                    if tc == 0:
                        prog.v(lambda e, tile=tile: e.tensor_copy(
                            out=sb_sd[:, tile:tile + 1],
                            in_=sb_sdt[:, tile:tile + 1]),
                            reads=("sdtmp",), writes=("sdelta",))
                    else:
                        prog.v(lambda e, tile=tile: e.tensor_add(
                            sb_sd[:, tile:tile + 1], sb_sd[:, tile:tile + 1],
                            sb_sdt[:, tile:tile + 1]),
                            reads=("sdtmp", "sdelta"), writes=("sdelta",))
            for n in range(NS):
                qn = n % 2
                pbB, pkB = cx.bank()
                prog.t(lambda e, n=n, pbB=pbB: e.matmul(
                    pbB[:], sb_es[:, n * P:(n + 1) * P], sb_B[:],
                    start=True, stop=True),
                    reads=("esel16t", "Bct"), writes=(pkB,))
                prog.s(lambda e, qn=qn, pbB=pbB: e.activation(
                    out=sb_Br[qn][:], in_=pbB[:], func=AF.Copy),
                    reads=(pkB,), writes=(f"Brs{qn}",))
                if second:
                    pbC, pkC = cx.bank()
                    prog.t(lambda e, n=n, pbC=pbC: e.matmul(
                        pbC[:], sb_es[:, n * P:(n + 1) * P], sb_C[:],
                        start=True, stop=True),
                        reads=("esel16t", "Cct"), writes=(pkC,))
                    prog.s(lambda e, qn=qn, pbC=pbC: e.activation(
                        out=sb_Cr[qn][:], in_=pbC[:], func=AF.Copy),
                        reads=(pkC,), writes=(f"Crs{qn}",))
                for tile in range(NT):
                    i = n * NT + tile
                    q = i % 3
                    sl = slice(tile * TC, (tile + 1) * TC)
                    col = tile * NS + n
                    acol = sb_ac[:, col:col + 1]
                    stcol = sb_st[:, col:col + 1]
                    stk = STK[tile][n]
                    prog.s(lambda e, sl=sl, q=q, acol=acol: e.activation(
                        out=sb_a[q][:], in_=sb_dla[:, sl], func=AF.Exp,
                        scale=acol),
                        reads=("dla", "acolt"), writes=(f"abuft{q}",))
                    prog.v(lambda e, sl=sl, q=q, qn=qn: e.tensor_mul(
                        sb_b[q][:], sb_dua[:, sl], sb_Br[qn][:]),
                        reads=("dua", f"Brs{qn}"), writes=(f"bbuft{q}",))
                    if second and tc == 0:
                        init = sb_h0[:, col:col + 1]
                        ikey = "h0key"
                    else:
                        init = stcol
                        ikey = stk
                    prog.v(lambda e, q=q, init=init: e.tensor_tensor_scan(
                        out=sb_h[q][:], data0=sb_a[q][:], data1=sb_b[q][:],
                        initial=init, op0=ALU.mult, op1=ALU.add),
                        reads=(f"abuft{q}", f"bbuft{q}", ikey),
                        writes=(f"hbuft{q}",))
                    cons = []
                    if (not second) or tc < NTC - 1:
                        cons.append(lambda q=q, stcol=stcol, stk=stk: prog.s(
                            lambda e, q=q, stcol=stcol: e.activation(
                                out=stcol, in_=sb_h[q][:, TC - 1:TC],
                                func=AF.Copy),
                            reads=(f"hbuft{q}",), writes=(stk,)))
                    if second:
                        eng = prog.v
                        if n == 0:
                            cons.append(
                                lambda q=q, sl=sl, qn=qn, eng=eng, tile=tile:
                                eng(lambda e, q=q, sl=sl, qn=qn: e.tensor_mul(
                                    sb_ya[:, sl], sb_h[q][:], sb_Cr[qn][:]),
                                    reads=(f"hbuft{q}", f"Crs{qn}"),
                                    writes=(YK[tile],)))
                        else:
                            cons.append(
                                lambda q=q, qn=qn, eng=eng:
                                eng(lambda e, q=q, qn=qn: e.tensor_mul(
                                    sb_tm[q][:], sb_h[q][:], sb_Cr[qn][:]),
                                    reads=(f"hbuft{q}", f"Crs{qn}"),
                                    writes=(f"tmt{q}",)))
                            cons.append(
                                lambda q=q, sl=sl, tile=tile:
                                prog.v(lambda e, q=q, sl=sl: e.tensor_add(
                                    sb_ya[:, sl], sb_ya[:, sl], sb_tm[q][:]),
                                    reads=(YK[tile], f"tmt{q}"),
                                    writes=(YK[tile],)))
                    for fn in pend:
                        fn()
                    pend = cons
            for fn in pend:
                fn()
            pend = []
            if second:
                for tile in range(NT):
                    qt = tile % 2
                    sl = slice(tile * TC, (tile + 1) * TC)
                    dcol = sb_ds[:, tile:tile + 1]
                    prog.load(lambda e, tile=tile, t0=t0, qt=qt: e.dma_start(
                        out=sb_uct[qt][:],
                        in_=dd_uc[tile * P:(tile + 1) * P, t0:t0 + TC]),
                        reads=(kuc,), writes=(f"uct{qt}",),
                        tag=f"lduf{tc}_{tile}")
                    prog.load(lambda e, tile=tile, t0=t0, qt=qt: e.dma_start(
                        out=sb_rest[qt][:],
                        in_=dd_res[tile * P:(tile + 1) * P, t0:t0 + TC]),
                        reads=(kres,), writes=(f"rest{qt}",),
                        tag=f"ldrf{tc}_{tile}")
                    prog.v(lambda e, sl=sl, qt=qt, dcol=dcol:
                           e.scalar_tensor_tensor(
                        out=sb_ya[:, sl], in0=sb_uct[qt][:], scalar=dcol,
                        in1=sb_ya[:, sl], op0=ALU.mult, op1=ALU.add),
                        reads=(f"uct{qt}", "dscolt", YK[tile]),
                        writes=(YK[tile],))
                    prog.s(lambda e, qt=qt: e.activation(
                        out=sb_sr[qt][:], in_=sb_rest[qt][:], func=AF.Silu),
                        reads=(f"rest{qt}",), writes=(f"srt{qt}",))
                    prog.v(lambda e, sl=sl, qt=qt: e.tensor_mul(
                        sb_ya[:, sl], sb_ya[:, sl], sb_sr[qt][:]),
                        reads=(YK[tile], f"srt{qt}"), writes=(YK[tile],))
                    prog.store(lambda e, tile=tile, t0=t0, sl=sl: e.dma_start(
                        out=dd_y[tile * P:(tile + 1) * P, t0:t0 + TC],
                        in_=sb_ya[:, sl]),
                        reads=(YK[tile],), writes=(ky,),
                        tag=f"sty{tc}_{tile}", sem_key="yts@st")

    one_pass(False)
    for tile in range(NT):
        prog.v(lambda e, tile=tile: e.tensor_scalar(
            out=sb_pp[:, tile * NS:(tile + 1) * NS],
            in0=sb_ac[:, tile * NS:(tile + 1) * NS],
            scalar1=sb_sd[:, tile:tile + 1], scalar2=None, op0=ALU.mult),
            reads=("acolt", "sdelta"), writes=("pdec",))
    prog.s(lambda e: e.activation(out=sb_pp[:], in_=sb_pp[:], func=AF.Exp),
           reads=("pdec",), writes=("pdec",))
    prog.store(lambda e: e.dma_start(out=s_hp[:, 0:NT * NS], in_=sb_st[:]),
               reads=ALLST, writes=(kshp,), tag="sthpH", sem_key="hp@st")
    prog.store(lambda e: e.dma_start(out=s_hp[:, NT * NS:], in_=sb_pp[:]),
               reads=("pdec",), writes=(kshp,), tag="sthpP", sem_key="hp@st")
    AGfn("hp", s_hp, kshp, ag_hp, kaghp)
    prog.v(lambda e: e.memset(sb_z[:], 0.0), writes=("zch",))
    prog.v(lambda e: e.memset(sb_h0[:], 0.0), writes=("h0key",))
    for j in range(NCORES - 1):
        qj = j % 2
        prog.load(lambda e, j=j, qj=qj: e.dma_start(
            out=sb_hpj[qj][:], in_=ag_hp[j * P:(j + 1) * P, :]),
            reads=(kaghp,), writes=(f"hpj{qj}",))
        prog.v(lambda e, qj=qj: e.tensor_mul(
            sb_z[:], sb_z[:], sb_hpj[qj][:, NT * NS:]),
            reads=("zch", f"hpj{qj}"), writes=("zch",))
        prog.v(lambda e, qj=qj: e.tensor_add(
            sb_z[:], sb_z[:], sb_hpj[qj][:, 0:NT * NS]),
            reads=("zch", f"hpj{qj}"), writes=("zch",))
        selcol = sb_sel[:, j + 1:j + 2]
        prog.v(lambda e, selcol=selcol: e.scalar_tensor_tensor(
            out=sb_h0[:], in0=sb_z[:], scalar=selcol, in1=sb_h0[:],
            op0=ALU.mult, op1=ALU.add),
            reads=("zch", "selc", "h0key"), writes=("h0key",))
    one_pass(True)


def build_scan_ts():
    nc = bass.Bass("TRN2", target_bir_lowering=False)
    E = "ExternalInput"
    t = nc.dram_tensor
    dd_dl = t("deltaS", [DI, TPC], F32, kind=E).ap()
    dd_uc = t("ucS", [DI, TPC], F32, kind=E).ap()
    dd_res = t("resS", [DI, TPC], F32, kind=E).ap()
    dd_bc = t("bcS", [32, TPC], F32, kind=E).ap()
    dd_al = t("alog", [P, 256], F32, kind=E).ap()
    dd_ds = t("dskip", [P, 16], F32, kind=E).ap()
    dd_sel = t("hsel", [P, 8], F32, kind=E).ap()
    s_hp = t("s_hp", [P, 512], F32, kind="Internal").ap()
    ag_hp = t("ag_hp", [NCORES * P, 512], F32, kind="Internal").ap()
    dd_y = t("yS", [DI, TPC], F32, kind="ExternalOutput").ap()
    RG = [list(range(NCORES))]
    with ExitStack() as stack:
        prog = Prog(nc)
        cx = Ctx(nc, prog, stack)
        cx.psum_banks(7)

        def AG(name, src_ap, ksrc, dst_ap, kdst):
            prog.cc(lambda e: e.collective_compute(
                "AllGather", ALU.bypass, replica_groups=RG,
                ins=[src_ap.opt()], outs=[dst_ap.opt()]),
                reads=(ksrc,), writes=(kdst,), tag=f"ag_{name}")

        with cx.section():
            emit_scan_ts(cx, dd_dl, "D:dl", dd_uc, "D:uc", dd_res, "D:res",
                         dd_bc, "D:bc", dd_al, dd_ds, dd_sel,
                         s_hp, "D:hp", ag_hp, "D:aghp", dd_y, "D:y", AG)
        prog.finalize(stack)
    return nc


def emit_mamba_mid(cx, dd_y, kyloc, dd_x1, kx1, dd_h, kh, dd_k, kk_, dd_pp,
                   kpp, dd_wo, dd_l0g, dd_l0b, dd_wq, dd_qb, dd_wk, dd_kb, dd_wg):
    nc, prog = cx.nc, cx.prog
    CH = 512
    if True:

        sb_yb = cx.sbuf("ybf", (P, DIT * CH), BF16)
        sb_m = cx.sbuf("marr", (P, DT * CH), F32)
        sb_x1 = cx.sbuf("x1c", (P, DT * CH), F32)
        sb_l1q = cx.sbuf("ln1q", (P, DT * CH), F32)
        sb_hs = cx.sbuf("hstg", (P, DT * CH), F32)
        sb_hb = cx.sbuf("hbf", (P, DT * CH), BF16)
        sb_qg = cx.sbuf("qg", (P, DT * CH), BF16)
        sb_wo = cx.sbuf("wo", (P, DIT * DM), BF16)
        sb_wq = cx.sbuf("wq", (P, DT * DM), BF16)
        sb_wk = cx.sbuf("wk", (P, DT * DM), BF16)
        sb_wg = cx.sbuf("wgb", (P, DT), BF16)
        sb_cols = cx.sbuf("cols", (P, 5 * DT), F32)
        sb_rep = cx.sbuf("repb", (P, CH), F32)
        sb_sq = cx.sbuf("sqb", (P, CH), F32)
        sb_row = cx.sbuf("rows2", (1, 1024), F32)
        sb_qw = cx.sbuf("qwrow", (1, 512), F32)
        sb_p = cx.sbuf("pcols", (P, 10), F32)
        sb_junk = cx.sbuf("junk", (P, CH), F32)

        prog.load(_wload(sb_wo, dd_wo, DM), writes=("wo",), queue="gpsimd")
        prog.load(_wload(sb_wq, dd_wq, DM), writes=("wq",), queue="gpsimd")
        prog.load(_wload(sb_wk, dd_wk, DM), writes=("wk",), queue="gpsimd")
        prog.load(lambda e: e.dma_start(out=sb_wg[:], in_=dd_wg[:]),
                  writes=("wgb",), queue="gpsimd")
        for i, d in enumerate([dd_l0g, dd_l0b, dd_qb, dd_kb]):
            prog.load(lambda e, i=i, d=d: e.dma_start(
                out=sb_cols[:, i * DT:(i + 1) * DT], in_=d[:]),
                writes=("cols",), sem_key="cols")
        prog.v(lambda e: e.memset(sb_p[:], 0.0), writes=("pcols",))

        for ci in range(TPC // CH):
            c0 = ci * CH
            prog.load(lambda e, c0=c0: e.dma_start(
                out=sb_yb.ap().rearrange("p (a n) -> p a n", a=DIT),
                in_=dd_y[:, c0:c0 + CH].rearrange("(a p) n -> p a n", p=P)),
                reads=(kyloc,), writes=("ybf",), queue="gpsimd", tag=f"ldy{ci}")
            prog.load(lambda e, c0=c0: e.dma_start(
                out=sb_x1.ap().rearrange("p (a n) -> p a n", a=DT),
                in_=dd_x1[:, c0:c0 + CH].rearrange("(a p) n -> p a n", p=P)),
                reads=(kx1,), writes=("x1c",), tag=f"ldx1{ci}")
            for md in range(DT):
                pb, pk = cx.bank()
                for j in range(DIT):
                    lhs = sb_wo[:, j * DM + md * P: j * DM + (md + 1) * P]
                    rhs = sb_yb[:, j * CH:(j + 1) * CH]
                    prog.t(lambda e, lhs=lhs, rhs=rhs, j=j, pb=pb: e.matmul(
                        pb[:], lhs, rhs, start=(j == 0), stop=(j == DIT - 1)),
                        reads=("wo", "ybf"), writes=(pk,))
                prog.v(lambda e, md=md, pb=pb: e.tensor_copy(
                    out=sb_m[:, md * CH:(md + 1) * CH], in_=pb[:]),
                    reads=(pk,), writes=("marr",))
            emit_ln_fm(cx, sb_x1, "x1c", CH, sb_l1q, "ln1q", eps=1e-5)
            for j in range(DT):
                prog.v(lambda e, j=j: e.tensor_add(
                    sb_m[:, j * CH:(j + 1) * CH], sb_m[:, j * CH:(j + 1) * CH],
                    sb_l1q[:, j * CH:(j + 1) * CH]),
                    reads=("marr", "ln1q"), writes=("marr",))
            emit_ln_fm(cx, sb_m, "marr", CH, sb_hs, "hstg", eps=1e-12,
                       gamma=sb_cols[:, 0:DT], beta=sb_cols[:, DT:2 * DT],
                       gb_buf="cols")
            prog.store(lambda e, c0=c0: e.dma_start(
                out=dd_h[:, c0:c0 + CH].rearrange("(a p) n -> p a n", p=P),
                in_=sb_hs.ap().rearrange("p (a n) -> p a n", a=DT)),
                reads=("hstg",), writes=(kh,), tag=f"sth{ci}", sem_key="hstg@st")
            for j in range(DT):
                prog.v(lambda e, j=j: e.tensor_copy(
                    out=sb_hb[:, j * CH:(j + 1) * CH],
                    in_=sb_hs[:, j * CH:(j + 1) * CH]),
                    reads=("hstg",), writes=("hbf",))
            for which, wkey, wsb, bof, dst, dkey in (
                    ("q", "wq", sb_wq, 2, sb_l1q, "ln1q"),
                    ("k", "wk", sb_wk, 3, sb_m, "marr")):
                for mo in range(DT):
                    pb, pk = cx.bank()
                    for j in range(DT):
                        lhs = wsb[:, j * DM + mo * P: j * DM + (mo + 1) * P]
                        rhs = sb_hb[:, j * CH:(j + 1) * CH]
                        prog.t(lambda e, lhs=lhs, rhs=rhs, j=j, pb=pb: e.matmul(
                            pb[:], lhs, rhs, start=(j == 0), stop=(j == DT - 1)),
                            reads=(wkey, "hbf"), writes=(pk,))
                    bcol = sb_cols[:, bof * DT + mo: bof * DT + mo + 1]
                    prog.v(lambda e, mo=mo, pb=pb, bcol=bcol, dst=dst,
                           dkey=dkey: e.tensor_scalar(
                        out=dst[:, mo * CH:(mo + 1) * CH], in0=pb[:],
                        scalar1=bcol, scalar2=None, op0=ALU.add),
                        reads=(pk, "cols"), writes=(dkey,))
                pbS, pkS = cx.bank()
                for j in range(DT):
                    prog.s(lambda e, j=j, dst=dst, dkey=dkey: e.activation(
                        out=sb_sq[:], in_=dst[:, j * CH:(j + 1) * CH],
                        func=AF.Square),
                        reads=(dkey,), writes=("sqb",))
                    prog.t(lambda e, j=j, pbS=pbS: e.matmul(
                        pbS[0:1, :], cx.sb["ones_col"][:], sb_sq[:],
                        start=(j == 0), stop=(j == DT - 1)),
                        reads=("sqb", "ones_col"), writes=(pkS,))
                prog.s(lambda e, pbS=pbS: e.activation(
                    out=sb_row[0:1, 0:CH], in_=pbS[0:1, :], func=AF.Sqrt),
                    reads=(pkS,), writes=("rows2",))
                prog.v(lambda e: e.tensor_scalar(
                    out=sb_row[0:1, 0:CH], in0=sb_row[0:1, 0:CH],
                    scalar1=1e-12, scalar2=None, op0=ALU.max),
                    reads=("rows2",), writes=("rows2",))
                prog.v(lambda e: e.reciprocal(
                    out=sb_row[0:1, CH:2 * CH], in_=sb_row[0:1, 0:CH]),
                    reads=("rows2",), writes=("rows2",))
                emit_broadcast_row(cx, sb_row[0:1, CH:2 * CH], ("rows2",),
                                   sb_rep, "repb", CH)
                if which == "q":
                    for j in range(DT):
                        prog.v(lambda e, j=j: e.tensor_mul(
                            sb_l1q[:, j * CH:(j + 1) * CH],
                            sb_l1q[:, j * CH:(j + 1) * CH], sb_rep[:]),
                            reads=("ln1q", "repb"), writes=("ln1q",))
                        prog.s(lambda e, j=j: e.activation(
                            out=sb_qg[:, j * CH:(j + 1) * CH],
                            in_=sb_l1q[:, j * CH:(j + 1) * CH], func=AF.Gelu),
                            reads=("ln1q",), writes=("qg",))
                else:
                    for j in range(DT):
                        prog.v(lambda e, j=j: e.tensor_mul(
                            sb_m[:, j * CH:(j + 1) * CH],
                            sb_m[:, j * CH:(j + 1) * CH], sb_rep[:]),
                            reads=("marr", "repb"), writes=("marr",))
                    prog.store(lambda e, c0=c0: e.dma_start(
                        out=dd_k[:, c0:c0 + CH].rearrange("(a p) n -> p a n", p=P),
                        in_=sb_m.ap().rearrange("p (a n) -> p a n", a=DT)),
                        reads=("marr",), writes=(kk_,), tag=f"stk{ci}", sem_key="marr@st")
            pbW, pkW = cx.bank()
            for j in range(DT):
                lhs = sb_wg[:, j:j + 1]
                rhs = sb_qg[:, j * CH:(j + 1) * CH]
                prog.t(lambda e, lhs=lhs, rhs=rhs, j=j, pbW=pbW: e.matmul(
                    pbW[0:1, :], lhs, rhs, start=(j == 0), stop=(j == DT - 1)),
                    reads=("wgb", "qg"), writes=(pkW,))
            c_s, ck_s = cx.cc(1.0 / 32.0, parts=1)
            prog.s(lambda e, pbW=pbW, c=c_s: e.activation(
                out=sb_qw[0:1, 0:CH], in_=pbW[0:1, :], func=AF.Copy,
                scale=c),
                reads=(pkW, ck_s), writes=("qwrow",))
            prog.s(lambda e: e.activation(
                out=sb_row[0:1, 0:CH], in_=sb_qw[0:1, 0:CH],
                func=AF.Square),
                reads=("qwrow",), writes=("rows2",))
            prog.v(lambda e: e.tensor_reduce(
                out=sb_p[0:1, 9:10], in_=sb_row[0:1, 0:CH], axis=AX.X,
                op=ALU.add),
                reads=("rows2",), writes=("pcols",))
            prog.v(lambda e: e.tensor_add(
                sb_p[0:1, 8:9], sb_p[0:1, 8:9], sb_p[0:1, 9:10]),
                reads=("pcols",), writes=("pcols",))
            emit_broadcast_row(cx, sb_qw[0:1, 0:CH], ("qwrow",),
                               sb_rep, "repb", CH)
            for j in range(DT):
                prog.v(lambda e, j=j: e.scalar_tensor_tensor(
                    out=sb_junk[:], in0=sb_qg[:, j * CH:(j + 1) * CH],
                    scalar=1.0, in1=sb_rep[:], op0=ALU.mult, op1=ALU.mult,
                    accum_out=sb_p[:, 9:10]),
                    reads=("qg", "repb"), writes=("junk", "pcols"))
                prog.v(lambda e, j=j: e.tensor_add(
                    sb_p[:, j:j + 1], sb_p[:, j:j + 1], sb_p[:, 9:10]),
                    reads=("pcols",), writes=("pcols",))
        prog.store(lambda e: e.dma_start(out=dd_pp[:], in_=sb_p[:, 0:9]),
                   reads=("pcols",), writes=(kpp,), tag="stpp", sem_key="pcols@st")

        


def emit_mamba_post1(cx, dd_h, kh, dd_k, kk_, dd_pp, kppag, dd_h2, kh2,
                     dd_wp, dd_pb, dd_wf, dd_fb, dd_lig, dd_lib, dd_l1g, dd_l1b):
    nc, prog = cx.nc, cx.prog
    CH = 1024
    if True:

        sb_h = cx.sbuf("hc", (P, DT * CH), F32)
        sb_kb = cx.sbuf("kcb", (P, DT * CH), BF16)
        sb_gk = cx.sbuf("gkb", (P, DT * CH), BF16)
        sb_hs1 = cx.sbuf("hs1", (P, DT * CH), BF16)
        sb_lni = cx.sbuf("lnib2", (P, DT * CH), F32)
        sb_pr = cx.sbuf("prodb", (P, DT * CH), BF16)
        sb_sc = cx.sbuf("scr", (P, DT * CH), F32)
        sb_wp = cx.sbuf("wp", (P, DT * DM), BF16)
        sb_wf = cx.sbuf("wf", (P, DT * DM), BF16)
        sb_cols = cx.sbuf("cols", (P, 6 * DT), F32)
        sb_pp = cx.sbuf("ppb", (P, NCORES * 9), F32)
        sb_g = cx.sbuf("gcol", (P, 2), F32)
        sb_r1 = cx.sbuf("r1", (1, 2), F32)

        prog.load(_wload(sb_wp, dd_wp, DM), writes=("wp",), queue="gpsimd")
        prog.load(_wload(sb_wf, dd_wf, DM), writes=("wf",), queue="gpsimd")
        for i, d in enumerate([dd_pb, dd_fb, dd_lig, dd_lib, dd_l1g, dd_l1b]):
            prog.load(lambda e, i=i, d=d: e.dma_start(
                out=sb_cols[:, i * DT:(i + 1) * DT], in_=d[:]),
                writes=("cols",), sem_key="cols")
        prog.load(lambda e: e.dma_start(
            out=sb_pp.ap().rearrange("p (a n) -> p a n", a=NCORES),
            in_=dd_pp.rearrange("(a p) n -> p a n", p=P)),
            reads=(kppag,), writes=("ppb",))
        sb_gc = cx.sbuf("Gc", (P, DT), F32)
        prog.v(lambda e: e.memset(sb_gc[:], 0.0), writes=("Gc",))
        for c in range(NCORES):
            prog.v(lambda e, c=c: e.tensor_add(
                sb_gc[:], sb_gc[:], sb_pp[:, c * 9: c * 9 + 8]),
                reads=("ppb", "Gc"), writes=("Gc",))
        prog.v(lambda e: e.memset(sb_r1[:, 0:1], 0.0), writes=("r1",))
        for c in range(NCORES):
            prog.v(lambda e, c=c: e.tensor_add(
                sb_r1[0:1, 0:1], sb_r1[0:1, 0:1], sb_pp[0:1, c * 9 + 8:c * 9 + 9]),
                reads=("ppb", "r1"), writes=("r1",))
        prog.s(lambda e: e.activation(out=sb_r1[0:1, 1:2], in_=sb_r1[0:1, 0:1],
                                      func=AF.Sqrt),
               reads=("r1",), writes=("r1",))
        prog.v(lambda e: e.tensor_scalar(
            out=sb_r1[0:1, 1:2], in0=sb_r1[0:1, 1:2], scalar1=1e-12,
            scalar2=None, op0=ALU.max), reads=("r1",), writes=("r1",))
        prog.v(lambda e: e.reciprocal(out=sb_r1[0:1, 0:1], in_=sb_r1[0:1, 1:2]),
               reads=("r1",), writes=("r1",))
        pbG, pkG = cx.bank()
        prog.t(lambda e, pbG=pbG: e.matmul(
            pbG[:, 0:1], cx.sb["ones_row"][:], sb_r1[0:1, 0:1],
            start=True, stop=True),
            reads=("r1", "ones_row"), writes=(pkG,))
        prog.v(lambda e, pbG=pbG: e.tensor_copy(out=sb_g[:, 0:1], in_=pbG[:, 0:1]),
               reads=(pkG,), writes=("gcol",))
        for j in range(DT):
            prog.v(lambda e, j=j: e.tensor_mul(
                sb_gc[:, j:j + 1], sb_gc[:, j:j + 1], sb_g[:, 0:1]),
                reads=("Gc", "gcol"), writes=("Gc",))

        for ci in range(TPC // CH):
            c0 = ci * CH
            prog.load(lambda e, c0=c0: e.dma_start(
                out=sb_h.ap().rearrange("p (a n) -> p a n", a=DT),
                in_=dd_h[:, c0:c0 + CH].rearrange("(a p) n -> p a n", p=P)),
                reads=(kh,), writes=("hc",), tag=f"ldh{ci}")
            prog.load(lambda e, c0=c0: e.dma_start(
                out=sb_kb.ap().rearrange("p (a n) -> p a n", a=DT),
                in_=dd_k[:, c0:c0 + CH].rearrange("(a p) n -> p a n", p=P)),
                reads=(kk_,), writes=("kcb",), queue="gpsimd", tag=f"ldk{ci}")
            for j in range(DT):
                prog.v(lambda e, j=j: e.tensor_scalar(
                    out=sb_gk[:, j * CH:(j + 1) * CH],
                    in0=sb_kb[:, j * CH:(j + 1) * CH],
                    scalar1=sb_gc[:, j:j + 1], scalar2=None, op0=ALU.mult),
                    reads=("kcb", "Gc"), writes=("gkb",))
            for mo in range(DT):
                for n0 in range(0, CH, 512):
                    pb, pk = cx.bank()
                    for j in range(DT):
                        lhs = sb_wp[:, j * DM + mo * P: j * DM + (mo + 1) * P]
                        rhs = sb_gk[:, j * CH + n0: j * CH + n0 + 512]
                        prog.t(lambda e, lhs=lhs, rhs=rhs, j=j, pb=pb: e.matmul(
                            pb[:], lhs, rhs, start=(j == 0), stop=(j == DT - 1)),
                            reads=("wp", "gkb"), writes=(pk,))
                    bcol = sb_cols[:, 0 * DT + mo: 0 * DT + mo + 1]
                    prog.v(lambda e, mo=mo, n0=n0, pb=pb, bcol=bcol:
                           e.tensor_scalar(
                        out=sb_hs1[:, mo * CH + n0: mo * CH + n0 + 512],
                        in0=pb[:], scalar1=bcol, scalar2=None, op0=ALU.add),
                        reads=(pk, "cols"), writes=("hs1",))
            emit_ln_fm(cx, sb_h, "hc", CH, sb_lni, "lnib2", eps=1e-12,
                       gamma=sb_cols[:, 2 * DT:3 * DT],
                       beta=sb_cols[:, 3 * DT:4 * DT], gb_buf="cols")
            for j in range(DT):
                prog.v(lambda e, j=j: e.tensor_mul(
                    sb_pr[:, j * CH:(j + 1) * CH],
                    sb_hs1[:, j * CH:(j + 1) * CH],
                    sb_lni[:, j * CH:(j + 1) * CH]),
                    reads=("hs1", "lnib2"), writes=("prodb",))
            for mo in range(DT):
                for n0 in range(0, CH, 512):
                    pb, pk = cx.bank()
                    for j in range(DT):
                        lhs = sb_wf[:, j * DM + mo * P: j * DM + (mo + 1) * P]
                        rhs = sb_pr[:, j * CH + n0: j * CH + n0 + 512]
                        prog.t(lambda e, lhs=lhs, rhs=rhs, j=j, pb=pb: e.matmul(
                            pb[:], lhs, rhs, start=(j == 0), stop=(j == DT - 1)),
                            reads=("wf", "prodb"), writes=(pk,))
                    bcol = sb_cols[:, 1 * DT + mo: 1 * DT + mo + 1]
                    prog.v(lambda e, mo=mo, n0=n0, pb=pb, bcol=bcol:
                           e.tensor_scalar(
                        out=sb_sc[:, mo * CH + n0: mo * CH + n0 + 512],
                        in0=pb[:], scalar1=bcol, scalar2=None, op0=ALU.add),
                        reads=(pk, "cols"), writes=("scr",))
            for j in range(DT):
                prog.v(lambda e, j=j: e.tensor_add(
                    sb_sc[:, j * CH:(j + 1) * CH],
                    sb_sc[:, j * CH:(j + 1) * CH],
                    sb_h[:, j * CH:(j + 1) * CH]),
                    reads=("scr", "hc"), writes=("scr",))
            emit_ln_fm(cx, sb_sc, "scr", CH, sb_lni, "lnib2", eps=1e-12,
                       gamma=sb_cols[:, 4 * DT:5 * DT],
                       beta=sb_cols[:, 5 * DT:6 * DT], gb_buf="cols")
            prog.store(lambda e, c0=c0: e.dma_start(
                out=dd_h2[:, c0:c0 + CH].rearrange("(a p) n -> p a n", p=P),
                in_=sb_lni.ap().rearrange("p (a n) -> p a n", a=DT)),
                reads=("lnib2",), writes=(kh2,), tag=f"sth2{ci}", sem_key="lnib2@st")

        


def emit_mamba_post2(cx, dd_h2, kh2, dd_x1, kx1, dd_out, kout,
                     dd_w1, dd_b1, dd_w2, dd_b2, dd_l2g, dd_l2b):
    nc, prog = cx.nc, cx.prog
    HC = 1024
    HT = HC // P
    if True:

        sb_h2b = cx.sbuf("h2b", (P, DT * TPC), BF16)
        sb_f = cx.sbuf("facc", (P, DT * TPC), F32)
        sb_w1 = cx.sbuf("w1c", (P, DT * HC), BF16)
        sb_w2 = cx.sbuf("w2c", (P, HT * DM), BF16)
        sb_hh = cx.sbuf("hh", (P, HT * TPC), BF16)
        sb_cols = cx.sbuf("cols", (P, 32 + 3 * DT), F32)
        FC = 256
        sb_pre = cx.sbuf("pre", (P, DT * FC), F32)
        sb_lo = cx.sbuf("lnout", (P, DT * FC), F32)
        sb_st1 = cx.sbuf("stream1", (P, FC), F32)
        sb_st2 = cx.sbuf("stream2", (P, FC), F32)

        prog.load(lambda e: e.dma_start(
            out=sb_h2b.ap().rearrange("p (a n) -> p a n", a=DT),
            in_=dd_h2.rearrange("(a p) n -> p a n", p=P)),
            reads=(kh2,), writes=("h2b",), queue="gpsimd", tag="ldh2")
        prog.load(lambda e: e.dma_start(out=sb_cols[:, 0:32], in_=dd_b1[:]),
                  writes=("cols",), sem_key="cols")
        for i, d in enumerate([dd_b2, dd_l2g, dd_l2b]):
            prog.load(lambda e, i=i, d=d: e.dma_start(
                out=sb_cols[:, 32 + i * DT:32 + (i + 1) * DT], in_=d[:]),
                writes=("cols",), sem_key="cols")

        for hc in range(4096 // HC):
            prog.load(lambda e, hc=hc: e.dma_start(
                out=sb_w1.ap().rearrange("p (a n) -> p a n", a=DT),
                in_=dd_w1[:, hc * HC:(hc + 1) * HC].rearrange(
                    "(a p) n -> p a n", p=P)),
                writes=("w1c",), queue="gpsimd", tag=f"ldw1{hc}")
            prog.load(lambda e, hc=hc: e.dma_start(
                out=sb_w2.ap().rearrange("p (a n) -> p a n", a=HT),
                in_=dd_w2[hc * HC:(hc + 1) * HC, :].rearrange(
                    "(a p) n -> p a n", p=P)),
                writes=("w2c",), queue="gpsimd", tag=f"ldw2{hc}")
            for mh in range(HT):
                for n0 in range(0, TPC, 512):
                    pb, pk = cx.bank()
                    for j in range(DT):
                        lhs = sb_w1[:, j * HC + mh * P: j * HC + (mh + 1) * P]
                        rhs = sb_h2b[:, j * TPC + n0: j * TPC + n0 + 512]
                        prog.t(lambda e, lhs=lhs, rhs=rhs, j=j, pb=pb: e.matmul(
                            pb[:], lhs, rhs, start=(j == 0), stop=(j == DT - 1)),
                            reads=("w1c", "h2b"), writes=(pk,))
                    bcol = sb_cols[:, hc * HT + mh: hc * HT + mh + 1]
                    prog.s(lambda e, mh=mh, n0=n0, pb=pb, bcol=bcol:
                           e.activation(
                        out=sb_hh[:, mh * TPC + n0: mh * TPC + n0 + 512],
                        in_=pb[:], func=AF.Gelu, bias=bcol),
                        reads=(pk, "cols"), writes=("hh",))
            for md in range(DT):
                for n0 in range(0, TPC, 512):
                    pb, pk = cx.bank()
                    for mh in range(HT):
                        lhs = sb_w2[:, mh * DM + md * P: mh * DM + (md + 1) * P]
                        rhs = sb_hh[:, mh * TPC + n0: mh * TPC + n0 + 512]
                        prog.t(lambda e, lhs=lhs, rhs=rhs, mh=mh, pb=pb:
                               e.matmul(pb[:], lhs, rhs, start=(mh == 0),
                                        stop=(mh == HT - 1)),
                               reads=("w2c", "hh"), writes=(pk,))
                    sl = slice(md * TPC + n0, md * TPC + n0 + 512)
                    if hc == 0:
                        prog.v(lambda e, sl=sl, pb=pb: e.tensor_copy(
                            out=sb_f[:, sl], in_=pb[:]),
                            reads=(pk,), writes=("facc",))
                    else:
                        prog.v(lambda e, sl=sl, pb=pb: e.tensor_add(
                            sb_f[:, sl], sb_f[:, sl], pb[:]),
                            reads=(pk, "facc"), writes=("facc",))

        for ci in range(TPC // FC):
            c0 = ci * FC
            for j in range(DT):
                prog.load(lambda e, j=j, c0=c0: e.dma_start(
                    out=sb_st1[:], in_=dd_h2[j * P:(j + 1) * P, c0:c0 + FC]),
                    reads=(kh2,), writes=("stream1",), tag="ldh2s")
                bcol = sb_cols[:, 32 + j: 32 + j + 1]
                prog.v(lambda e, j=j, c0=c0, bcol=bcol: e.tensor_scalar(
                    out=sb_st1[:], in0=sb_st1[:], scalar1=bcol, scalar2=None,
                    op0=ALU.add),
                    reads=("stream1", "cols"), writes=("stream1",))
                prog.v(lambda e, j=j, c0=c0: e.tensor_add(
                    sb_pre[:, j * FC:(j + 1) * FC],
                    sb_f[:, j * TPC + c0: j * TPC + c0 + FC], sb_st1[:]),
                    reads=("facc", "stream1"), writes=("pre",))
            emit_ln_fm(cx, sb_pre, "pre", FC, sb_lo, "lnout", eps=1e-12,
                       gamma=sb_cols[:, 32 + DT:32 + 2 * DT],
                       beta=sb_cols[:, 32 + 2 * DT:32 + 3 * DT], gb_buf="cols")
            for j in range(DT):
                prog.load(lambda e, j=j, c0=c0: e.dma_start(
                    out=sb_st2[:], in_=dd_x1[j * P:(j + 1) * P, c0:c0 + FC]),
                    reads=(kx1,), writes=("stream2",), tag="ldx1s")
                prog.v(lambda e, j=j: e.tensor_add(
                    sb_lo[:, j * FC:(j + 1) * FC],
                    sb_lo[:, j * FC:(j + 1) * FC], sb_st2[:]),
                    reads=("lnout", "stream2"), writes=("lnout",))
            prog.store(lambda e, c0=c0: e.dma_start(
                out=dd_out[:, c0:c0 + FC].rearrange("(a p) n -> p a n", p=P),
                in_=sb_lo.ap().rearrange("p (a n) -> p a n", a=DT)),
                reads=("lnout",), writes=(kout,), tag=f"stx2{ci}", sem_key="lnout@st")

        


def emit_qkv(cx, dd_x, kx, dd_kq, kkq, dd_qf, kqf, dd_v, kv, dd_wq, dd_wk, dd_wv):
    nc, prog = cx.nc, cx.prog
    if True:

        sb_xa = cx.sbuf("xa", (P, DT * TPC), F32)
        sb_ln = cx.sbuf("lnx", (P, DT * TPC), BF16)
        sb_w = cx.sbuf("wbuf", (P, DT * DM), BF16)
        sb_o = cx.sbuf("obuf", (P, DT * TPC), BF16)

        prog.load(_wload(sb_xa, dd_x, TPC), reads=(kx,), writes=("xa",), tag="ldx")
        emit_ln_fm(cx, sb_xa, "xa", TPC, sb_ln, "lnx", eps=1e-5)

        for wi, d_w in enumerate([dd_wk, dd_wq]):
            prog.load(_wload(sb_w, d_w, DM), writes=("wbuf",),
                      queue="gpsimd", tag=f"ldw{wi}")
            for mo in range(DT):
                for n0 in range(0, TPC, 512):
                    pb, pk = cx.bank()
                    for j in range(DT):
                        lhs = sb_w[:, j * DM + mo * P: j * DM + (mo + 1) * P]
                        rhs = sb_ln[:, j * TPC + n0: j * TPC + n0 + 512]
                        prog.t(lambda e, lhs=lhs, rhs=rhs, j=j, pb=pb: e.matmul(
                            pb[:], lhs, rhs, start=(j == 0), stop=(j == DT - 1)),
                            reads=("wbuf", "lnx"), writes=(pk,))
                    prog.v(lambda e, mo=mo, n0=n0, pb=pb: e.tensor_copy(
                        out=sb_o[:, mo * TPC + n0: mo * TPC + n0 + 512], in_=pb[:]),
                        reads=(pk,), writes=("obuf",))
            if wi == 0:
                prog.store(lambda e: e.dma_start(
                    out=dd_kq.rearrange("(a p) t -> p a t", p=P),
                    in_=sb_o.ap().rearrange("p (a t) -> p a t", a=DT)),
                    reads=("obuf",), writes=(kkq,), tag="stk",
                    sem_key="stk")
            else:
                for l in range(16):
                    qrow = (l % 8) * 2 * DM + (l // 8) * DM
                    prog.store(lambda e, l=l, qrow=qrow: e.dma_start(
                        out=dd_qf[qrow: qrow + DM, :].rearrange(
                            "(a p) n -> p a n", p=P),
                        in_=sb_o.ap().rearrange("p (a t) -> p a t", a=DT)
                        [:, :, l * P:(l + 1) * P]),
                        reads=("obuf",), writes=(kqf,), tag=f"stq{l}",
                        sem_key="stq")

        prog.load(_wload(sb_w, dd_wv, DM), writes=("wbuf",),
                  queue="gpsimd", tag="ldwv")
        for tt in range(16):
            for n0 in range(0, DM, 512):
                pb, pk = cx.bank()
                for j in range(DT):
                    lhs = sb_ln[:, j * TPC + tt * P: j * TPC + (tt + 1) * P]
                    rhs = sb_w[:, j * DM + n0: j * DM + n0 + 512]
                    prog.t(lambda e, lhs=lhs, rhs=rhs, j=j, pb=pb: e.matmul(
                        pb[:], lhs, rhs, start=(j == 0), stop=(j == DT - 1)),
                        reads=("wbuf", "lnx"), writes=(pk,))
                prog.v(lambda e, tt=tt, n0=n0, pb=pb: e.tensor_copy(
                    out=sb_o[:, tt * DM + n0: tt * DM + n0 + 512], in_=pb[:]),
                    reads=(pk,), writes=("obuf",))
        prog.store(lambda e: e.dma_start(
            out=dd_v.rearrange("(a p) t -> p a t", p=P),
            in_=sb_o.ap().rearrange("p (a t) -> p a t", a=16)),
            reads=("obuf",), writes=(kv,), tag="stv", sem_key="stv")

        



def emit_attn_fused(cx, ag_kq, kagkq, ag_v, kagv, qloc, kqloc, dd_rtab, dd_o, ko):
    nc, prog = cx.nc, cx.prog
    SCALE = 1.0 / 32.0
    KBW = 1024

    pt_ps = cx.ps_bf16
    identb = cx.sbuf("identb", (P, P), BF16)
    prog.v(lambda e, identb=identb: e.tensor_copy(
        out=identb[:], in_=cx.sb["ident"][:]),
        reads=("ident",), writes=("identb",))

    sb_q = cx.sbuf("qat", (P, 16 * DT * P), BF16)
    sb_o = cx.sbuf("oat", (P, 16 * DM), F32)
    sb_ml = cx.sbuf("mlat", (P, 16 * 4), F32)
    sb_k = [cx.sbuf(f"kat{s}", (P, DT * KBW), BF16) for s in range(2)]
    sb_v = [cx.sbuf(f"vat{s}", (P, 8 * DM), BF16) for s in range(2)]
    sb_p = cx.sbuf("pat", (P, KBW), BF16)
    sb_pt = cx.sbuf("ptat", (P, KBW), BF16)
    sb_st = cx.sbuf("sstat", (P, KBW), F32)
    sb_msk = cx.sbuf("mskat", (P, KBW), F32)
    sb_rt = cx.sbuf("rtat", (P, 256), F32)
    sb_io = cx.sbuf("ioat", (P, KBW), F32)
    sb_col = cx.sbuf("colat", (P, 2), F32)
    sb_ot = cx.sbuf("otst", (P, P), BF16)

    prog.load(lambda e: e.dma_start(out=sb_rt[:], in_=dd_rtab[:]),
              writes=("rtat",))
    prog.g(lambda e: e.iota(sb_io[:], pattern=[[1, KBW]], base=0,
                            channel_multiplier=-1,
                            allow_small_or_imprecise_dtypes=True),
           writes=("ioat",))
    for i in range(16):
        prog.load(lambda e, i=i: e.dma_start(
            out=sb_q.ap().rearrange("p (b n) -> p b n", b=16 * DT)
            [:, i * DT:(i + 1) * DT, :],
            in_=qloc[i * DM:(i + 1) * DM, :].rearrange(
                "(a p) n -> p a n", p=P)),
            reads=(kqloc,), writes=("qat",), tag=f"ldq{i}")
    prog.v(lambda e: e.memset(sb_o[:], 0.0), writes=("oat",))
    prog.v(lambda e: e.memset(sb_ml[:], 0.0), writes=("mlat",))
    for i in range(16):
        prog.v(lambda e, i=i: e.memset(sb_ml[:, i * 4: i * 4 + 1], -1e30),
               writes=("mlat",))

    for kb in range(16):
        sl_k = kb % 2
        kkey, vkey = f"kat{sl_k}", f"vat{sl_k}"
        rk = kb // 2
        kcol = 1024 * (kb % 2)
        prog.load(lambda e, rk=rk, kcol=kcol, sl_k=sl_k: e.dma_start(
            out=sb_k[sl_k].ap().rearrange("p (a n) -> p a n", a=DT),
            in_=ag_kq[rk * DM:(rk + 1) * DM, kcol:kcol + KBW]
            .rearrange("(a p) n -> p a n", p=P)),
            reads=(kagkq,), writes=(kkey,), tag=f"ldk{kb}")
        prog.load(lambda e, kb=kb, sl_k=sl_k: e.dma_start(
            out=sb_v[sl_k].ap().rearrange("p (a n) -> p a n", a=8),
            in_=ag_v[kb * KBW:(kb + 1) * KBW, :]
            .rearrange("(a p) n -> p a n", p=P)),
            reads=(kagv,), writes=(vkey,), tag=f"ldv{kb}", queue="gpsimd")
        for i in range(kb, 16):
            pbs = []
            for nn in range(2):
                pb, pk = cx.bank()
                pbs.append((pb, pk))
                for j in range(DT):
                    lhs = sb_q[:, (i * DT + j) * P:(i * DT + j + 1) * P]
                    rhs = sb_k[sl_k][:, j * KBW + nn * 512: j * KBW + nn * 512 + 512]
                    prog.t(lambda e, lhs=lhs, rhs=rhs, j=j, pb=pb: e.matmul(
                        pb[:], lhs, rhs, start=(j == 0), stop=(j == DT - 1)),
                        reads=("qat", kkey), writes=(pk,))
            rcol = sb_rt[:, i * 16 + kb: i * 16 + kb + 1]
            c_m, ck_m = cx.cc(-1e30)
            prog.v(lambda e, rcol=rcol, c_m=c_m: e.tensor_scalar(
                out=sb_msk[:], in0=sb_io[:], scalar1=rcol, scalar2=c_m,
                op0=ALU.is_gt, op1=ALU.mult),
                reads=("ioat", "rtat", ck_m), writes=("mskat",))
            for nn, (pb, pk) in enumerate(pbs):
                prog.v(lambda e, nn=nn, pb=pb: e.scalar_tensor_tensor(
                    out=sb_st[:, nn * 512:(nn + 1) * 512], in0=pb[:],
                    scalar=SCALE, in1=sb_msk[:, nn * 512:(nn + 1) * 512],
                    op0=ALU.mult, op1=ALU.add),
                    reads=(pk, "mskat"), writes=("sstat",))
            mcol = sb_ml[:, i * 4: i * 4 + 1]
            lcol = sb_ml[:, i * 4 + 1: i * 4 + 2]
            acol = sb_ml[:, i * 4 + 2: i * 4 + 3]
            ncol = sb_ml[:, i * 4 + 3: i * 4 + 4]
            prog.v(lambda e: e.tensor_reduce(
                out=sb_col[:, 0:1], in_=sb_st[:], axis=AX.X, op=ALU.max),
                reads=("sstat",), writes=("colat",))
            prog.v(lambda e, mcol=mcol: e.tensor_max(
                sb_col[:, 1:2], sb_col[:, 0:1], mcol),
                reads=("colat", "mlat"), writes=("colat",))
            prog.v(lambda e, ncol=ncol: e.tensor_scalar(
                out=ncol, in0=sb_col[:, 1:2], scalar1=-1.0, scalar2=None,
                op0=ALU.mult), reads=("colat",), writes=("mlat",))
            prog.s(lambda e, mcol=mcol, acol=acol, ncol=ncol: e.activation(
                out=acol, in_=mcol, func=AF.Exp, bias=ncol),
                reads=("mlat",), writes=("mlat",))
            prog.v(lambda e, mcol=mcol: e.tensor_copy(
                out=mcol, in_=sb_col[:, 1:2]),
                reads=("colat",), writes=("mlat",))
            prog.s(lambda e, ncol=ncol: e.activation(
                out=sb_p[:], in_=sb_st[:], func=AF.Exp, bias=ncol,
                accum_out=sb_col[:, 0:1]),
                reads=("sstat", "mlat"), writes=("pat", "colat"))
            prog.v(lambda e, lcol=lcol, acol=acol: e.scalar_tensor_tensor(
                out=lcol, in0=lcol, scalar=1.0, in1=acol,
                op0=ALU.mult, op1=ALU.mult),
                reads=("mlat",), writes=("mlat",))
            prog.v(lambda e, lcol=lcol: e.tensor_add(
                lcol, lcol, sb_col[:, 0:1]),
                reads=("mlat", "colat"), writes=("mlat",))
            prog.v(lambda e, i=i, acol=acol: e.tensor_scalar(
                out=sb_o[:, i * DM:(i + 1) * DM],
                in0=sb_o[:, i * DM:(i + 1) * DM],
                scalar1=acol, scalar2=None, op0=ALU.mult),
                reads=("oat", "mlat"), writes=("oat",))
            for tt in range(8):
                prog.t(lambda e, tt=tt: e.transpose(
                    pt_ps[:, tt * P:(tt + 1) * P],
                    sb_p[:, tt * P:(tt + 1) * P], identb[:]),
                    reads=("pat", "identb"), writes=("psbf",))
            prog.v(lambda e: e.tensor_copy(out=sb_pt[:], in_=pt_ps[:]),
                   reads=("psbf",), writes=("ptat",))
            for n0 in range(0, DM, 512):
                pb2, pk2 = cx.bank()
                for tt in range(8):
                    lhs = sb_pt[:, tt * P:(tt + 1) * P]
                    rhs = sb_v[sl_k][:, tt * DM + n0: tt * DM + n0 + 512]
                    prog.t(lambda e, lhs=lhs, rhs=rhs, tt=tt, pb2=pb2:
                           e.matmul(pb2[:], lhs, rhs, start=(tt == 0),
                                    stop=(tt == 7)),
                           reads=("ptat", vkey), writes=(pk2,))
                prog.v(lambda e, i=i, n0=n0, pb2=pb2: e.tensor_add(
                    sb_o[:, i * DM + n0: i * DM + n0 + 512],
                    sb_o[:, i * DM + n0: i * DM + n0 + 512], pb2[:]),
                    reads=(pk2, "oat"), writes=("oat",))

    for i in range(16):
        lcol = sb_ml[:, i * 4 + 1: i * 4 + 2]
        prog.v(lambda e, lcol=lcol: e.reciprocal(
            out=sb_col[:, 0:1], in_=lcol), reads=("mlat",), writes=("colat",))
        prog.v(lambda e, i=i: e.tensor_scalar(
            out=sb_o[:, i * DM:(i + 1) * DM], in0=sb_o[:, i * DM:(i + 1) * DM],
            scalar1=sb_col[:, 0:1], scalar2=None, op0=ALU.mult),
            reads=("oat", "colat"), writes=("oat",))
        for j in range(DT):
            pb, pk = cx.bank()
            prog.t(lambda e, i=i, j=j, pb=pb: e.transpose(
                pb[:, 0:P], sb_o[:, i * DM + j * P: i * DM + (j + 1) * P],
                cx.sb["ident"][:]),
                reads=("oat", "ident"), writes=(pk,))
            prog.v(lambda e, pb=pb: e.tensor_copy(
                out=sb_ot[:], in_=pb[:, 0:P]),
                reads=(pk,), writes=("otst",))
            prog.store(lambda e, i=i, j=j: e.dma_start(
                out=dd_o[i * DM + j * P: i * DM + (j + 1) * P, :],
                in_=sb_ot[:]),
                reads=("otst",), writes=(ko,), tag=f"sto{i}_{j}",
                sem_key="otst@st")


def emit_attn_resid(cx, oloc, koloc, dd_x, kx, dd_out, kout):
    nc, prog = cx.nc, cx.prog
    sb_x = cx.sbuf("xre5", (P, DT * P), F32)
    sb_ob = cx.sbuf("ore5", (P, DT * P), BF16)
    sb_o5 = cx.sbuf("out5", (P, DT * P), F32)
    for m in range(16):
        mb, r = m // 8, m % 8
        prog.load(lambda e, mb=mb, r=r: e.dma_start(
            out=sb_ob.ap().rearrange("p (a n) -> p a n", a=DT),
            in_=oloc[mb * DM:(mb + 1) * DM, r * P:(r + 1) * P].rearrange(
                "(a p) n -> p a n", p=P)),
            reads=(koloc,), writes=("ore5",), tag=f"ldo{m}")
        prog.load(lambda e, m=m: e.dma_start(
            out=sb_x.ap().rearrange("p (a n) -> p a n", a=DT),
            in_=dd_x[:, m * P:(m + 1) * P].rearrange("(a p) n -> p a n", p=P)),
            reads=(kx,), writes=("xre5",), tag=f"ldx5{m}")
        for j in range(DT):
            prog.v(lambda e, j=j: e.tensor_add(
                sb_o5[:, j * P:(j + 1) * P], sb_x[:, j * P:(j + 1) * P],
                sb_ob[:, j * P:(j + 1) * P]),
                reads=("xre5", "ore5"), writes=("out5",))
        prog.store(lambda e, m=m: e.dma_start(
            out=dd_out[:, m * P:(m + 1) * P].rearrange("(a p) n -> p a n", p=P),
            in_=sb_o5.ap().rearrange("p (a n) -> p a n", a=DT)),
            reads=("out5",), writes=(kout,), tag=f"stx5{m}", sem_key="out5@st")



def build_fused():
    nc = bass.Bass("TRN2", target_bir_lowering=False)
    E = "ExternalInput"
    I = "Internal"
    t = nc.dram_tensor
    d = {}
    d["xT"] = t("xT", [DM, TPC], F32, kind=E).ap()
    d["gate_wT"] = t("gate_wT", [DM, 8], F32, kind=E).ap()
    d["ebias"] = t("ebias", [1, 8], F32, kind=E).ap()
    d["wupT"] = t("wupT", [9, DM, 1408], F32, kind=E).ap()
    d["wdnT"] = t("wdnT", [9, 1408, DM], F32, kind=E).ap()
    d["inprojT"] = t("inprojT", [DM, 2 * DI], F32, kind=E).ap()
    d["xprojT"] = t("xprojT", [DI, 96], F32, kind=E).ap()
    d["dtprojT"] = t("dtprojT", [64, DI], F32, kind=E).ap()
    d["convw"] = t("convw", [4, P, DIT], F32, kind=E).ap()
    d["convb"] = t("convb", [P, DIT], F32, kind=E).ap()
    d["dtb"] = t("dtb", [P, DIT], F32, kind=E).ap()
    d["halomask"] = t("halomask", [P, DT * 8], F32, kind=E).ap()
    d["alog"] = t("alog", [P, 16 * 16], F32, kind=E).ap()
    d["dskip"] = t("dskip", [P, 16], F32, kind=E).ap()
    d["hsel"] = t("hsel", [P, NCORES], F32, kind=E).ap()
    d["outprojT"] = t("outprojT", [DI, DM], F32, kind=E).ap()
    d["ln0g"] = t("ln0g", [P, DT], F32, kind=E).ap()
    d["ln0b"] = t("ln0b", [P, DT], F32, kind=E).ap()
    d["qwT"] = t("qwT", [DM, DM], F32, kind=E).ap()
    d["qb"] = t("qb", [P, DT], F32, kind=E).ap()
    d["kwT"] = t("kwT", [DM, DM], F32, kind=E).ap()
    d["kb"] = t("kb", [P, DT], F32, kind=E).ap()
    d["wgc"] = t("wgc", [P, DT], F32, kind=E).ap()
    d["projT"] = t("projT", [DM, DM], F32, kind=E).ap()
    d["projb"] = t("projb", [P, DT], F32, kind=E).ap()
    d["flT"] = t("flT", [DM, DM], F32, kind=E).ap()
    d["flb"] = t("flb", [P, DT], F32, kind=E).ap()
    d["lnig"] = t("lnig", [P, DT], F32, kind=E).ap()
    d["lnib"] = t("lnib", [P, DT], F32, kind=E).ap()
    d["ln1g"] = t("ln1g", [P, DT], F32, kind=E).ap()
    d["ln1b"] = t("ln1b", [P, DT], F32, kind=E).ap()
    d["d1T"] = t("d1T", [DM, 4096], F32, kind=E).ap()
    d["d1b"] = t("d1b", [P, 32], F32, kind=E).ap()
    d["d2T"] = t("d2T", [4096, DM], F32, kind=E).ap()
    d["d2b"] = t("d2b", [P, DT], F32, kind=E).ap()
    d["ln2g"] = t("ln2g", [P, DT], F32, kind=E).ap()
    d["ln2b"] = t("ln2b", [P, DT], F32, kind=E).ap()
    d["router_wT"] = t("router_wT", [DM, 8], F32, kind=E).ap()
    d["router_b"] = t("router_b", [1, 8], F32, kind=E).ap()
    d["lang_wT"] = t("lang_wT", [DM, 4], F32, kind=E).ap()
    d["lang_b"] = t("lang_b", [1, 4], F32, kind=E).ap()
    d["w1"] = t("w1", [8, DM, 4096], F32, kind=E).ap()
    d["b1c"] = t("b1c", [8, P, 32], F32, kind=E).ap()
    d["w2"] = t("w2", [8, 4096, DM], F32, kind=E).ap()
    d["b2"] = t("b2", [8, DM], F32, kind=E).ap()
    d["Wq"] = t("Wq", [DM, DM], F32, kind=E).ap()
    d["Wk"] = t("Wk", [DM, DM], F32, kind=E).ap()
    d["Wv"] = t("Wv", [DM, DM], F32, kind=E).ap()
    d["rtab"] = t("rtab", [P, 256], F32, kind=E).ap()
    x1 = t("x1s", [DM, TPC], F32, kind=I).ap()
    x2 = t("x2s", [DM, TPC], F32, kind=I).ap()
    x3 = t("x3s", [DM, TPC], F32, kind=I).ap()
    x4 = t("x4s", [DM, TPC], F32, kind=I).ap()
    x5 = t("x5s", [DM, TPC], F32, kind=I).ap()
    cc_tail = t("cc_tail", [DM, 8], F32, kind=I).ap()
    ag_tail = t("ag_tail", [NCORES * DM, 8], F32, kind=I).ap()
    ag_tail2 = t("ag_tail2", [2 * NCORES * DM, 8], F32, kind=I).ap()
    haloT = t("haloT", [DM, 8], F32, kind=I).ap()
    s_uc = t("s_uc", [DI, TPC], F32, kind=I).ap()
    s_res = t("s_res", [DI, TPC], F32, kind=I).ap()
    s_dl = t("s_dl", [DI, TPC], F32, kind=I).ap()
    s_bc = t("s_bc", [32, TPC], F32, kind=I).ap()
    s_hp = t("s_hp", [P, 512], F32, kind=I).ap()
    ag_hp = t("ag_hp", [NCORES * P, 512], F32, kind=I).ap()
    qloc = t("qloc", [16 * DM, P], BF16, kind=I).ap()
    oloc = t("oloc", [2 * DM, NCORES * P], BF16, kind=I).ap()
    yloc = t("yloc", [DI, TPC], F32, kind=I).ap()
    s_h = t("s_h", [DM, TPC], F32, kind=I).ap()
    s_k = t("s_k", [DM, TPC], F32, kind=I).ap()
    s_pp = t("s_pp", [P, 9], F32, kind=I).ap()
    ag_pp = t("ag_pp", [NCORES * P, 9], F32, kind=I).ap()
    s_h2 = t("s_h2", [DM, TPC], F32, kind=I).ap()
    cc_kq = t("cc_kq", [DM, TPC], BF16, kind=I).ap()
    ag_kq = t("ag_kq", [NCORES * DM, TPC], BF16, kind=I).ap()
    cc_qf = t("cc_qf", [16 * DM, P], BF16, kind=I).ap()
    ag_qf = t("ag_qf", [NCORES * 16 * DM, P], BF16, kind=I).ap()
    cc_v = t("cc_v", [TPC, DM], BF16, kind=I).ap()
    ag_v = t("ag_v", [SEQ, DM], BF16, kind=I).ap()
    cc_o = t("cc_o", [16 * DM, P], BF16, kind=I).ap()
    ag_o = t("ag_o", [NCORES * 16 * DM, P], BF16, kind=I).ap()
    d_out = t("x6T", [DM, TPC], F32, kind="ExternalOutput").ap()

    RG = [list(range(NCORES))]

    with ExitStack() as stack:
        prog = Prog(nc)
        cx = Ctx(nc, prog, stack)
        cx.psum_banks(7)
        cx.ps_bf16 = stack.enter_context(
            nc.psum_tensor("psbf", [P, 1024], BF16))
        emit_const_ones(cx)
        cx.sbuf("ln_sq", (P, 512), F32, persistent=True)
        cx.sbuf("ln_rows", (1, 512 * 3), F32, persistent=True)
        cx.sbuf("ln_rep", (P, 512 * 2), F32, persistent=True)
        for v in (1.0 / 1024, 1e-5, 1e-12, -0.5, 1.0 / 32.0):
            cx.cc(v, parts=1)
        cx.cc(-1e30, parts=P)

        def AG(name, src_ap, ksrc, dst_ap, kdst):
            prog.cc(lambda e: e.collective_compute(
                "AllGather", ALU.bypass, replica_groups=RG,
                ins=[src_ap.opt()], outs=[dst_ap.opt()]),
                reads=(ksrc,), writes=(kdst,), tag=f"ag_{name}")

        with cx.section():
            emit_ffn0(cx, d["xT"], "D:x", x1, "D:x1", d["gate_wT"], d["ebias"],
                      d["wupT"], d["wdnT"], tail=cc_tail, ktail="D:tail")
        prog.barrier()
        AG("tail", cc_tail, "D:tail", ag_tail, "D:agtail")
        prog.load(lambda e: e.dma_start(
            out=ag_tail2[0:NCORES * DM, :], in_=ag_tail[:]),
            reads=("D:agtail",), writes=("D:agt2",), tag="t2a",
            sem_key="D:agt2")
        prog.load(lambda e: e.dma_start(
            out=ag_tail2[NCORES * DM:, :], in_=ag_tail[:]),
            reads=("D:agtail",), writes=("D:agt2",), tag="t2b",
            sem_key="D:agt2")
        prog.load(lambda e: e.dma_start(
            out=haloT[:], in_=ag_tail2[bass.ds(
                (e.partition_id() + NCORES - 1) * DM, DM), :]),
            reads=("D:agt2",), writes=("D:halo",), tag="ldhalo0",
            sem_key="halo0", queue="gpsimd")
        with cx.section():
            emit_mamba_pre(cx, x1, "D:x1", haloT, "D:halo", d["halomask"],
                           s_uc, "D:uc", s_res, "D:res", s_dl, "D:dl",
                           s_bc, "D:bc", d["inprojT"], d["xprojT"],
                           d["dtprojT"], d["convw"], d["convb"], d["dtb"])
        prog.barrier()
        with cx.section():
            emit_scan_ts(cx, s_dl, "D:dl", s_uc, "D:uc", s_res, "D:res",
                         s_bc, "D:bc", d["alog"], d["dskip"], d["hsel"],
                         s_hp, "D:hp", ag_hp, "D:aghp", yloc, "D:yloc", AG)
        prog.barrier()
        with cx.section():
            emit_mamba_mid(cx, yloc, "D:yloc", x1, "D:x1", s_h, "D:h",
                           s_k, "D:k", s_pp, "D:pp", d["outprojT"], d["ln0g"],
                           d["ln0b"], d["qwT"], d["qb"], d["kwT"], d["kb"],
                           d["wgc"])
        prog.barrier()
        AG("pp", s_pp, "D:pp", ag_pp, "D:agpp")
        with cx.section():
            emit_mamba_post1(cx, s_h, "D:h", s_k, "D:k", ag_pp, "D:agpp",
                             s_h2, "D:h2", d["projT"], d["projb"], d["flT"],
                             d["flb"], d["lnig"], d["lnib"], d["ln1g"],
                             d["ln1b"])
        prog.barrier()
        with cx.section():
            emit_mamba_post2(cx, s_h2, "D:h2", x1, "D:x1", x2, "D:x2",
                             d["d1T"], d["d1b"], d["d2T"], d["d2b"],
                             d["ln2g"], d["ln2b"])
        prog.barrier()
        with cx.section():
            emit_ffn0(cx, x2, "D:x2", x3, "D:x3", d["gate_wT"], d["ebias"],
                      d["wupT"], d["wdnT"])
        prog.barrier()
        with cx.section():
            emit_ffn1(cx, x3, "D:x3", x4, "D:x4", d["router_wT"],
                      d["router_b"], d["lang_wT"], d["lang_b"], d["w1"],
                      d["b1c"], d["w2"], d["b2"])
        prog.barrier()
        with cx.section():
            emit_qkv(cx, x4, "D:x4", cc_kq, "D:kq", cc_qf, "D:qf", cc_v,
                     "D:v", d["Wq"], d["Wk"], d["Wv"])
        prog.barrier()
        AG("qf", cc_qf, "D:qf", ag_qf, "D:agqf")
        AG("kq", cc_kq, "D:kq", ag_kq, "D:agkq")
        AG("v", cc_v, "D:v", ag_v, "D:agv")
        for hh in range(2):
            prog.load(lambda e, hh=hh: e.dma_start(
                out=qloc.rearrange("(r w) n -> w r n", w=2 * DM)
                [hh * DM:(hh + 1) * DM, :, :],
                in_=ag_qf.rearrange("(r w) n -> w r n", w=16 * DM)
                [bass.ds(e.partition_id() * 2 * DM + hh * DM, DM), :, :]),
                reads=("D:agqf",), writes=("D:qloc",), tag=f"qgather{hh}",
                sem_key="D:qloc", queue="sync")
        with cx.section():
            emit_attn_fused(cx, ag_kq, "D:agkq", ag_v, "D:agv", qloc,
                            "D:qloc", d["rtab"], cc_o, "D:o")
        prog.barrier()
        AG("o", cc_o, "D:o", ag_o, "D:ago")
        for hh in range(2):
            prog.load(lambda e, hh=hh: e.dma_start(
                out=oloc.rearrange("w (r n) -> w r n", r=NCORES)
                [hh * DM:(hh + 1) * DM, :, :],
                in_=ag_o.rearrange("(r w) n -> w r n", w=16 * DM)
                [bass.ds(e.partition_id() * 2 * DM + hh * DM, DM), :, :]),
                reads=("D:ago",), writes=("D:oloc",), tag=f"ogather{hh}",
                sem_key="D:oloc", queue="sync")
        with cx.section():
            emit_attn_resid(cx, oloc, "D:oloc", x4, "D:x4", x5, "D:x5")
        prog.barrier()
        with cx.section():
            emit_ffn1(cx, x5, "D:x5", d_out, "D:x6", d["router_wT"],
                      d["router_b"], d["lang_wT"], d["lang_b"], d["w1"],
                      d["b1c"], d["w2"], d["b2"])
        prog.finalize(stack)
    return nc


def _fused_in_maps(inputs):
    convw = np.stack([col_layout(inputs["L0_attn_conv_w"][:, k])
                      for k in range(4)])
    b1c = np.stack([col_layout(inputs["L1_ffn_b1"][e]) for e in range(8)])
    alog = np.asarray(inputs["L0_attn_A_log"], np.float32)
    dskip = np.asarray(inputs["L0_attn_D_skip"], np.float32)
    DSL = DI // NCORES
    common = {
        "gate_wT": _t(inputs["L0_ffn_gate_w"]),
        "ebias": np.ascontiguousarray(
            np.asarray(inputs["L0_ffn_expert_bias"], np.float32).reshape(1, 8)),
        "wupT": np.stack([_t(inputs["L0_ffn_w_up"][e]) for e in range(8)]
                         + [_t(inputs["L0_ffn_shared_w_up"])]),
        "wdnT": np.stack([_t(inputs["L0_ffn_w_down"][e]) for e in range(8)]
                         + [_t(inputs["L0_ffn_shared_w_down"])]),
        "inprojT": _t(inputs["L0_attn_in_proj_w"]),
        "xprojT": _t(inputs["L0_attn_x_proj_w"]),
        "dtprojT": _t(inputs["L0_attn_dt_proj_w"]),
        "convw": convw,
        "convb": col_layout(inputs["L0_attn_conv_b"]),
        "dtb": col_layout(inputs["L0_attn_dt_proj_b"]),
        "outprojT": _t(inputs["L0_attn_out_proj_w"]),
        "ln0g": col_layout(inputs["L0_attn_ln0_g"]),
        "ln0b": col_layout(inputs["L0_attn_ln0_b"]),
        "qwT": _t(inputs["L0_attn_q_w"]), "qb": col_layout(inputs["L0_attn_q_b"]),
        "kwT": _t(inputs["L0_attn_k_w"]), "kb": col_layout(inputs["L0_attn_k_b"]),
        "wgc": col_layout(np.asarray(inputs["L0_attn_w_g"])[:, 0]),
        "projT": _t(inputs["L0_attn_proj_w"]),
        "projb": col_layout(inputs["L0_attn_proj_b"]),
        "flT": _t(inputs["L0_attn_fl_w"]),
        "flb": col_layout(inputs["L0_attn_fl_b"]),
        "lnig": col_layout(inputs["L0_attn_lni_g"]),
        "lnib": col_layout(inputs["L0_attn_lni_b"]),
        "ln1g": col_layout(inputs["L0_attn_ln1_g"]),
        "ln1b": col_layout(inputs["L0_attn_ln1_b"]),
        "d1T": _t(inputs["L0_attn_d1_w"]), "d1b": col_layout(inputs["L0_attn_d1_b"]),
        "d2T": _t(inputs["L0_attn_d2_w"]), "d2b": col_layout(inputs["L0_attn_d2_b"]),
        "ln2g": col_layout(inputs["L0_attn_ln2_g"]),
        "ln2b": col_layout(inputs["L0_attn_ln2_b"]),
        "router_wT": _t(inputs["L1_ffn_router_w"]),
        "router_b": np.ascontiguousarray(
            np.asarray(inputs["L1_ffn_router_b"], np.float32).reshape(1, 8)),
        "lang_wT": _t(inputs["L1_ffn_lang_w"]),
        "lang_b": np.ascontiguousarray(
            np.asarray(inputs["L1_ffn_lang_b"], np.float32).reshape(1, 4)),
        "w1": np.ascontiguousarray(inputs["L1_ffn_w1"]), "b1c": b1c,
        "w2": np.ascontiguousarray(inputs["L1_ffn_w2"]),
        "b2": np.ascontiguousarray(inputs["L1_ffn_b2"]),
        "Wq": np.asarray(inputs["L1_attn_Wq"]),
        "Wk": np.asarray(inputs["L1_attn_Wk"]),
        "Wv": np.asarray(inputs["L1_attn_Wv"]),
    }
    x = np.asarray(inputs["L0_attn_input_tensor"], np.float32)[0]
    xT = np.ascontiguousarray(x.T)
    xs = _slices(xT)
    al_c = np.ascontiguousarray(
        alog.reshape(16, P, 16).transpose(1, 0, 2).reshape(P, 256))
    ds_c = np.ascontiguousarray(dskip.reshape(16, P).T)
    in_maps = []
    for c in range(NCORES):
        hm = np.zeros((P, DT * 8), np.float32) if c == 0 else \
            np.ones((P, DT * 8), np.float32)
        hsel = np.zeros((P, NCORES), np.float32)
        if c > 0:
            hsel[:, c] = 1.0
        rt = np.zeros((P, 256), np.float32)
        for i in range(16):
            for kb in range(i + 1):
                rt[:, i * 16 + kb] = 128.0 * (c + 8 * i) - 1024.0 * kb
        in_maps.append(dict(common, xT=xs[c], halomask=hm, alog=al_c,
                            dskip=ds_c, hsel=hsel,
                            rtab=np.ascontiguousarray(rt)))
    return in_maps


def kernel(**inputs):
    nc = _get_nc("fused", build_fused)
    in_maps = _fused_in_maps(inputs)
    res = _run(nc, in_maps)
    x6T = np.concatenate([np.asarray(res.results[c]["x6T"])
                          for c in range(NCORES)], axis=1)
    return np.ascontiguousarray(x6T.T)[None, :, :].astype(np.float32)

